# version 05
# baseline (speedup 1.0000x reference)
import math
from contextlib import ExitStack

import numpy as np
import ml_dtypes

import concourse.bass as bass
import concourse.mybir as mybir
from concourse.bass_utils import run_bass_kernel_spmd

F32 = mybir.dt.float32
BF16 = mybir.dt.bfloat16
AF = mybir.ActivationFunctionType
ALU = mybir.AluOpType
AX = mybir.AxisListType.X

NCORES = 8
BPC = 2
T = 2048
D = 1024
TB = BPC * T
DBLK = D // 128

ENGS = ("t", "v", "s", "g", "y")
SECTION = {"t": "tensor", "v": "vector", "s": "scalar", "g": "gpsimd", "y": "sync"}
DMA_QUEUES = ("y", "g", "s", "v")
SELF_EDGES = True


class Buf:
    __slots__ = ("w", "r", "name")

    def __init__(self, name=""):
        self.w = None
        self.r = set()
        self.name = name


class Op:
    __slots__ = ("eng", "waits", "emit", "incs")

    def __init__(self, eng, waits, emit, incs):
        self.eng, self.waits, self.emit, self.incs = eng, waits, emit, incs


class Sched:
    def __init__(self):
        self.ops = []
        self.count = {e: 0 for e in ENGS}
        self.dcount = {q: 0 for q in DMA_QUEUES}
        self.known = {e: {} for e in ENGS}
        self.pending = {e: set() for e in ENGS}

    def _waits(self, eng, tags):
        ws = []
        for key, val in sorted(tags):
            if key == ("c", eng) and (eng == "t" or not SELF_EDGES):
                continue
            if self.known[eng].get(key, 0) >= val:
                continue
            ws.append((key, val))
            self.known[eng][key] = val
        return ws

    def _tags(self, reads, writes):
        tags = set()
        for b in reads:
            if b.w:
                tags.add(b.w)
        for b in writes:
            if b.w:
                tags.add(b.w)
            tags.update(b.r)
        return tags

    def _sound_dma(self, tags):
        out = set()
        for key, val in tags:
            if key[0] == "d":
                out.add((key, self.dcount[key[1]]))
            else:
                out.add((key, val))
        return out

    def op(self, eng, emit, reads=(), writes=()):
        tags = self._sound_dma(self._tags(reads, writes)) | self.pending[eng]
        self.pending[eng] = set()
        ws = self._waits(eng, tags)
        self.count[eng] += 1
        tag = (("c", eng), self.count[eng])
        self.ops.append(Op(eng, ws, emit, [(("c", eng), 1)]))
        for b in writes:
            b.w = tag
            b.r = set()
        for b in reads:
            b.r.add(tag)
        return tag

    def dma(self, q, emit, reads=(), writes=()):
        tags = self._sound_dma(self._tags(reads, writes)) | self.pending[q]
        self.pending[q] = set()
        ws = self._waits(q, tags)
        self.dcount[q] += 16
        tag = (("d", q), self.dcount[q])
        self.ops.append(Op(q, ws, emit, [(("d", q), 16)]))
        for b in writes:
            b.w = tag
            b.r = set()
        for b in reads:
            b.r.add(tag)
        return tag

    def barrier(self):
        tags = set()
        for e in ENGS:
            if self.count[e]:
                tags.add((("c", e), self.count[e]))
        for q in DMA_QUEUES:
            if self.dcount[q]:
                tags.add((("d", q), self.dcount[q]))
        for e in ENGS:
            self.pending[e] |= tags

    def flush_waits(self, eng):
        tags = self.pending[eng]
        self.pending[eng] = set()
        ws = self._waits(eng, tags)
        if ws:
            self.ops.append(Op(eng, ws, None, []))

    def realize(self, nc, sems):
        with nc.Block() as block:
            for eng in ENGS:
                ops = [o for o in self.ops if o.eng == eng]
                if not ops:
                    continue
                dec = getattr(block, SECTION[eng])

                def body(e, ops=ops):
                    for o in ops:
                        for key, val in o.waits:
                            e.wait_ge(sems[key], val)
                        if o.emit is None:
                            continue
                        inst = o.emit(e)
                        for key, amt in o.incs:
                            inst.then_inc(sems[key], amt)

                dec(body)


class K:
    def __init__(self, nc, ctx):
        self.nc = nc
        self.ctx = ctx
        self.sched = Sched()
        self.bufs = {}
        self.tensors = {}
        self.scopes = []
        self.uid = 0

    def push_scope(self):
        self.scopes.append((ExitStack(), []))

    def pop_scope(self):
        ctx, names = self.scopes.pop()
        ctx.close()
        for n in names:
            self.tensors.pop(n, None)

    def sb(self, name, shape, dtype):
        if name in self.tensors:
            return self.tensors[name]
        self.uid += 1
        pname = f"{name}__{self.uid}"
        if self.scopes:
            ctx, names = self.scopes[-1]
            h = ctx.enter_context(self.nc.sbuf_tensor(pname, shape, dtype))
            names.append(name)
        else:
            h = self.ctx.enter_context(self.nc.sbuf_tensor(pname, shape, dtype))
        self.tensors[name] = h
        return h

    def ps(self, name, shape, dtype=F32):
        h = self.ctx.enter_context(self.nc.psum_tensor(name, shape, dtype))
        self.tensors[name] = h
        return h

    def nbuf(self, name):
        b = self.bufs.get(name)
        if b is None:
            b = self.bufs[name] = Buf(name)
        return b

    def op(self, eng, emit, reads=(), writes=()):
        return self.sched.op(eng, emit, [self.nbuf(r) for r in reads],
                             [self.nbuf(w) for w in writes])

    def dma(self, q, out_ap, in_ap, reads=(), writes=(), **kw):
        return self.sched.dma(
            q, lambda e: e.dma_start(out=out_ap, in_=in_ap, **kw),
            [self.nbuf(r) for r in reads], [self.nbuf(w) for w in writes])

    def mm(self, ps_ap, lhsT_ap, rhs_ap, start, stop, reads=(), writes=()):
        return self.op(
            "t", lambda e: e.matmul(ps_ap, lhsT_ap, rhs_ap, start=start, stop=stop),
            reads=reads, writes=writes)



def _t(a):
    return np.ascontiguousarray(np.asarray(a).T).astype(np.float32, copy=False)


def _row(a):
    return np.ascontiguousarray(np.asarray(a).reshape(1, -1)).astype(np.float32, copy=False)


def prep_weights(inp):
    w = {}
    assert np.asarray(inp["L3_ffn_router_prob"]).shape[2] == 4
    w["l0a_wqT"] = _t(inp["L0_attn_Wq"])
    w["l0a_wkT"] = _t(inp["L0_attn_Wk"])
    w["l0a_wvT"] = _t(inp["L0_attn_Wv"])
    w["l0a_woT"] = _t(inp["L0_attn_Wo"])
    w["l0a_bq"] = _row(inp["L0_attn_bq"])
    w["l0a_bk"] = _row(inp["L0_attn_bk"])
    w["l0a_bv"] = _row(inp["L0_attn_bv"])
    w["l0a_bo"] = _row(inp["L0_attn_bo"])
    w["l0a_U"] = np.ascontiguousarray(np.asarray(inp["L0_attn_U"], np.float32))
    w["l0a_V"] = np.ascontiguousarray(np.asarray(inp["L0_attn_V"], np.float32))
    w["l0f_wfcT"] = _t(inp["L0_ffn_W_fc"])
    w["l0f_wprojT"] = _t(inp["L0_ffn_W_proj"])
    w["l0f_lnw"] = _row(inp["L0_ffn_ln_w"])
    w["l0f_lnb"] = _row(inp["L0_ffn_ln_b"])
    w["l1a_watt"] = np.ascontiguousarray(np.asarray(inp["L1_attn_w_attn"], np.float32))
    w["l1a_batt"] = _row(inp["L1_attn_b_attn"])
    w["l1a_wproj"] = np.ascontiguousarray(np.asarray(inp["L1_attn_w_proj"], np.float32))
    w["l1a_bproj"] = _row(inp["L1_attn_b_proj"])
    w["l1f_w1s"] = np.ascontiguousarray(
        np.asarray(inp["L1_ffn_W1"], np.float32).transpose(1, 0, 2).reshape(D, D))
    w["l1f_b1"] = _row(inp["L1_ffn_b1"])
    w["l1f_w2s"] = np.ascontiguousarray(
        np.asarray(inp["L1_ffn_W2"], np.float32).reshape(D, D))
    w["l1f_b2"] = np.ascontiguousarray(np.asarray(inp["L1_ffn_b2"], np.float32))
    w["l1f_wr"] = np.ascontiguousarray(np.asarray(inp["L1_ffn_Wr"], np.float32))
    w["l1f_br"] = _row(inp["L1_ffn_br"])
    s = np.zeros((16, D), np.float32)
    for m in range(D):
        s[m // 64, m] = 1.0
    w["l1f_sel"] = s
    perm = np.arange(3072)
    for sec in range(2):
        for hh in range(16):
            base = sec * 1024 + hh * 64
            perm[base:base + 32] = base + np.arange(0, 64, 2)
            perm[base + 32:base + 64] = base + np.arange(1, 64, 2)
    wq = np.asarray(inp["L2_attn_w_qkv"], np.float32)
    w["l2a_wqkvT"] = np.ascontiguousarray(wq[perm].T)
    bq = np.asarray(inp["L2_attn_b_qkv"], np.float32)[perm]
    w["l2a_bqkv"] = np.ascontiguousarray(bq.reshape(1, -1))
    w["l2a_woutT"] = _t(inp["L2_attn_w_out"])
    w["l2a_bout"] = _row(inp["L2_attn_b_out"])
    inv_freq = (1.0 / np.float32(10000.0) **
                (np.arange(0, 64, 2, dtype=np.float32) / np.float32(64.0))
                ).astype(np.float32)
    pos = np.arange(T, dtype=np.float32)
    freqs = (pos[None, :] * inv_freq[:, None]).astype(np.float32)
    sin = np.sin(freqs).astype(np.float32)
    cos = np.cos(freqs).astype(np.float32)
    w["l2a_cos4"] = np.ascontiguousarray(np.tile(cos, (4, 1)))
    w["l2a_ssin4"] = np.ascontiguousarray(
        np.concatenate([-sin, sin, -sin, sin], axis=0))
    w["l2f_baseWT"] = _t(inp["L2_ffn_base_W"])
    w["l2f_base_b"] = _row(inp["L2_ffn_base_b"])
    w["l2f_muT"] = _t(inp["L2_ffn_disc_mu"])
    w["l2f_sigT"] = _t(inp["L2_ffn_disc_sigma"])
    w["l2f_downWT"] = np.ascontiguousarray(
        np.asarray(inp["L2_ffn_down_W"], np.float32).transpose(1, 0, 2)
        .reshape(D, 8 * 64))
    w["l2f_down_b"] = np.ascontiguousarray(np.asarray(inp["L2_ffn_down_b"],
                                                      np.float32))
    w["l2f_upWs"] = np.ascontiguousarray(
        np.asarray(inp["L2_ffn_up_W"], np.float32).reshape(8 * 64, D))
    w["l2f_up_b"] = np.ascontiguousarray(np.asarray(inp["L2_ffn_up_b"],
                                                    np.float32))
    w["l2f_convWT"] = np.ascontiguousarray(
        np.asarray(inp["L2_ffn_conv_W"], np.float32).transpose(0, 2, 1)
        .reshape(8 * D, D))
    w["l2f_conv_b"] = np.ascontiguousarray(np.asarray(inp["L2_ffn_conv_b"],
                                                      np.float32))
    w["l3f_qwT"] = _t(inp["L3_ffn_q_w"])
    w["l3f_kwT"] = _t(inp["L3_ffn_k_w"])
    w["l3f_vwT"] = _t(inp["L3_ffn_v_w"])
    w["l3f_owT"] = _t(inp["L3_ffn_o_w"])
    w["l3f_qb"] = _row(inp["L3_ffn_q_b"])
    w["l3f_kb"] = _row(inp["L3_ffn_k_b"])
    w["l3f_vb"] = _row(inp["L3_ffn_v_b"])
    w["l3f_ob"] = _row(inp["L3_ffn_o_b"])
    w["l3f_nw"] = _row(inp["L3_ffn_norm_w"])
    w["l3f_nb"] = _row(inp["L3_ffn_norm_b"])
    w["l3a_ipT"] = _t(inp["L3_attn_in_proj_w"])
    w["l3a_convw"] = np.ascontiguousarray(
        np.asarray(inp["L3_attn_conv_w"], np.float32).reshape(2048, 4))
    w["l3a_convb"] = np.ascontiguousarray(
        np.asarray(inp["L3_attn_conv_b"], np.float32).reshape(2048, 1))
    w["l3a_dtb"] = np.ascontiguousarray(
        np.asarray(inp["L3_attn_dt_head_b"], np.float32).reshape(2048, 1))
    w["l3a_D"] = np.ascontiguousarray(
        np.asarray(inp["L3_attn_D"], np.float32).reshape(2048, 1))
    w["l3a_xpT"] = _t(inp["L3_attn_x_proj_w"])
    w["l3a_dthT"] = _t(inp["L3_attn_dt_head_w"])
    w["l3a_opT"] = _t(inp["L3_attn_out_proj_w"])
    return w


WSHAPES = {
    "l0a_wqT": [D, D], "l0a_wkT": [D, D], "l0a_wvT": [D, D], "l0a_woT": [D, D],
    "l0a_bq": [1, D], "l0a_bk": [1, D], "l0a_bv": [1, D], "l0a_bo": [1, D],
    "l0a_U": [64, 32], "l0a_V": [64, 32],
    "l0f_wfcT": [D, 4096], "l0f_wprojT": [4096, D],
    "l0f_lnw": [1, 4096], "l0f_lnb": [1, 4096],
    "l1a_watt": [D, 3072], "l1a_batt": [1, 3072],
    "l1a_wproj": [D, D], "l1a_bproj": [1, D],
    "l1f_w1s": [D, D], "l1f_b1": [1, D], "l1f_w2s": [D, D], "l1f_b2": [16, D],
    "l1f_wr": [D, 16], "l1f_br": [1, 16], "l1f_sel": [16, D],
    "l2a_wqkvT": [D, 3072], "l2a_bqkv": [1, 3072],
    "l2a_woutT": [D, D], "l2a_bout": [1, D],
    "l2a_cos4": [128, T], "l2a_ssin4": [128, T],
    "l2f_baseWT": [D, D], "l2f_base_b": [1, D], "l2f_muT": [D, 8],
    "l2f_sigT": [D, 8], "l2f_downWT": [D, 512], "l2f_down_b": [8, 64],
    "l2f_upWs": [512, D], "l2f_up_b": [8, D], "l2f_convWT": [8 * D, D],
    "l2f_conv_b": [8, D],
    "l3f_qwT": [D, D], "l3f_kwT": [D, D], "l3f_vwT": [D, D], "l3f_owT": [D, D],
    "l3f_qb": [1, D], "l3f_kb": [1, D], "l3f_vb": [1, D], "l3f_ob": [1, D],
    "l3f_nw": [1, D], "l3f_nb": [1, D],
    "l3a_ipT": [D, 4096], "l3a_convw": [2048, 4], "l3a_convb": [2048, 1],
    "l3a_dtb": [2048, 1], "l3a_D": [2048, 1], "l3a_xpT": [2048, 96],
    "l3a_dthT": [64, 2048], "l3a_opT": [2048, D],
}

WSCRATCH = {
    "l0_ffn": {"l0f_wfcq_dram": ([D, 4096], BF16)},
    "l2_ffn": {"l2f_wsel_dram": ([2 * D, D], BF16),
               "l2f_base_dram": ([D, D], BF16)},
    "l3_attn": {"l3a_xssm_dram": ([2 * 2048, T], BF16),
                "l3a_yfin_dram": ([2 * 2048, T], BF16),
                "l3a_opT_bf": ([2048, D], BF16)},
}

WNAMES = {
    "l0_attn": ["l0a_wqT", "l0a_wkT", "l0a_wvT", "l0a_woT",
                "l0a_bq", "l0a_bk", "l0a_bv", "l0a_bo", "l0a_U", "l0a_V"],
    "l0_ffn": ["l0f_wfcT", "l0f_wprojT", "l0f_lnw", "l0f_lnb"],
    "l1_attn": ["l1a_watt", "l1a_batt", "l1a_wproj", "l1a_bproj"],
    "l1_ffn": ["l1f_w1s", "l1f_b1", "l1f_w2s", "l1f_b2", "l1f_wr", "l1f_br",
               "l1f_sel"],
    "l2_attn": ["l2a_wqkvT", "l2a_bqkv", "l2a_woutT", "l2a_bout",
                "l2a_cos4", "l2a_ssin4"],
    "l2_ffn": ["l2f_baseWT", "l2f_base_b", "l2f_muT", "l2f_sigT", "l2f_downWT",
               "l2f_down_b", "l2f_upWs", "l2f_up_b", "l2f_convWT", "l2f_conv_b"],
    "l3_ffn": ["l3f_qwT", "l3f_kwT", "l3f_vwT", "l3f_owT", "l3f_qb", "l3f_kb",
               "l3f_vb", "l3f_ob", "l3f_nw", "l3f_nb"],
    "l3_attn": ["l3a_ipT", "l3a_convw", "l3a_convb", "l3a_dtb", "l3a_D",
                "l3a_xpT", "l3a_dthT", "l3a_opT"],
}


def shard_x(x):
    shards = []
    for c in range(NCORES):
        xs = np.asarray(x[c * BPC:(c + 1) * BPC])
        xs = np.ascontiguousarray(xs.transpose(2, 0, 1).reshape(D, TB))
        shards.append(xs.astype(np.float32, copy=False))
    return shards


def unshard_x(shards):
    out = np.empty((NCORES * BPC, T, D), np.float32)
    for c, xs in enumerate(shards):
        out[c * BPC:(c + 1) * BPC] = np.asarray(xs).reshape(D, BPC, T).transpose(1, 2, 0)
    return out



ROW_MU, ROW_R, ROW_RECIP = 32, 0, 64


def build_common(k: K, consts=(1e-5,)):
    for i, cv in enumerate(consts):
        ct = k.sb(f"const_{i}", [128, 1], F32)
        k.op("v", lambda e, ct=ct, cv=cv: e.memset(ct[:], cv), writes=[f"const_{i}"])
        k.nc.const_aps.register(F32, cv, ct[:, 0:1])
    k.sched.barrier()
    ones_col = k.sb("ones_col", [128, 1], BF16)
    ones_row = k.sb("ones_row", [1, 512], BF16)
    orf = k.sb("orf", [1, 128], F32)
    ocf = k.sb("ones_col_f", [128, 1], F32)
    k.op("v", lambda e: e.memset(ocf[:], 1.0), writes=["ones_col_f"])
    k.op("v", lambda e: e.memset(ones_col[:], 1.0), writes=["ones_col"])
    k.op("v", lambda e: e.memset(ones_row[:], 1.0), writes=["ones_row"])
    k.op("v", lambda e: e.memset(orf[:], 1.0), writes=["orf"])
    k.sb("row_t1", [1, 512], F32)
    k.sb("row_t2", [32, 512], F32)
    k.sb("row_t3", [1, 512], F32)
    k.sb("lnbf", [1, TB], BF16)
    k.sb("r_rep", [128, TB], BF16)
    k.sb("sq_t", [128, 1024], BF16)
    k.ps("psA", [128, 512])
    k.ps("psB", [128, 512])
    k.ps("psC", [128, 512])
    k.ps("psD", [128, 512])
    k.ps("psE1", [65, 512])
    k.ps("psE2", [65, 512])
    k.ps("psR1", [1, 512])
    k.ps("psR2", [1, 512])


def rowsl(k, base, t0, n):
    return k.tensors["rowbank"][base:base + 1, t0:t0 + n]


def build_io_load(k: K, xin_ap):
    x_sb = k.sb("x_sb", [128, DBLK * TB], BF16)
    stg = k.sb("io_stg", [128, 1024], F32)
    for db in range(DBLK):
        for hh in range(4):
            k.dma("y", stg[:, 0:1024],
                  xin_ap[db * 128:(db + 1) * 128, hh * 1024:(hh + 1) * 1024],
                  reads=[], writes=["io_stg"])
            k.op("g" if db % 2 else "s",
                 lambda e, db=db, hh=hh: (e.tensor_copy if hasattr(e, "tensor_copy")
                                          else e.copy)(
                     x_sb[:, db * TB + hh * 1024:db * TB + (hh + 1) * 1024],
                     stg[:, 0:1024]),
                 reads=["io_stg"], writes=[f"x{db}"])
    return x_sb


def build_io_store(k: K, xout_ap):
    x = k.tensors["x_sb"]
    stg = k.tensors["io_stg"]
    for db in range(DBLK):
        for hh in range(4):
            k.op("s", lambda e, db=db, hh=hh: e.copy(
                stg[:, 0:1024],
                x[:, db * TB + hh * 1024:db * TB + (hh + 1) * 1024]),
                reads=[f"x{db}"], writes=["io_stg"])
            k.dma("y", xout_ap[db * 128:(db + 1) * 128, hh * 1024:(hh + 1) * 1024],
                  stg[:, 0:1024], reads=["io_stg"], writes=[])
    k.sched.pending["y"].add((("d", "y"), k.sched.dcount["y"]))
    k.sched.flush_waits("y")


def build_ln_rows(k: K):
    x = k.tensors["x_sb"]
    lnbf = k.tensors["lnbf"]
    r_rep = k.tensors["r_rep"]
    sq = k.tensors["sq_t"]
    oc = k.tensors["ones_col"]
    orf = k.tensors["orf"]
    t1, t3 = k.tensors["row_t1"], k.tensors["row_t3"]
    t2 = k.tensors["row_t2"][0:1, :]
    psR1, psR2 = k.tensors["psR1"], k.tensors["psR2"]
    psA, psB = k.tensors["psA"], k.tensors["psB"]
    for fc in range(TB // 512):
        f0 = fc * 512
        for db in range(DBLK):
            xs = x[:, db * TB + f0: db * TB + f0 + 512]
            sqs = sq[:, (db % 2) * 512:(db % 2) * 512 + 512]
            k.op("s", lambda e, xs=xs, sqs=sqs: e.activation(sqs, xs, AF.Square),
                 reads=[f"x{db}"], writes=[f"sq{db % 2}"])
            k.mm(psR1[:], oc[:], xs, start=db == 0, stop=db == DBLK - 1,
                 reads=["ones_col", f"x{db}"], writes=["psR1"])
            k.mm(psR2[:], oc[:], sqs, start=db == 0, stop=db == DBLK - 1,
                 reads=["ones_col", f"sq{db % 2}"], writes=["psR2"])
        k.op("v", lambda e: e.tensor_scalar_mul(t1[:], psR1[:], 1.0 / D),
             reads=["psR1"], writes=["row_t1"])
        k.op("s", lambda e, f0=f0: e.copy(lnbf[0:1, f0:f0 + 512], t1[:]),
             reads=["row_t1"], writes=["lnbf"])
        k.op("v", lambda e: e.tensor_scalar_mul(t3[:], psR2[:], 1.0 / D),
             reads=["psR2"], writes=["row_t3"])
        k.op("s", lambda e: e.activation(t2, t1[:], AF.Square),
             reads=["row_t1"], writes=["row_t2"])
        k.op("v", lambda e: e.tensor_sub(t3[:], t3[:], t2),
             reads=["row_t3", "row_t2"], writes=["row_t3"])
        if "recipbf" in k.tensors:
            rb = k.tensors["recipbf"]
            k.op("s", lambda e: e.activation(t2, t3[:], AF.Sqrt, bias=1e-5),
                 reads=["row_t3"], writes=["row_t2"])
            k.op("s", lambda e, f0=f0, rb=rb: e.copy(rb[0:1, f0:f0 + 512], t2),
                 reads=["row_t2"], writes=["recipbf"])
        k.op("s", lambda e: e.activation(t3[:], t3[:], AF.Ln, bias=1e-5),
             reads=["row_t3"], writes=["row_t3"])
        k.op("s", lambda e: e.activation(t3[:], t3[:], AF.Exp, scale=-0.5),
             reads=["row_t3"], writes=["row_t3"])
        ps = psA if fc % 2 == 0 else psB
        pn = "psA" if fc % 2 == 0 else "psB"
        k.mm(ps[:], orf[:], t3[:], True, True, reads=["orf", "row_t3"], writes=[pn])
        k.op("s", lambda e, f0=f0, ps=ps: e.copy(r_rep[:, f0:f0 + 512], ps[:]),
             reads=[pn], writes=["r_rep"])
        if "rcol" in k.tensors:
            rc = k.tensors["rcol"]
            psO = psA if fc % 2 else psB
            pnO = "psA" if fc % 2 else "psB"
            for j in range(4):
                k.mm(psO[:, j:j + 1], t3[0:1, j * 128:(j + 1) * 128], orf[0:1, 0:1],
                     True, True, reads=["row_t3", "orf"], writes=[pnO])
            k.op("s", lambda e, fc=fc, psO=psO: e.copy(rc[:, fc * 4:fc * 4 + 4],
                                                       psO[:, 0:4]),
                 reads=[pnO], writes=["rcol"])



def build_l0_attn(k: K, W):
    x = k.tensors["x_sb"]
    r_rep = k.tensors["r_rep"]
    lnbf = k.tensors["lnbf"]
    oc, orow, orf = k.tensors["ones_col"], k.tensors["ones_row"], k.tensors["orf"]
    psA, psB, psC, psD = (k.tensors[n] for n in ("psA", "psB", "psC", "psD"))
    psE = [k.tensors["psE1"][0:33, :], k.tensors["psE2"][0:33, :]]
    psR1 = k.tensors["psR1"]

    k.sb("recipbf", [1, TB], BF16)
    build_ln_rows(k)

    uv = k.sb("l0_uv", [128, 64], BF16)
    stg = k.sb("l0_wstg", [128, 2 * 1024], F32)
    wtb = k.sb("l0_wtb", [128, 3 * 1024], BF16)
    lf = k.sb("l0_lf", [1, 3 * 1024], BF16)
    bias_bf = k.sb("l0_bias_bf", [1, 3 * 1024], BF16)
    bb = k.sb("l0_bias", [128, 1024], F32)
    wo_p = k.sb("l0_wop", [128, 4 * 1024], BF16)
    bo_bf = k.sb("l0_bo_bf", [1, 1024], BF16)
    qkv_hp = k.sb("l0_qkv_hp", [128, 3 * T], BF16)
    zqT = k.sb("l0_zqT", [128, 2 * T], BF16)
    zkT = k.sb("l0_zkT", [128, 2 * T], BF16)
    zv = k.sb("l0_zv", [128, 4 * 16 * 33], BF16)
    yr = k.sb("l0_yr", [128, 4 * T], BF16)
    p_sb = k.sb("l0_p", [128, 2 * 512], BF16)
    srow = k.tensors["row_t1"]
    srep = k.tensors["row_t2"][0:32, :]

    k.dma("g", stg[0:64, 0:32], W["l0a_V"][:, :], writes=["l0stg_0"])
    k.dma("g", stg[0:64, 32:64], W["l0a_U"][:, :], writes=["l0stg_0"])
    k.dma("g", stg[64:128, 0:32], W["l0a_V"][:, :], writes=["l0stg_0"])
    k.dma("g", stg[64:128, 32:64], W["l0a_U"][:, :], writes=["l0stg_0"])
    k.op("s", lambda e: e.copy(uv[:, 0:64], stg[0:128, 0:64]),
         reads=["l0stg_0"], writes=["l0_uv"])
    for i, nm in enumerate(("bq", "bk", "bv", "bo")):
        k.dma("g", bb[32 * i:32 * i + 1, :], W["l0a_" + nm][:, :],
              writes=["l0_bias"])
    for i in range(3):
        k.op("s", lambda e, i=i: e.copy(bias_bf[0:1, i * 1024:(i + 1) * 1024],
                                        bb[32 * i:32 * i + 1, :]),
             reads=["l0_bias"], writes=["l0_bias_bf"])
    k.op("s", lambda e: e.copy(bo_bf[:], bb[96:97, :]),
         reads=["l0_bias"], writes=["l0_bo_bf"])

    ALLS = [f"l0stg_{i}" for i in range(2)]
    ALLW = [f"l0wt_{i}" for i in range(2)]
    for ch in range(4):
        f0 = ch * 256
        k.dma("y", stg[:, 0:2048].rearrange("p (a c) -> p a c", a=8),
              W["l0a_woT"].rearrange("(a p) c -> p a c", p=128)[:, :, f0:f0 + 256],
              reads=[], writes=ALLS)
        k.op("g", lambda e: e.tensor_copy(wtb[:, 0:2048], stg[:, 0:2048]),
             reads=ALLS, writes=ALLW)
        for h in range(16):
            hg, hi = h // 4, h % 4
            rb, base = h // 2, 64 * (h % 2)
            ps = psA if h % 2 == 0 else psB
            pn = "psA" if h % 2 == 0 else "psB"
            k.mm(ps[0:32, 0:256], uv[base:base + 64, 32:64],
                 wtb[base:base + 64, rb * 256:(rb + 1) * 256],
                 True, True, reads=["l0_uv"] + ALLW, writes=[pn])
            k.op("s", lambda e, hg=hg, hi=hi, f0=f0, ps=ps:
                 e.copy(wo_p[32 * hi:32 * hi + 32,
                             hg * 1024 + f0:hg * 1024 + f0 + 256], ps[0:32, 0:256]),
                 reads=[pn], writes=["l0_wop"])

    SC = 1.0 / math.sqrt(32.0)
    for b in range(BPC):
        for quarter in range(4):
            for hp in range(2):
                hp_g = quarter * 2 + hp
                for wi, wn in enumerate(("wq", "wk", "wv")):
                    st = stg[:, (wi % 2) * 1024:(wi % 2 + 1) * 1024]
                    wt = wtb[:, wi * 1024:(wi + 1) * 1024]
                    stn, wtn = f"l0stg_{wi % 2}", f"l0wt_{wi}"
                    k.dma("y", st.rearrange("p (a c) -> p a c", a=8),
                          W["l0a_" + wn + "T"].rearrange("(a p) c -> p a c", p=128)
                          [:, :, hp_g * 128:(hp_g + 1) * 128],
                          reads=[], writes=[stn])
                    k.op("g", lambda e, wt=wt, st=st: e.tensor_copy(wt, st),
                         reads=[stn], writes=[wtn])
                    for db in range(DBLK):
                        k.mm(psR1[:, 0:128], oc[:], wt[:, db * 128:(db + 1) * 128],
                             start=db == 0, stop=db == DBLK - 1,
                             reads=["ones_col", wtn], writes=["psR1"])
                    k.op("s", lambda e, wi=wi, hp_g=hp_g:
                         e.activation(lf[0:1, wi * 1024 + hp_g * 128:
                                         wi * 1024 + (hp_g + 1) * 128],
                                      psR1[:, 0:128], AF.Copy, scale=-1.0),
                         reads=["psR1"], writes=["l0_lf"])
                    for fc in range(T // 512):
                        f0 = b * T + fc * 512
                        ps = psA if fc % 2 == 0 else psB
                        pn = "psA" if fc % 2 == 0 else "psB"
                        for db in range(DBLK):
                            k.mm(ps[:], wt[:, db * 128:(db + 1) * 128],
                                 x[:, db * TB + f0:db * TB + f0 + 512],
                                 start=db == 0, stop=False,
                                 reads=[wtn, f"x{db}"], writes=[pn])
                        k.mm(ps[:], lf[0:1, wi * 1024 + hp_g * 128:
                                       wi * 1024 + (hp_g + 1) * 128],
                             lnbf[0:1, f0:f0 + 512], start=False, stop=False,
                             reads=["l0_lf", "lnbf"], writes=[pn])
                        k.mm(ps[:], bias_bf[0:1, wi * 1024 + hp_g * 128:
                                            wi * 1024 + (hp_g + 1) * 128],
                             k.tensors["recipbf"][0:1, f0:f0 + 512],
                             start=False, stop=True,
                             reads=["l0_bias_bf", "recipbf"], writes=[pn])
                        dst = qkv_hp[:, wi * T + fc * 512: wi * T + fc * 512 + 512]
                        k.op("v", lambda e, dst=dst, ps=ps, f0=f0:
                             e.tensor_mul(dst, ps[:], r_rep[:, f0:f0 + 512]),
                             reads=[pn, "r_rep"], writes=[f"l0qkv_{wi}"])
                for hh in range(2):
                    h_loc = hp * 2 + hh
                    base = 64 * hh
                    rowz = 64 * (h_loc % 2)
                    colg = h_loc // 2
                    for wi in range(2):
                        ps = psA if wi == 0 else psB
                        pn = "psA" if wi == 0 else "psB"
                        zdst = zqT if wi == 0 else zkT
                        zdn = "l0_zqT" if wi == 0 else "l0_zkT"
                        for fc in range(T // 512):
                            k.mm(ps[0:32, :], uv[base:base + 64, 0:32],
                                 qkv_hp[base:base + 64,
                                        wi * T + fc * 512:wi * T + fc * 512 + 512],
                                 True, True, reads=["l0_uv", f"l0qkv_{wi}"],
                                 writes=[pn])
                            k.op("s", lambda e, zdst=zdst, rowz=rowz, colg=colg,
                                 fc=fc, ps=ps:
                                 e.copy(zdst[rowz:rowz + 32,
                                             colg * T + fc * 512:colg * T + fc * 512 + 512],
                                        ps[0:32, :]),
                                 reads=[pn], writes=[zdn])
                    for kt in range(T // 128):
                        k.mm(psC[:, (kt % 4) * 128:(kt % 4) * 128 + 32],
                             qkv_hp[base:base + 64, 2 * T + kt * 128:2 * T + (kt + 1) * 128],
                             uv[base:base + 64, 0:32], True, True,
                             reads=["l0qkv_2", "l0_uv"], writes=["psC"])
                        if kt % 4 == 3:
                            kt0 = kt - 3
                            k.op("s", lambda e, h_loc=h_loc, kt0=kt0:
                                 e.copy(zv[:, h_loc * 16 * 33 + kt0 * 33:
                                           h_loc * 16 * 33 + (kt0 + 4) * 33]
                                        .rearrange("p (a c) -> p a c", a=4)[:, :, 0:32],
                                        psC[:].rearrange("p (a c) -> p a c", a=4)
                                        [:, :, 0:32]),
                                 reads=["psC"], writes=["l0_zv"])
            k.op("v", lambda e: e.memset(
                zv[:].rearrange("p (a c) -> p a c", c=33)[:, :, 32:33], 1.0),
                writes=["l0_zv"])

            for h_loc in range(4):
                h = quarter * 4 + h_loc
                hg, hi = h // 4, h % 4
                rowz = 64 * (h_loc % 2)
                colg = h_loc // 2
                for qc in range(T // 512):
                    pse = psE[qc % 2]
                    pen = "psE1" if qc % 2 == 0 else "psE2"
                    for kt in range(T // 128):
                        pss = psC if kt % 2 == 0 else psD
                        psn = "psC" if kt % 2 == 0 else "psD"
                        k.mm(pss[:],
                             zkT[rowz:rowz + 32,
                                 colg * T + kt * 128:colg * T + (kt + 1) * 128],
                             zqT[rowz:rowz + 32,
                                 colg * T + qc * 512:colg * T + qc * 512 + 512],
                             True, True, reads=["l0_zqT", "l0_zkT"], writes=[psn])
                        pt = p_sb[:, (kt % 2) * 512:(kt % 2) * 512 + 512]
                        ptn = f"l0p{kt % 2}"
                        k.op("s", lambda e, pt=pt, pss=pss:
                             e.activation(pt, pss[:], AF.Exp, scale=SC),
                             reads=[psn], writes=[ptn])
                        k.mm(pse[:],
                             zv[:, h_loc * 16 * 33 + kt * 33:h_loc * 16 * 33 + (kt + 1) * 33],
                             pt, start=kt == 0, stop=kt == 15,
                             reads=["l0_zv", ptn], writes=[pen])
                    k.op("v", lambda e, pse=pse: e.reciprocal(srow[:], pse[32:33, :]),
                         reads=[pen], writes=["row_t1"])
                    k.mm(psC[0:32, :], orf[0:1, 0:32], srow[:], True, True,
                         reads=["orf", "row_t1"], writes=["psC"])
                    k.op("s", lambda e: e.copy(srep[:], psC[0:32, :]),
                         reads=["psC"], writes=["row_t2"])
                    k.op("v", lambda e, hi=hi, hg=hg, qc=qc, pse=pse:
                         e.tensor_mul(yr[32 * hi:32 * hi + 32,
                                         hg * T + qc * 512:hg * T + qc * 512 + 512],
                                      pse[0:32, :], srep[:]),
                         reads=[pen, "row_t2"], writes=["l0_yr"])

        for db in range(DBLK):
            for fc in range(T // 512):
                f0 = fc * 512
                ps = psA if fc % 2 == 0 else psB
                pn = "psA" if fc % 2 == 0 else "psB"
                for hg in range(4):
                    k.mm(ps[:], wo_p[:, hg * 1024 + db * 128:hg * 1024 + (db + 1) * 128],
                         yr[:, hg * T + f0:hg * T + f0 + 512],
                         start=hg == 0, stop=False,
                         reads=["l0_wop", "l0_yr"], writes=[pn])
                k.mm(ps[:], bo_bf[0:1, db * 128:(db + 1) * 128], orow[:],
                     start=False, stop=True,
                     reads=["l0_bo_bf", "ones_row"], writes=[pn])
                xs = x[:, db * TB + b * T + f0: db * TB + b * T + f0 + 512]
                k.op("v", lambda e, xs=xs, ps=ps: e.tensor_add(xs, ps[:], xs),
                     reads=[pn], writes=[f"x{db}"])

    k.sched.barrier()





def build_l0_ffn(k: K, W):
    x = k.tensors["x_sb"]
    r_rep = k.tensors["r_rep"]
    lnbf = k.tensors["lnbf"]
    oc, orf = k.tensors["ones_col"], k.tensors["orf"]
    ocf = k.tensors["ones_col_f"]
    sq = k.tensors["sq_t"]
    psA, psB, psC, psD = (k.tensors[n] for n in ("psA", "psB", "psC", "psD"))
    psR1, psR2 = k.tensors["psR1"], k.tensors["psR2"]
    t1r, t3r = k.tensors["row_t1"], k.tensors["row_t3"]
    t2r = k.tensors["row_t2"][0:1, :]

    build_ln_rows(k)

    wstg = k.tensors["io_stg"]
    tmp = k.tensors["io_stg"]
    acc = k.sb("f0_acc", [128, 8], F32)
    k.sched.barrier()
    rep = k.sb("f0_rep", [128, 1024], F32)
    wb = k.sb("f0_wb", [128, 64], F32)
    lf = k.sb("f0_lf", [1, 4096], BF16)
    wqt = k.sb("f0_wqt", [128, 2 * 1024], BF16)
    wprojq = k.sb("f0_wprojq", [128, 32 * 1024], BF16)
    scr = k.sb("f0_scr", [128, 32 * 256], BF16)
    NW = 4096.0 * 1024.0

    k.op("v", lambda e: e.memset(acc[:, 0:2], 0.0), writes=["f0_acc01"])
    for wi in range(2):
        for ch in range(32):
            st = wstg[:, 0:1024]
            stn = "f0_wstg0"
            if wi == 0:
                src_ap = W["l0f_wfcT"][(ch // 4) * 128:(ch // 4 + 1) * 128,
                                       (ch % 4) * 1024:(ch % 4 + 1) * 1024]
            else:
                src_ap = W["l0f_wprojT"][ch * 128:(ch + 1) * 128, :]
            k.dma("y", st, src_ap, writes=[stn])
            k.op("v", lambda e, st=st: e.tensor_reduce(
                acc[:, 4:5], st, AX, ALU.add, apply_absolute_value=True),
                reads=[stn], writes=["f0_acc4"])
            k.op("v", lambda e, wi=wi: e.tensor_add(
                acc[:, wi:wi + 1], acc[:, wi:wi + 1], acc[:, 4:5]),
                reads=["f0_acc4"], writes=["f0_acc01"])
    for wi in range(2):
        k.mm(psR1[:, 0:1], ocf[:], acc[:, wi:wi + 1], True, True,
             reads=["ones_col_f", "f0_acc01"], writes=["psR1"])
        k.op("v", lambda e, wi=wi: e.tensor_scalar(
            t1r[0:1, wi:wi + 1], psR1[:, 0:1], 0.5 / NW, 0.5e-6, ALU.mult, ALU.add),
            reads=["psR1"], writes=["row_t1"])
    k.op("v", lambda e: e.tensor_scalar_mul(t1r[0:1, 2:4], t1r[0:1, 0:2], -1.0),
         reads=["row_t1"], writes=["row_t1"])
    for j, src_c in enumerate((0, 2, 1, 3)):
        k.mm(psA[:, j:j + 1], orf[:], t1r[0:1, src_c:src_c + 1], True, True,
             reads=["row_t1", "orf"], writes=["psA"])
    k.op("s", lambda e: e.copy(acc[:, 2:6], psA[:, 0:4]),
         reads=["psA"], writes=["f0_thr"])

    def quant_chunk(src_ap, dst, dstn, sl, thr_c, nthr_c):
        st = wstg[:, 0:1024]
        stn = "f0_wstg0"
        pos = scr[:, sl * 1024:sl * 1024 + 1024]
        posn = f"f0_pos{sl}"
        k.dma("y", st, src_ap, writes=[stn])
        k.op("g", lambda e, st=st, pos=pos, thr_c=thr_c: e.tensor_scalar(
            pos, st, acc[:, thr_c:thr_c + 1], None, ALU.is_gt),
            reads=[stn, "f0_thr"], writes=[posn])
        k.op("v", lambda e, st=st, pos=pos, dst=dst, nthr_c=nthr_c:
             e.scalar_tensor_tensor(dst, st, acc[:, nthr_c:nthr_c + 1], pos,
                                    ALU.is_lt, ALU.subtract),
             reads=[stn, posn, "f0_thr"], writes=[dstn])
        k.op("v", lambda e, dst=dst: e.tensor_scalar_mul(dst, dst, -1.0),
             reads=[dstn], writes=[dstn])

    ci = 0
    for hc in range(4):
        for dmb in range(8):
            sl = ci % 2
            dst = wqt[:, sl * 1024:(sl + 1) * 1024]
            dstn = f"f0_wqt{sl}"
            quant_chunk(W["l0f_wfcT"][dmb * 128:(dmb + 1) * 128,
                                      hc * 1024:(hc + 1) * 1024],
                        dst, dstn, sl, 2, 3)
            k.dma("g", W["l0f_wfcq_dram"][dmb * 128:(dmb + 1) * 128,
                                          hc * 1024:(hc + 1) * 1024], dst,
                  reads=[dstn], writes=[])
            for j in range(2):
                ps = psR1 if j == 0 else psR2
                pn = "psR1" if j == 0 else "psR2"
                k.mm(ps[:], oc[:], dst[:, j * 512:(j + 1) * 512],
                     start=dmb == 0, stop=dmb == 7,
                     reads=["ones_col", dstn], writes=[pn])
            if dmb == 7:
                for j in range(2):
                    ps = psR1 if j == 0 else psR2
                    pn = "psR1" if j == 0 else "psR2"
                    k.op("s", lambda e, hc=hc, j=j, ps=ps: e.activation(
                        lf[0:1, hc * 1024 + j * 512:hc * 1024 + j * 512 + 512],
                        ps[:], AF.Copy, scale=-1.0),
                        reads=[pn], writes=["f0_lf"])
            ci += 1
    for ch in range(32):
        quant_chunk(W["l0f_wprojT"][ch * 128:(ch + 1) * 128, :],
                    wprojq[:, ch * 1024:(ch + 1) * 1024], "f0_wprojq", ch % 2, 4, 5)

    for ri, wnm in enumerate(("l0f_lnw", "l0f_lnb")):
        for ch in range(8):
            k.dma("g", t3r[0:1, :], W[wnm][0:1, ch * 512:(ch + 1) * 512],
                  reads=["row_t3"], writes=["row_t3"])
            for j in range(4):
                k.mm(psB[:, j:j + 1], t3r[0:1, j * 128:(j + 1) * 128],
                     orf[0:1, 0:1], True, True,
                     reads=["row_t3", "orf"], writes=["psB"])
            k.op("s", lambda e, ri=ri, ch=ch: e.copy(
                wb[:, ri * 32 + ch * 4:ri * 32 + ch * 4 + 4], psB[:, 0:4]),
                reads=["psB"], writes=["f0_wb"])
    k.sched.barrier()

    CH = 256
    for fc in range(TB // CH):
        f0 = fc * CH
        for hb in range(32):
            ps = psA if hb % 2 == 0 else psB
            pn = "psA" if hb % 2 == 0 else "psB"
            wqs = wqt[:, (hb % 2) * 1024:(hb % 2) * 1024 + 1024]
            wqsn = f"f0_wqt{hb % 2}"
            if fc % 4 == 0 or True:
                k.dma("y", wqs.rearrange("p (a c) -> p a c", a=8),
                      W["l0f_wfcq_dram"].rearrange("(a p) c -> p a c", p=128)
                      [:, :, hb * 128:(hb + 1) * 128],
                      reads=[], writes=[wqsn])
            for db in range(DBLK):
                k.mm(ps[:, 0:CH], wqs[:, db * 128:(db + 1) * 128],
                     x[:, db * TB + f0:db * TB + f0 + CH],
                     start=db == 0, stop=False, reads=[wqsn, f"x{db}"], writes=[pn])
            k.mm(ps[:, 0:CH], lf[0:1, hb * 128:(hb + 1) * 128], lnbf[0:1, f0:f0 + CH],
                 start=False, stop=True, reads=["f0_lf", "lnbf"], writes=[pn])
            tm = tmp[:, (hb % 2) * CH:(hb % 2) * CH + CH]
            tmn = f"f0_tmp{hb % 2}"
            k.op("v", lambda e, tm=tm, ps=ps, f0=f0: e.tensor_mul(
                tm, ps[:, 0:CH], r_rep[:, f0:f0 + CH]), reads=[pn, "r_rep"],
                writes=[tmn])
            hg = scr[:, hb * CH:(hb + 1) * CH]
            hgn = f"f0_scr{hb}"
            k.op("s", lambda e, hg=hg, tm=tm: e.activation(hg, tm, AF.Gelu),
                 reads=[tmn], writes=[hgn])
            sqs = sq[:, (hb % 2) * CH:(hb % 2) * CH + CH]
            k.op("s", lambda e, sqs=sqs, hg=hg: e.activation(sqs, hg, AF.Square),
                 reads=[hgn], writes=[f"sq{hb % 2}"])
            k.mm(psR1[:, 0:CH], oc[:], hg, start=hb == 0, stop=hb == 31,
                 reads=["ones_col", hgn], writes=["psR1"])
            k.mm(psR2[:, 0:CH], oc[:], sqs, start=hb == 0, stop=hb == 31,
                 reads=["ones_col", f"sq{hb % 2}"], writes=["psR2"])
        H4 = 4096.0
        t1c, t3c, t2c = t1r[0:1, 0:CH], t3r[0:1, 0:CH], t2r[0:1, 0:CH]
        k.op("v", lambda e: e.tensor_scalar_mul(t1c, psR1[:, 0:CH], 1.0 / H4),
             reads=["psR1"], writes=["row_t1"])
        k.op("v", lambda e: e.tensor_scalar_mul(t3c, psR2[:, 0:CH], 1.0 / H4),
             reads=["psR2"], writes=["row_t3"])
        k.op("s", lambda e: e.activation(t2c, t1c, AF.Square),
             reads=["row_t1"], writes=["row_t2"])
        k.op("v", lambda e: e.tensor_sub(t3c, t3c, t2c),
             reads=["row_t3", "row_t2"], writes=["row_t3"])
        k.op("s", lambda e: e.activation(t3c, t3c, AF.Ln, bias=1e-5),
             reads=["row_t3"], writes=["row_t3"])
        k.op("s", lambda e: e.activation(t3c, t3c, AF.Exp, scale=-0.5),
             reads=["row_t3"], writes=["row_t3"])
        k.mm(psC[:, 0:CH], orf[:], t1c, True, True,
             reads=["orf", "row_t1"], writes=["psC"])
        k.op("s", lambda e: e.copy(rep[:, 0:CH], psC[:, 0:CH]),
             reads=["psC"], writes=["f0_rep"])
        k.mm(psD[:, 0:CH], orf[:], t3c, True, True,
             reads=["orf", "row_t3"], writes=["psD"])
        k.op("s", lambda e: e.copy(rep[:, 512:512 + CH], psD[:, 0:CH]),
             reads=["psD"], writes=["f0_rep"])
        for hb in range(32):
            hg = scr[:, hb * CH:(hb + 1) * CH]
            hgn = f"f0_scr{hb}"
            tm = tmp[:, 512 + (hb % 2) * CH:512 + (hb % 2) * CH + CH]
            tmn = f"f0_tmp2{hb % 2}"
            k.op("v", lambda e, tm=tm, hg=hg: e.tensor_sub(tm, hg, rep[:, 0:CH]),
                 reads=[hgn, "f0_rep"], writes=[tmn])
            k.op("v", lambda e, tm=tm: e.tensor_mul(tm, tm, rep[:, 512:512 + CH]),
                 reads=[tmn, "f0_rep"], writes=[tmn])
            k.op("g", lambda e, hg=hg, tm=tm, hb=hb: e.tensor_scalar(
                hg, tm, wb[:, hb:hb + 1], wb[:, 32 + hb:32 + hb + 1],
                ALU.mult, ALU.add),
                reads=[tmn, "f0_wb"], writes=[hgn])
        for db in range(DBLK):
            ps = psA if db % 2 == 0 else psB
            pn = "psA" if db % 2 == 0 else "psB"
            for hb in range(32):
                k.mm(ps[:, 0:CH],
                     wprojq[:, hb * 1024 + db * 128:hb * 1024 + (db + 1) * 128],
                     scr[:, hb * CH:(hb + 1) * CH],
                     start=hb == 0, stop=hb == 31,
                     reads=["f0_wprojq", f"f0_scr{hb}"], writes=[pn])
            xs = x[:, db * TB + f0:db * TB + f0 + CH]
            k.op("v", lambda e, xs=xs, ps=ps: e.tensor_add(xs, ps[:, 0:CH], xs),
                 reads=[pn], writes=[f"x{db}"])

    k.sched.barrier()



def build_l1_attn(k: K, W):
    x = k.tensors["x_sb"]
    r_rep = k.tensors["r_rep"]
    oc, orow, orf = k.tensors["ones_col"], k.tensors["ones_row"], k.tensors["orf"]
    sq = k.tensors["sq_t"]
    psA, psB, psC, psD = (k.tensors[n] for n in ("psA", "psB", "psC", "psD"))
    psR1 = k.tensors["psR1"]
    t1 = k.tensors["row_t1"]

    k.sb("recipbf", [1, TB], BF16)
    k.sb("rcol", [128, 32], F32)
    build_ln_rows(k)
    lnbf = k.tensors["lnbf"]
    recipbf = k.tensors["recipbf"]
    rcol = k.tensors["rcol"]

    kT = k.sb("l1_kT", [128, 8 * T], BF16)
    vb = k.sb("l1_v", [128, 16 * 1024], BF16)
    qch = k.sb("l1_q", [128, 8 * 512], BF16)
    P = k.sb("l1_p", [128, 16 * 512], BF16)
    ybuf = qch
    ws = k.sb("l1_ws", [128, 2 * 1024], BF16)
    lf = k.sb("l1_lf", [1, 3072], BF16)
    bbf = k.sb("l1_bb", [1, 3072], BF16)
    bf32 = k.tensors["io_stg"]
    bpcol = k.sb("l1_bpcol", [128, 8], F32)
    srep = k.sb("l1_srep", [128, 512], F32)
    stg = k.tensors["io_stg"]
    SC = 1.0 / 32.0
    t3 = k.tensors["row_t3"]

    def wtile(src_ap, slot, tag):
        st = stg[:, 0:1024]
        k.dma("y", st.rearrange("p (a c) -> p a c", a=8),
              src_ap.rearrange("(a p) c -> p a c", p=128), writes=["io_stg"])
        k.op("g", lambda e, slot=slot: e.tensor_copy(slot, st[:, 0:1024]),
             reads=["io_stg"], writes=[tag])

    k.dma("g", bf32[0:1, :], W["l1a_batt"][0:1, 0:1024], writes=["l1_bf32"])
    k.dma("g", bf32[32:33, :], W["l1a_batt"][0:1, 1024:2048], writes=["l1_bf32"])
    k.dma("g", bf32[64:65, :], W["l1a_batt"][0:1, 2048:3072], writes=["l1_bf32"])
    for i in range(3):
        k.op("s", lambda e, i=i: e.copy(bbf[0:1, i * 1024:(i + 1) * 1024],
                                        bf32[32 * i:32 * i + 1, :]),
             reads=["l1_bf32"], writes=["l1_bbf"])
    for hh in range(2):
        k.dma("g", t3[0:1, :], W["l1a_bproj"][0:1, hh * 512:(hh + 1) * 512],
              reads=["row_t3"], writes=["row_t3"])
        for j in range(4):
            k.mm(psD[:, j:j + 1], t3[0:1, j * 128:(j + 1) * 128], orf[0:1, 0:1],
                 True, True, reads=["row_t3", "orf"], writes=["psD"])
        k.op("s", lambda e, hh=hh: e.copy(bpcol[:, hh * 4:hh * 4 + 4], psD[:, 0:4]),
             reads=["psD"], writes=["l1_bpcol"])
    k.sched.barrier()
    for ot in range(24):
        slot = ws[:, (ot % 2) * 1024:(ot % 2) * 1024 + 1024]
        stag = f"l1_ws{ot % 2}"
        wtile(W["l1a_watt"][:, ot * 128:(ot + 1) * 128], slot, stag)
        for cb in range(8):
            k.mm(psR1[:, 0:128], oc[:], slot[:, cb * 128:(cb + 1) * 128],
                 start=cb == 0, stop=cb == 7, reads=["ones_col", stag],
                 writes=["psR1"])
        k.op("s", lambda e, ot=ot: e.activation(lf[0:1, ot * 128:(ot + 1) * 128],
                                                psR1[:, 0:128], AF.Copy, scale=-1.0),
             reads=["psR1"], writes=["l1_lf"])
    for b in range(BPC):
        bt = b * T
        for ot in range(8):
            slot = ws[:, (ot % 2) * 1024:(ot % 2) * 1024 + 1024]
            stag = f"l1_ws{ot % 2}"
            wtile(W["l1a_watt"][:, 1024 + ot * 128:1024 + (ot + 1) * 128], slot, stag)
            for fc in range(4):
                f0 = bt + fc * 512
                ps = psA if fc % 2 == 0 else psB
                pn = "psA" if fc % 2 == 0 else "psB"
                for cb in range(DBLK):
                    k.mm(ps[:], slot[:, cb * 128:(cb + 1) * 128],
                         x[:, cb * TB + f0:cb * TB + f0 + 512],
                         start=cb == 0, stop=False, reads=[stag, f"x{cb}"],
                         writes=[pn])
                k.mm(ps[:], lf[0:1, 1024 + ot * 128:1024 + (ot + 1) * 128],
                     lnbf[0:1, f0:f0 + 512], start=False, stop=False,
                     reads=["l1_lf", "lnbf"], writes=[pn])
                k.mm(ps[:], bbf[0:1, 1024 + ot * 128:1024 + (ot + 1) * 128],
                     recipbf[0:1, f0:f0 + 512], start=False, stop=True,
                     reads=["l1_bbf", "recipbf"], writes=[pn])
                k.op("v", lambda e, ot=ot, fc=fc, ps=ps, f0=f0: e.tensor_mul(
                    kT[:, ot * T + fc * 512:ot * T + fc * 512 + 512],
                    ps[:], r_rep[:, f0:f0 + 512]),
                    reads=[pn, "r_rep"], writes=["l1_kT"])
        for vt in range(8):
            slot = ws[:, (vt % 2) * 1024:(vt % 2) * 1024 + 1024]
            stag = f"l1_ws{vt % 2}"
            wtile(W["l1a_watt"][:, 2048 + vt * 128:2048 + (vt + 1) * 128], slot, stag)
            for tt in range(16):
                t0 = bt + tt * 128
                ps = psC if tt % 2 == 0 else psD
                pn = "psC" if tt % 2 == 0 else "psD"
                for cb in range(DBLK):
                    k.mm(ps[:, 0:128], x[:, cb * TB + t0:cb * TB + t0 + 128],
                         slot[:, cb * 128:(cb + 1) * 128],
                         start=cb == 0, stop=False, reads=[stag, f"x{cb}"],
                         writes=[pn])
                k.mm(ps[:, 0:128], lnbf[0:1, t0:t0 + 128],
                     lf[0:1, 2048 + vt * 128:2048 + (vt + 1) * 128],
                     start=False, stop=False, reads=["l1_lf", "lnbf"], writes=[pn])
                k.mm(ps[:, 0:128], recipbf[0:1, t0:t0 + 128],
                     bbf[0:1, 2048 + vt * 128:2048 + (vt + 1) * 128],
                     start=False, stop=True, reads=["l1_bbf", "recipbf"], writes=[pn])
                k.op("v", lambda e, tt=tt, vt=vt, ps=ps, b=b: e.tensor_scalar_mul(
                    vb[:, tt * 1024 + vt * 128:tt * 1024 + vt * 128 + 128],
                    ps[:, 0:128], rcol[:, b * 16 + tt:b * 16 + tt + 1]),
                    reads=[pn, "rcol"], writes=["l1_vb"])
        for qc in range(4):
            f0 = bt + qc * 512
            for ot in range(8):
                slot = ws[:, (ot % 2) * 1024:(ot % 2) * 1024 + 1024]
                stag = f"l1_ws{ot % 2}"
                wtile(W["l1a_watt"][:, ot * 128:(ot + 1) * 128], slot, stag)
                ps = psA if ot % 2 == 0 else psB
                pn = "psA" if ot % 2 == 0 else "psB"
                for cb in range(DBLK):
                    k.mm(ps[:], slot[:, cb * 128:(cb + 1) * 128],
                         x[:, cb * TB + f0:cb * TB + f0 + 512],
                         start=cb == 0, stop=False, reads=[stag, f"x{cb}"],
                         writes=[pn])
                k.mm(ps[:], lf[0:1, ot * 128:(ot + 1) * 128],
                     lnbf[0:1, f0:f0 + 512], start=False, stop=False,
                     reads=["l1_lf", "lnbf"], writes=[pn])
                k.mm(ps[:], bbf[0:1, ot * 128:(ot + 1) * 128],
                     recipbf[0:1, f0:f0 + 512], start=False, stop=True,
                     reads=["l1_bbf", "recipbf"], writes=[pn])
                k.op("v", lambda e, ot=ot, ps=ps, f0=f0: e.tensor_mul(
                    qch[:, ot * 512:(ot + 1) * 512], ps[:], r_rep[:, f0:f0 + 512]),
                    reads=[pn, "r_rep"], writes=["l1_q"])
            kmax = qc * 4 + 4
            for kt in range(kmax):
                ps = psC if kt % 2 == 0 else psD
                pn = "psC" if kt % 2 == 0 else "psD"
                for cb in range(8):
                    k.mm(ps[:], kT[:, cb * T + kt * 128:cb * T + (kt + 1) * 128],
                         qch[:, cb * 512:(cb + 1) * 512],
                         start=cb == 0, stop=cb == 7,
                         reads=["l1_kT", "l1_q"], writes=[pn])
                delta = kt * 128 - qc * 512
                pdst = P[:, kt * 512:(kt + 1) * 512]
                if delta < 0:
                    k.op("s", lambda e, pdst=pdst, ps=ps: e.activation(
                        pdst, ps[:], AF.Exp, scale=SC),
                        reads=[pn], writes=["l1_P"])
                else:
                    pt = sq[:, 0:512]
                    k.op("s", lambda e, pt=pt, ps=ps: e.activation(
                        pt, ps[:], AF.Exp, scale=SC), reads=[pn], writes=["sq0"])
                    k.op("g", lambda e, pdst=pdst, pt=pt, delta=delta:
                         e.affine_select(pdst, pt, [[1, 512]], ALU.is_ge, 0.0,
                                         base=-delta, channel_multiplier=-1),
                         reads=["sq0"], writes=["l1_P"])
            for kt in range(kmax):
                k.mm(psR1[:], oc[:, 0:1],
                     P[:, kt * 512:(kt + 1) * 512], start=kt == 0, stop=kt == kmax - 1,
                     reads=["ones_col", "l1_P"], writes=["psR1"])
            k.op("v", lambda e: e.reciprocal(t1[:], psR1[:]),
                 reads=["psR1"], writes=["row_t1"])
            k.mm(psC[:], orf[:], t1[:], True, True,
                 reads=["orf", "row_t1"], writes=["psC"])
            k.op("s", lambda e: e.copy(srep[:], psC[:]), reads=["psC"],
                 writes=["l1_srep"])
            for ct in range(8):
                ps = psA if ct % 2 == 0 else psB
                pn = "psA" if ct % 2 == 0 else "psB"
                for kt in range(kmax):
                    k.mm(ps[:], vb[:, kt * 1024 + ct * 128:kt * 1024 + ct * 128 + 128],
                         P[:, kt * 512:(kt + 1) * 512],
                         start=kt == 0, stop=kt == kmax - 1,
                         reads=["l1_vb", "l1_P"], writes=[pn])
                k.op("v", lambda e, ct=ct, ps=ps: e.tensor_mul(
                    ybuf[:, ct * 512:(ct + 1) * 512], ps[:], srep[:]),
                    reads=[pn, "l1_srep"], writes=["l1_y"])
            for db in range(DBLK):
                slot = ws[:, (db % 2) * 1024:(db % 2) * 1024 + 1024]
                stag = f"l1_ws{db % 2}"
                wtile(W["l1a_wproj"][:, db * 128:(db + 1) * 128], slot, stag)
                ps = psC if db % 2 == 0 else psD
                pn = "psC" if db % 2 == 0 else "psD"
                for cb in range(8):
                    k.mm(ps[:], slot[:, cb * 128:(cb + 1) * 128],
                         ybuf[:, cb * 512:(cb + 1) * 512],
                         start=cb == 0, stop=cb == 7, reads=[stag, "l1_y"], writes=[pn])
                xs = x[:, db * TB + f0:db * TB + f0 + 512]
                k.op("v", lambda e, xs=xs, ps=ps, db=db: e.scalar_tensor_tensor(
                    xs, ps[:], bpcol[:, db:db + 1], xs, ALU.add, ALU.add),
                    reads=[pn, "l1_bpcol"], writes=[f"x{db}"])

    k.sched.barrier()



def build_l1_ffn(k: K, W):
    x = k.tensors["x_sb"]
    r_rep = k.tensors["r_rep"]
    oc, orow, orf = k.tensors["ones_col"], k.tensors["ones_row"], k.tensors["orf"]
    sq = k.tensors["sq_t"]
    psA, psB, psC, psD = (k.tensors[n] for n in ("psA", "psB", "psC", "psD"))
    psR1 = k.tensors["psR1"]
    t3 = k.tensors["row_t3"]

    k.sb("recipbf", [1, TB], BF16)
    build_ln_rows(k)
    lnbf = k.tensors["lnbf"]
    recipbf = k.tensors["recipbf"]
    stg = k.tensors["io_stg"]
    ocf = k.tensors["ones_col_f"]
    t1 = k.tensors["row_t1"]
    t2 = k.tensors["row_t2"][0:1, :]

    h = k.sb("f1_h", [128, 8 * 2048], BF16)
    x32 = k.sb("f1_x32", [128, 8 * 256], F32)
    sq32 = k.sb("f1_sq32", [128, 256], F32)
    wr32 = k.sb("f1_wr32", [128, 128], F32)
    rcol32 = k.sb("f1_rcol32", [128, 4], F32)
    r32 = k.sb("f1_r32", [1, 64], F32)
    w2 = k.sb("f1_w2", [128, 8 * 1024], BF16)
    ws = k.sb("f1_ws", [128, 2 * 1024], BF16)
    b2bf = k.sb("f1_b2", [16, 1024], BF16)
    sel = k.sb("f1_sel", [16, 1024], BF16)
    ident = k.sb("ident_bf", [128, 128], BF16)
    gT = k.sb("f1_gT", [16, TB], BF16)
    logits = k.sb("f1_logits", [128, 32 * 16], F32)
    gwork = k.sb("f1_gw", [128, 64], F32)
    gb = k.sb("f1_gb", [128, 16], BF16)
    rowsA = k.sb("f1_rowsA", [1, 2048], BF16)
    rowsB = k.sb("f1_rowsB", [1, 1024], BF16)
    negone = k.sb("negone_row", [1, 128], BF16)
    k.op("v", lambda e: e.memset(negone[:], -1.0), writes=["negone_row"])
    k.op("v", lambda e: e.memset(ident[:], 1.0), writes=["ident_bf"])
    k.op("g", lambda e: e.affine_select(ident[:], ident[:], [[1, 128]],
                                        ALU.is_equal, 0.0, base=0,
                                        channel_multiplier=-1),
         reads=["ident_bf"], writes=["ident_bf"])

    def wtile(src_ap, slot, tag):
        st = stg[:, 0:1024]
        k.dma("y", st.rearrange("p (a c) -> p a c", a=8),
              src_ap.rearrange("(a p) c -> p a c", p=128), writes=["io_stg"])
        k.op("g", lambda e, slot=slot: e.tensor_copy(slot, st[:, 0:1024]),
             reads=["io_stg"], writes=[tag])

    for rb in range(8):
        k.dma("y", stg[:, 0:1024], W["l1f_w2s"][rb * 128:(rb + 1) * 128, :],
              writes=["io_stg"])
        k.op("g", lambda e, rb=rb: e.tensor_copy(w2[:, rb * 1024:(rb + 1) * 1024],
                                                 stg[:, 0:1024]),
             reads=["io_stg"], writes=["f1_w2"])
    k.dma("y", wr32[:].rearrange("p (a c) -> p a c", a=8),
          W["l1f_wr"].rearrange("(a p) c -> p a c", p=128), writes=["f1_wr32"])
    k.dma("y", stg[0:16, 0:1024], W["l1f_b2"][:, :], writes=["io_stg"])
    k.op("s", lambda e: e.copy(b2bf[:], stg[0:16, 0:1024]),
         reads=["io_stg"], writes=["f1_b2"])
    k.dma("y", stg[0:16, 0:1024], W["l1f_sel"][:, :], writes=["io_stg"])
    k.op("s", lambda e: e.copy(sel[:], stg[0:16, 0:1024]),
         reads=["io_stg"], writes=["f1_sel"])
    k.dma("y", stg[0:1, 0:1024], W["l1f_b1"][0:1, :], writes=["io_stg"])
    k.op("s", lambda e: e.copy(rowsA[0:1, 1024:2048], stg[0:1, 0:1024]),
         reads=["io_stg"], writes=["f1_rowsA"])
    k.dma("y", r32[0:1, 16:32], W["l1f_br"][0:1, :], writes=["f1_r32"])
    for cb in range(8):
        k.mm(psR1[:, 0:16], ocf[:], wr32[:, cb * 16:(cb + 1) * 16],
             start=cb == 0, stop=cb == 7, reads=["ones_col_f", "f1_wr32"],
             writes=["psR1"])
    k.op("s", lambda e: e.activation(r32[0:1, 0:16], psR1[:, 0:16],
                                     AF.Copy, scale=-1.0),
         reads=["psR1"], writes=["f1_r32"])
    k.sched.barrier()

    BIG = 1.0e9
    for fc in range(16):
        f0 = fc * 256
        for db in range(DBLK):
            k.dma("y", x32[:, db * 256:(db + 1) * 256],
                  W["x32"][db * 128:(db + 1) * 128, f0:f0 + 256],
                  writes=[f"f1_x32_{db}"])
        for db in range(DBLK):
            xsl = x32[:, db * 256:(db + 1) * 256]
            k.op("s", lambda e, xsl=xsl: e.activation(sq32[:], xsl, AF.Square),
                 reads=[f"f1_x32_{db}"], writes=["f1_sq32"])
            k.mm(psR1[:, 0:256], ocf[:], xsl, start=db == 0, stop=db == DBLK - 1,
                 reads=["ones_col_f", f"f1_x32_{db}"], writes=["psR1"])
            k.mm(k.tensors["psR2"][:, 0:256], ocf[:], sq32[:], start=db == 0,
                 stop=db == DBLK - 1, reads=["ones_col_f", "f1_sq32"],
                 writes=["psR2"])
        t1c, t3c, t2c = t1[0:1, 0:256], t3[0:1, 0:256], t2[0:1, 0:256]
        k.op("v", lambda e: e.tensor_scalar_mul(t1c, psR1[:, 0:256], 1.0 / D),
             reads=["psR1"], writes=["row_t1"])
        k.op("v", lambda e: e.tensor_scalar_mul(t3c, k.tensors["psR2"][:, 0:256],
                                                1.0 / D),
             reads=["psR2"], writes=["row_t3"])
        k.op("s", lambda e: e.activation(t2c, t1c, AF.Square),
             reads=["row_t1"], writes=["row_t2"])
        k.op("v", lambda e: e.tensor_sub(t3c, t3c, t2c),
             reads=["row_t3", "row_t2"], writes=["row_t3"])
        k.op("s", lambda e: e.activation(t2c, t3c, AF.Sqrt, bias=1e-5),
             reads=["row_t3"], writes=["row_t2"])
        k.op("s", lambda e: e.activation(t3c, t3c, AF.Ln, bias=1e-5),
             reads=["row_t3"], writes=["row_t3"])
        k.op("s", lambda e: e.activation(t3c, t3c, AF.Exp, scale=-0.5),
             reads=["row_t3"], writes=["row_t3"])
        for j in range(2):
            k.mm(psD[:, j:j + 1], t3[0:1, j * 128:(j + 1) * 128], ocf[0:1, 0:1],
                 True, True, reads=["row_t3", "ones_col_f"], writes=["psD"])
        k.op("s", lambda e: e.copy(rcol32[:, 0:2], psD[:, 0:2]),
             reads=["psD"], writes=["f1_rcol32"])
        for ti in range(2):
            tt = fc * 2 + ti
            t0l = ti * 128
            lt = logits[:, tt * 16:(tt + 1) * 16]
            pslt = psC[:, 0:16]
            for db in range(DBLK):
                k.mm(pslt, x32[:, db * 256 + t0l:db * 256 + t0l + 128],
                     wr32[:, db * 16:(db + 1) * 16],
                     start=db == 0, stop=False,
                     reads=[f"f1_x32_{db}", "f1_wr32"], writes=["psC"])
            k.mm(pslt, t1[0:1, t0l:t0l + 128], r32[0:1, 0:16],
                 start=False, stop=False, reads=["row_t1", "f1_r32"], writes=["psC"])
            k.mm(pslt, t2[0:1, t0l:t0l + 128], r32[0:1, 16:32],
                 start=False, stop=True, reads=["row_t2", "f1_r32"], writes=["psC"])
            k.op("v", lambda e, lt=lt, ti=ti: e.tensor_scalar_mul(
                lt, psC[:, 0:16], rcol32[:, ti:ti + 1]),
                reads=["psC", "f1_rcol32"], writes=["f1_logits"])
    for tt in range(32):
        t0 = tt * 128
        lt = logits[:, tt * 16:(tt + 1) * 16]
        k.op("v", lambda e, lt=lt: e.tensor_reduce(gwork[:, 0:1], lt, AX, ALU.max),
             reads=["f1_logits"], writes=["f1_gw"])
        k.op("v", lambda e, lt=lt: e.tensor_scalar(gwork[:, 16:32], lt,
                                                   gwork[:, 0:1], None, ALU.is_ge),
             reads=["f1_logits", "f1_gw"], writes=["f1_gw"])
        k.op("v", lambda e, lt=lt: e.scalar_tensor_tensor(
            gwork[:, 32:48], gwork[:, 16:32], -BIG, lt, ALU.mult, ALU.add),
            reads=["f1_logits", "f1_gw"], writes=["f1_gw"])
        k.op("v", lambda e: e.tensor_reduce(gwork[:, 1:2], gwork[:, 32:48],
                                            AX, ALU.max),
             reads=["f1_gw"], writes=["f1_gw"])
        k.op("v", lambda e: e.tensor_scalar(gwork[:, 48:64], gwork[:, 32:48],
                                            gwork[:, 1:2], None, ALU.is_ge),
             reads=["f1_gw"], writes=["f1_gw"])
        k.op("v", lambda e: e.tensor_add(gb[:], gwork[:, 16:32], gwork[:, 48:64]),
             reads=["f1_gw"], writes=["f1_gb"])
        k.mm(psD[0:16, 0:128], gb[:], ident[:], True, True,
             reads=["f1_gb", "ident_bf"], writes=["psD"])
        k.op("s", lambda e, t0=t0: e.copy(gT[:, t0:t0 + 128], psD[0:16, 0:128]),
             reads=["psD"], writes=["f1_gT"])

    for half in range(2):
        bt = half * 2048
        for ehb in range(8):
            slot = ws[:, (ehb % 2) * 1024:(ehb % 2) * 1024 + 1024]
            stag = f"f1_ws{ehb % 2}"
            wtile(W["l1f_w1s"][:, ehb * 128:(ehb + 1) * 128], slot, stag)
            if half == 0:
                for cb in range(8):
                    k.mm(psR1[:, 0:128], oc[:], slot[:, cb * 128:(cb + 1) * 128],
                         start=cb == 0, stop=cb == 7, reads=["ones_col", stag],
                         writes=["psR1"])
                k.op("s", lambda e, ehb=ehb: e.activation(
                    rowsA[0:1, ehb * 128:(ehb + 1) * 128], psR1[:, 0:128],
                    AF.Copy, scale=-1.0), reads=["psR1"], writes=["f1_rowsA"])
            for fcl in range(4):
                f0 = bt + fcl * 512
                ps = psA if fcl % 2 == 0 else psB
                pn = "psA" if fcl % 2 == 0 else "psB"
                for cb in range(DBLK):
                    k.mm(ps[:], slot[:, cb * 128:(cb + 1) * 128],
                         x[:, cb * TB + f0:cb * TB + f0 + 512],
                         start=cb == 0, stop=False, reads=[stag, f"x{cb}"],
                         writes=[pn])
                k.mm(ps[:], rowsA[0:1, ehb * 128:(ehb + 1) * 128],
                     lnbf[0:1, f0:f0 + 512], start=False, stop=False,
                     reads=["f1_rowsA", "lnbf"], writes=[pn])
                k.mm(ps[:], rowsA[0:1, 1024 + ehb * 128:1024 + (ehb + 1) * 128],
                     recipbf[0:1, f0:f0 + 512], start=False, stop=True,
                     reads=["f1_rowsA", "recipbf"], writes=[pn])
                hs = h[:, ehb * 2048 + fcl * 512:ehb * 2048 + fcl * 512 + 512]
                hn = f"f1_h{ehb}"
                k.op("v", lambda e, hs=hs, ps=ps, f0=f0: e.scalar_tensor_tensor(
                    hs, ps[:], 0.0, r_rep[:, f0:f0 + 512], ALU.max, ALU.mult),
                    reads=[pn, "r_rep"], writes=[hn])
                psg = psC if fcl % 2 == 0 else psD
                pgn = "psC" if fcl % 2 == 0 else "psD"
                k.mm(psg[:], sel[:, ehb * 128:(ehb + 1) * 128], gT[:, f0:f0 + 512],
                     True, True, reads=["f1_sel", "f1_gT"], writes=[pgn])
                k.op("v", lambda e, hs=hs, psg=psg: e.tensor_mul(hs, psg[:], hs),
                     reads=[pgn, hn], writes=[hn])

        for fcl in range(4):
            f0 = bt + fcl * 512
            k.op("v", lambda e, f0=f0: e.tensor_mul(
                rowsB[0:1, 512:1024], lnbf[0:1, f0:f0 + 512],
                r_rep[0:1, f0:f0 + 512]),
                reads=["lnbf", "r_rep"], writes=["f1_rowsB"])
            for db in range(DBLK):
                ps = psA if db % 2 == 0 else psB
                pn = "psA" if db % 2 == 0 else "psB"
                for ehb in range(8):
                    k.mm(ps[:],
                         w2[:, ehb * 1024 + db * 128:ehb * 1024 + (db + 1) * 128],
                         h[:, ehb * 2048 + fcl * 512:ehb * 2048 + fcl * 512 + 512],
                         start=ehb == 0, stop=False,
                         reads=["f1_w2", f"f1_h{ehb}"], writes=[pn])
                k.mm(ps[:], b2bf[:, db * 128:(db + 1) * 128], gT[:, f0:f0 + 512],
                     start=False, stop=False, reads=["f1_b2", "f1_gT"], writes=[pn])
                k.mm(ps[:], negone[:], rowsB[0:1, 512:1024],
                     start=False, stop=True, reads=["negone_row", "f1_rowsB"],
                     writes=[pn])
                tmpa = sq[:, 0:512]
                k.op("g", lambda e, tmpa=tmpa, f0=f0: e.tensor_scalar(
                    tmpa, r_rep[:, f0:f0 + 512], 1.0, None, ALU.add),
                    reads=["r_rep"], writes=["sq0"])
                xs = x[:, db * TB + f0:db * TB + f0 + 512]
                tmpb = stg[:, 0:512]
                k.op("v", lambda e, tmpb=tmpb, xs=xs, tmpa=tmpa: e.tensor_mul(
                    tmpb, xs, tmpa), reads=[f"x{db}", "sq0"], writes=["io_stg"])
                k.op("v", lambda e, xs=xs, ps=ps, tmpb=tmpb: e.tensor_add(
                    xs, ps[:], tmpb), reads=[pn, "io_stg"], writes=[f"x{db}"])

    k.sched.barrier()



def build_l2_attn(k: K, W):
    x = k.tensors["x_sb"]
    r_rep = k.tensors["r_rep"]
    oc, orow, orf = k.tensors["ones_col"], k.tensors["ones_row"], k.tensors["orf"]
    sq = k.tensors["sq_t"]
    psA, psB, psC, psD = (k.tensors[n] for n in ("psA", "psB", "psC", "psD"))
    psE1 = k.tensors["psE1"]
    psR1 = k.tensors["psR1"]
    t1 = k.tensors["row_t1"]
    t3 = k.tensors["row_t3"]

    k.sb("recipbf", [1, TB], BF16)
    k.sb("rcol", [128, 32], F32)
    build_ln_rows(k)
    lnbf = k.tensors["lnbf"]
    recipbf = k.tensors["recipbf"]
    rcol = k.tensors["rcol"]
    stg = k.tensors["io_stg"]

    kT = k.sb("l2_kT", [128, 8 * T], BF16)
    vb = k.sb("l2_v", [128, 16 * 1040], BF16)
    qch = k.sb("l2_q", [128, 8 * 512], BF16)
    P = k.sb("l2_p", [128, 2 * 512], BF16)
    ybuf = qch
    ws = k.sb("l2_ws", [128, 2 * 1024], BF16)
    lf = k.sb("l2_lf", [1, 3072], BF16)
    bbf = k.sb("l2_bb", [1, 3072], BF16)
    bpcol = k.sb("l2_bpcol", [128, 8], F32)
    srep = k.sb("l2_srep", [128, 512], F32)
    sc_t = k.sb("l2_sincos", [128, 2 * 2048], BF16)
    rt = k.sb("l2_rt", [128, 512], F32)
    swp = k.sb("l2_swp", [128, 512], F32)
    SC = 1.0 / 8.0

    def wtile(src_ap, slot, tag):
        st = stg[:, 0:1024]
        k.dma("y", st.rearrange("p (a c) -> p a c", a=8),
              src_ap.rearrange("(a p) c -> p a c", p=128), writes=["io_stg"])
        k.op("g", lambda e, slot=slot: e.tensor_copy(slot, st[:, 0:1024]),
             reads=["io_stg"], writes=[tag])

    for j, nm in enumerate(("l2a_cos4", "l2a_ssin4")):
        for hh in range(2):
            k.dma("y", stg[:, 0:1024], W[nm][:, hh * 1024:(hh + 1) * 1024],
                  writes=["io_stg"])
            k.op("s", lambda e, j=j, hh=hh: e.copy(
                sc_t[:, j * 2048 + hh * 1024:j * 2048 + (hh + 1) * 1024],
                stg[:, 0:1024]),
                reads=["io_stg"], writes=["l2_sc"])
    for i in range(3):
        k.dma("y", stg[0:1, 0:1024], W["l2a_bqkv"][0:1, i * 1024:(i + 1) * 1024],
              writes=["io_stg"])
        k.op("s", lambda e, i=i: e.copy(bbf[0:1, i * 1024:(i + 1) * 1024],
                                        stg[0:1, 0:1024]),
             reads=["io_stg"], writes=["l2_bbf"])
    for hh in range(2):
        k.dma("y", t3[0:1, :], W["l2a_bout"][0:1, hh * 512:(hh + 1) * 512],
              reads=["row_t3"], writes=["row_t3"])
        for j in range(4):
            k.mm(psD[:, j:j + 1], t3[0:1, j * 128:(j + 1) * 128], orf[0:1, 0:1],
                 True, True, reads=["row_t3", "orf"], writes=["psD"])
        k.op("s", lambda e, hh=hh: e.copy(bpcol[:, hh * 4:hh * 4 + 4], psD[:, 0:4]),
             reads=["psD"], writes=["l2_bpcol"])
    for ot in range(24):
        slot = ws[:, (ot % 2) * 1024:(ot % 2) * 1024 + 1024]
        stag = f"l2_ws{ot % 2}"
        wtile(W["l2a_wqkvT"][:, ot * 128:(ot + 1) * 128], slot, stag)
        for cb in range(8):
            k.mm(psR1[:, 0:128], oc[:], slot[:, cb * 128:(cb + 1) * 128],
                 start=cb == 0, stop=cb == 7, reads=["ones_col", stag],
                 writes=["psR1"])
        k.op("s", lambda e, ot=ot: e.activation(lf[0:1, ot * 128:(ot + 1) * 128],
                                                psR1[:, 0:128], AF.Copy, scale=-1.0),
             reads=["psR1"], writes=["l2_lf"])
    k.op("v", lambda e: e.memset(
        vb[:].rearrange("p (a c) -> p a c", c=65)[:, :, 64:65], 1.0),
        writes=["l2_vones"])
    k.sched.barrier()

    def qk_proj(sec, ot, f0, dst, dstn):
        slot = ws[:, (ot % 2) * 1024:(ot % 2) * 1024 + 1024]
        stag = f"l2_ws{ot % 2}"
        wcol = sec * 1024 + ot * 128
        wtile(W["l2a_wqkvT"][:, wcol:wcol + 128], slot, stag)
        ps = psA if ot % 2 == 0 else psB
        pn = "psA" if ot % 2 == 0 else "psB"
        for cb in range(DBLK):
            k.mm(ps[:], slot[:, cb * 128:(cb + 1) * 128],
                 x[:, cb * TB + f0:cb * TB + f0 + 512],
                 start=cb == 0, stop=False, reads=[stag, f"x{cb}"], writes=[pn])
        k.mm(ps[:], lf[0:1, wcol:wcol + 128], lnbf[0:1, f0:f0 + 512],
             start=False, stop=False, reads=["l2_lf", "lnbf"], writes=[pn])
        k.mm(ps[:], bbf[0:1, wcol:wcol + 128], recipbf[0:1, f0:f0 + 512],
             start=False, stop=True, reads=["l2_bbf", "recipbf"], writes=[pn])
        k.op("v", lambda e, ps=ps, f0=f0: e.tensor_mul(rt[:], ps[:],
                                                       r_rep[:, f0:f0 + 512]),
             reads=[pn, "r_rep"], writes=["l2_rt"])
        tl = f0 % T
        for j in range(4):
            sb_, db_ = 32 * (j ^ 1), 32 * j
            k.op("v", lambda e, sb_=sb_, db_=db_: e.tensor_copy(
                swp[db_:db_ + 32, :], rt[sb_:sb_ + 32, :]),
                reads=["l2_rt"], writes=["l2_swp"])
        k.op("v", lambda e, tl=tl: e.tensor_mul(
            swp[:], swp[:], sc_t[:, 2048 + tl:2048 + tl + 512]),
            reads=["l2_swp", "l2_sc"], writes=["l2_swp"])
        k.op("v", lambda e, tl=tl: e.tensor_mul(
            rt[:], rt[:], sc_t[:, tl:tl + 512]),
            reads=["l2_rt", "l2_sc"], writes=["l2_rt"])
        k.op("v", lambda e, dst=dst: e.tensor_add(dst, rt[:], swp[:]),
             reads=["l2_rt", "l2_swp"], writes=[dstn])

    for b in range(BPC):
        bt = b * T
        for ot in range(8):
            h0c = ot
            for fc in range(4):
                f0 = bt + fc * 512
                tl = fc * 512
                qk_proj(1, ot, f0, kT[:, h0c * T + tl:h0c * T + tl + 512],
                        "l2_kT")
        for vt in range(8):
            slot = ws[:, (vt % 2) * 1024:(vt % 2) * 1024 + 1024]
            stag = f"l2_ws{vt % 2}"
            wtile(W["l2a_wqkvT"][:, 2048 + vt * 128:2048 + (vt + 1) * 128],
                  slot, stag)
            for tt in range(16):
                t0 = bt + tt * 128
                ps = psC if tt % 2 == 0 else psD
                pn = "psC" if tt % 2 == 0 else "psD"
                for cb in range(DBLK):
                    k.mm(ps[:, 0:128], x[:, cb * TB + t0:cb * TB + t0 + 128],
                         slot[:, cb * 128:(cb + 1) * 128],
                         start=cb == 0, stop=False, reads=[stag, f"x{cb}"],
                         writes=[pn])
                k.mm(ps[:, 0:128], lnbf[0:1, t0:t0 + 128],
                     lf[0:1, 2048 + vt * 128:2048 + (vt + 1) * 128],
                     start=False, stop=False, reads=["l2_lf", "lnbf"], writes=[pn])
                k.mm(ps[:, 0:128], recipbf[0:1, t0:t0 + 128],
                     bbf[0:1, 2048 + vt * 128:2048 + (vt + 1) * 128],
                     start=False, stop=True, reads=["l2_bbf", "recipbf"],
                     writes=[pn])
                h0 = vt * 2
                for j in range(2):
                    k.op("v", lambda e, tt=tt, h0=h0, j=j, ps=ps, b=b:
                         e.tensor_scalar_mul(
                             vb[:, tt * 1040 + (h0 + j) * 65:
                                tt * 1040 + (h0 + j) * 65 + 64],
                             ps[:, j * 64:j * 64 + 64],
                             rcol[:, b * 16 + tt:b * 16 + tt + 1]),
                         reads=[pn, "rcol"], writes=["l2_vb"])
        for qc in range(4):
            f0 = bt + qc * 512
            tl = qc * 512
            for ot in range(8):
                h0c = ot
                qk_proj(0, ot, f0, qch[:, h0c * 512:h0c * 512 + 512], "l2_q")
            for h in range(16):
                cbk = h // 2
                rb = 64 * (h % 2)
                kmax = qc * 4 + 4
                for kt in range(kmax):
                    ps = psC if kt % 2 == 0 else psD
                    pn = "psC" if kt % 2 == 0 else "psD"
                    k.mm(ps[:], kT[rb:rb + 64, cbk * T + kt * 128:
                                   cbk * T + (kt + 1) * 128],
                         qch[rb:rb + 64, cbk * 512:(cbk + 1) * 512],
                         True, True, reads=["l2_kT", "l2_q"], writes=[pn])
                    delta = kt * 128 - qc * 512
                    pdst = P[:, (kt % 2) * 512:(kt % 2) * 512 + 512]
                    pdn = f"l2_P{kt % 2}"
                    if delta < 0:
                        k.op("s", lambda e, pdst=pdst, ps=ps: e.activation(
                            pdst, ps[:], AF.Exp, scale=SC),
                            reads=[pn], writes=[pdn])
                    else:
                        pt = sq[:, 512:1024]
                        k.op("s", lambda e, pt=pt, ps=ps: e.activation(
                            pt, ps[:], AF.Exp, scale=SC), reads=[pn],
                            writes=["sq1"])
                        k.op("g", lambda e, pdst=pdst, pt=pt, delta=delta:
                             e.affine_select(pdst, pt, [[1, 512]], ALU.is_ge, 0.0,
                                             base=-delta, channel_multiplier=-1),
                             reads=["sq1"], writes=[pdn])
                    k.mm(psE1[:], vb[:, kt * 1040 + h * 65:kt * 1040 + h * 65 + 65],
                         pdst, start=kt == 0, stop=kt == kmax - 1,
                         reads=["l2_vb", "l2_vones", pdn], writes=["psE1"])
                k.op("v", lambda e: e.reciprocal(t1[:], psE1[64:65, :]),
                     reads=["psE1"], writes=["row_t1"])
                k.mm(psC[0:64, :], orf[0:1, 0:64], t1[:], True, True,
                     reads=["orf", "row_t1"], writes=["psC"])
                k.op("s", lambda e: e.copy(srep[0:64, :], psC[0:64, :]),
                     reads=["psC"], writes=["l2_srep"])
                k.op("v", lambda e, cbk=cbk, rb=rb: e.tensor_mul(
                    ybuf[rb:rb + 64, cbk * 512:(cbk + 1) * 512],
                    psE1[0:64, :], srep[0:64, :]),
                    reads=["psE1", "l2_srep"], writes=["l2_y"])
            for db in range(DBLK):
                slot = ws[:, (db % 2) * 1024:(db % 2) * 1024 + 1024]
                stag = f"l2_ws{db % 2}"
                wtile(W["l2a_woutT"][:, db * 128:(db + 1) * 128], slot, stag)
                ps = psC if db % 2 == 0 else psD
                pn = "psC" if db % 2 == 0 else "psD"
                for cb in range(8):
                    k.mm(ps[:], slot[:, cb * 128:(cb + 1) * 128],
                         ybuf[:, cb * 512:(cb + 1) * 512],
                         start=cb == 0, stop=cb == 7, reads=[stag, "l2_y"],
                         writes=[pn])
                xs = x[:, db * TB + f0:db * TB + f0 + 512]
                k.op("v", lambda e, xs=xs, ps=ps, db=db: e.scalar_tensor_tensor(
                    xs, ps[:], bpcol[:, db:db + 1], xs, ALU.add, ALU.add),
                    reads=[pn, "l2_bpcol"], writes=[f"x{db}"])

    k.sched.barrier()



def build_l2_ffn(k: K, W):
    x = k.tensors["x_sb"]
    r_rep = k.tensors["r_rep"]
    oc, orow, orf = k.tensors["ones_col"], k.tensors["ones_row"], k.tensors["orf"]
    ocf = k.tensors["ones_col_f"]
    sq = k.tensors["sq_t"]
    psA, psB, psC, psD = (k.tensors[n] for n in ("psA", "psB", "psC", "psD"))
    psR1, psR2 = k.tensors["psR1"], k.tensors["psR2"]
    t3 = k.tensors["row_t3"]

    build_ln_rows(k)
    lnbf = k.tensors["lnbf"]
    stg = k.tensors["io_stg"]

    ident = k.sb("ident_bf", [128, 128], BF16)
    k.op("v", lambda e: e.memset(ident[:], 1.0), writes=["ident_bf"])
    k.op("g", lambda e: e.affine_select(ident[:], ident[:], [[1, 128]],
                                        ALU.is_equal, 0.0, base=0,
                                        channel_multiplier=-1),
         reads=["ident_bf"], writes=["ident_bf"])

    xln = k.sb("f2_xln", [128, 8 * T], BF16)
    wsl = k.sb("f2_ws", [128, 2 * 1024], BF16)
    hbuf = k.sb("f2_h", [64, T], BF16)
    abuf = k.sb("f2_a", [128, 8 * 512], BF16)
    dsel = k.sb("f2_dsel", [128, 2 * 512], BF16)
    usel = k.sb("f2_usel", [64, 2 * 1024], BF16)
    diag = k.sb("f2_diag", [128, 2 * 128], BF16)
    mcol = k.sb("f2_mcol", [128, 16], F32)
    dvals = k.sb("f2_dv", [128, 256], F32)
    ohcol = k.sb("f2_ohc", [128, 16], F32)
    ohc8 = k.sb("f2_oh8", [8, 2], F32)
    ohc8b = k.sb("f2_oh8b", [8, 2], BF16)
    lrow = k.sb("f2_lrow", [1, 64], F32)
    brows = k.sb("f2_brows", [1, 2 * 2112], BF16)

    k.dma("y", stg[:, 0:64].rearrange("p (a c) -> p a c", a=8),
          W["l2f_sigT"].rearrange("(a p) c -> p a c", p=128), writes=["io_stg"])
    k.op("v", lambda e: e.tensor_mul(dvals[:, 192:256], stg[:, 0:64], stg[:, 0:64]),
         reads=["io_stg"], writes=["f2_dv"])
    k.op("v", lambda e: e.reciprocal(dvals[:, 0:64], dvals[:, 192:256]),
         reads=["f2_dv"], writes=["f2_dv"])
    k.dma("y", stg[:, 0:64].rearrange("p (a c) -> p a c", a=8),
          W["l2f_muT"].rearrange("(a p) c -> p a c", p=128), writes=["io_stg"])
    k.op("v", lambda e: e.tensor_mul(dvals[:, 64:128], stg[:, 0:64], dvals[:, 0:64]),
         reads=["io_stg", "f2_dv"], writes=["f2_dv"])
    k.op("v", lambda e: e.tensor_mul(dvals[:, 128:192], stg[:, 0:64],
                                     dvals[:, 64:128]),
         reads=["io_stg", "f2_dv"], writes=["f2_dv"])
    for db in range(8):
        k.mm(psR1[:, 0:8], ocf[:], dvals[:, 128 + db * 8:128 + (db + 1) * 8],
             start=db == 0, stop=db == 7, reads=["ones_col_f", "f2_dv"],
             writes=["psR1"])
    k.op("s", lambda e: e.copy(lrow[0:1, 0:8], psR1[:, 0:8]),
         reads=["psR1"], writes=["f2_lrow"])

    for b in range(BPC):
        bt = b * T
        k.op("v", lambda e: e.memset(mcol[:], 0.0), writes=["f2_mcol"])
        for db in range(DBLK):
            for fc in range(4):
                f0 = bt + fc * 512
                k.op("v", lambda e, f0=f0: e.tensor_mul(
                    t3[:], lnbf[0:1, f0:f0 + 512], r_rep[0:1, f0:f0 + 512]),
                    reads=["lnbf", "r_rep"], writes=["row_t3"])
                k.mm(psC[:], orf[:], t3[:], True, True,
                     reads=["orf", "row_t3"], writes=["psC"])
                tmp = sq[:, 0:512]
                k.op("v", lambda e, f0=f0, tmp=tmp, db=db: e.tensor_mul(
                    tmp, x[:, db * TB + f0:db * TB + f0 + 512],
                    r_rep[:, f0:f0 + 512]),
                    reads=[f"x{db}", "r_rep"], writes=["sq0"])
                k.op("v", lambda e, tmp=tmp: e.tensor_sub(tmp, tmp, psC[:]),
                     reads=["sq0", "psC"], writes=["sq0"])
                k.op("v", lambda e, tmp=tmp: e.tensor_reduce(
                    dvals[:, 248:249], tmp, AX, ALU.add),
                    reads=["sq0"], writes=["f2_dv249"])
                k.op("v", lambda e, db=db: e.tensor_add(
                    mcol[:, 2 * db + 1:2 * db + 2], mcol[:, 2 * db + 1:2 * db + 2],
                    dvals[:, 248:249]),
                    reads=["f2_dv249"], writes=["f2_mcol"])
                sq2 = sq[:, 512:1024]
                k.op("s", lambda e, tmp=tmp, sq2=sq2: e.activation(
                    sq2, tmp, AF.Square), reads=["sq0"], writes=["sq1"])
                k.op("v", lambda e, sq2=sq2: e.tensor_reduce(
                    dvals[:, 249:250], sq2, AX, ALU.add),
                    reads=["sq1"], writes=["f2_dv249"])
                k.op("v", lambda e, db=db: e.tensor_add(
                    mcol[:, 2 * db:2 * db + 1], mcol[:, 2 * db:2 * db + 1],
                    dvals[:, 249:250]),
                    reads=["f2_dv249"], writes=["f2_mcol"])
        k.op("v", lambda e: e.tensor_scalar_mul(mcol[:], mcol[:], 1.0 / T),
             reads=["f2_mcol"], writes=["f2_mcol"])
        for db in range(8):
            k.mm(psR1[:, 0:8], mcol[:, 2 * db:2 * db + 1],
                 dvals[:, db * 8:(db + 1) * 8],
                 start=db == 0, stop=db == 7, reads=["f2_mcol", "f2_dv"],
                 writes=["psR1"])
            k.mm(psR2[:, 0:8], mcol[:, 2 * db + 1:2 * db + 2],
                 dvals[:, 64 + db * 8:64 + (db + 1) * 8],
                 start=db == 0, stop=db == 7, reads=["f2_mcol", "f2_dv"],
                 writes=["psR2"])
        k.op("v", lambda e: e.scalar_tensor_tensor(
            lrow[0:1, 8:16], psR2[:, 0:8], -2.0, lrow[0:1, 0:8],
            ALU.mult, ALU.add),
            reads=["psR2", "f2_lrow"], writes=["f2_lrow"])
        k.op("v", lambda e: e.tensor_add(lrow[0:1, 16:24], lrow[0:1, 8:16],
                                         psR1[:, 0:8]),
             reads=["psR1", "f2_lrow"], writes=["f2_lrow"])
        k.op("v", lambda e: e.tensor_scalar_mul(lrow[0:1, 16:24],
                                                lrow[0:1, 16:24], 1.0 / D),
             reads=["f2_lrow"], writes=["f2_lrow"])
        k.op("v", lambda e: e.tensor_reduce(lrow[0:1, 32:33], lrow[0:1, 16:24],
                                            AX, ALU.min),
             reads=["f2_lrow"], writes=["f2_lrow"])
        k.op("v", lambda e: e.tensor_scalar(lrow[0:1, 40:48], lrow[0:1, 16:24],
                                            lrow[0:1, 32:33], None, ALU.is_le),
             reads=["f2_lrow"], writes=["f2_lrow"])
        for e_ in range(8):
            k.mm(psC[:, 0:1], orf[:], lrow[0:1, 40 + e_:41 + e_], True, True,
                 reads=["orf", "f2_lrow"], writes=["psC"])
            k.op("s", lambda e, e_=e_, b=b: e.copy(
                ohcol[:, b * 8 + e_:b * 8 + e_ + 1], psC[:, 0:1]),
                reads=["psC"], writes=["f2_ohc"])
        k.mm(psC[0:8, 0:1], lrow[0:1, 40:48], orf[0:1, 0:1], True, True,
             reads=["f2_lrow", "orf"], writes=["psC"])
        k.op("s", lambda e, b=b: e.copy(ohc8[:, b:b + 1], psC[0:8, 0:1]),
             reads=["psC"], writes=["f2_oh8"])
        k.op("s", lambda e, b=b: e.copy(ohc8b[:, b:b + 1], psC[0:8, 0:1]),
             reads=["psC"], writes=["f2_oh8b"])
        for e_ in range(8):
            k.op("v", lambda e, e_=e_, b=b: e.tensor_scalar_mul(
                diag[:, 0:128], ident[:], ohcol[:, b * 8 + e_:b * 8 + e_ + 1]),
                reads=["ident_bf", "f2_ohc"], writes=["f2_diag"])
            k.dma("y", stg[:, 0:512].rearrange("p (a c) -> p a c", a=8),
                  W["l2f_downWT"].rearrange("(a p) c -> p a c", p=128)
                  [:, :, e_ * 64:(e_ + 1) * 64], writes=["io_stg"])
            k.op("g", lambda e: e.tensor_copy(wsl[:, 0:512], stg[:, 0:512]),
                 reads=["io_stg"], writes=["f2_ws0"])
            for db in range(8):
                k.mm(psA[:, db * 64:(db + 1) * 64], diag[:, 0:128],
                     wsl[:, db * 64:(db + 1) * 64],
                     start=e_ == 0, stop=e_ == 7,
                     reads=["f2_diag", "f2_ws0"], writes=["psA"])
            k.dma("y", stg[0:64, 0:1024],
                  W["l2f_upWs"][e_ * 64:(e_ + 1) * 64, :], writes=["io_stg"])
            k.op("g", lambda e: e.tensor_copy(wsl[0:64, 1024:2048],
                                              stg[0:64, 0:1024]),
                 reads=["io_stg"], writes=["f2_ws1"])
            for ch in range(2):
                ps_u = psB if ch == 0 else psD
                pn_u = "psB" if ch == 0 else "psD"
                k.mm(ps_u[0:64, :], diag[0:64, 0:64],
                     wsl[0:64, 1024 + ch * 512:1024 + (ch + 1) * 512],
                     start=e_ == 0, stop=e_ == 7,
                     reads=["f2_diag", "f2_ws1"], writes=[pn_u])
        k.op("s", lambda e, b=b: e.copy(dsel[:, b * 512:(b + 1) * 512], psA[:]),
             reads=["psA"], writes=["f2_dsel"])
        k.op("s", lambda e, b=b: e.copy(usel[:, b * 1024:b * 1024 + 512],
                                        psB[0:64, :]),
             reads=["psB"], writes=["f2_usel"])
        k.op("s", lambda e, b=b: e.copy(usel[:, b * 1024 + 512:b * 1024 + 1024],
                                        psD[0:64, :]),
             reads=["psD"], writes=["f2_usel"])
        BO = b * 2112
        k.dma("y", stg[0:8, 0:64], W["l2f_down_b"][:, :], writes=["io_stg"])
        k.op("s", lambda e: e.copy(wsl[0:8, 0:64], stg[0:8, 0:64]),
             reads=["io_stg"], writes=["f2_wsb"])
        k.mm(psC[0:1, 0:64], ohc8b[:, b:b + 1], wsl[0:8, 0:64], True, True,
             reads=["f2_oh8b", "f2_wsb"], writes=["psC"])
        k.op("s", lambda e, BO=BO: e.copy(brows[0:1, BO:BO + 64], psC[0:1, 0:64]),
             reads=["psC"], writes=["f2_brows"])
        k.dma("y", stg[0:8, 0:1024], W["l2f_up_b"][:, :], writes=["io_stg"])
        k.op("s", lambda e: e.copy(wsl[0:8, 0:1024], stg[0:8, 0:1024]),
             reads=["io_stg"], writes=["f2_wsb"])
        for ch in range(2):
            ps_u = psC if ch == 0 else psD
            pn_u = "psC" if ch == 0 else "psD"
            k.mm(ps_u[0:1, :], ohc8b[:, b:b + 1], wsl[0:8, ch * 512:(ch + 1) * 512],
                 True, True, reads=["f2_oh8b", "f2_wsb"], writes=[pn_u])
            k.op("s", lambda e, BO=BO, ch=ch, ps_u=ps_u: e.copy(
                brows[0:1, BO + 64 + ch * 512:BO + 64 + (ch + 1) * 512],
                ps_u[0:1, :]),
                reads=[pn_u], writes=["f2_brows"])
        k.dma("y", stg[0:8, 0:1024], W["l2f_conv_b"][:, :], writes=["io_stg"])
        k.dma("y", stg[32:33, 0:1024], W["l2f_base_b"][0:1, :], writes=["io_stgb"])
        k.op("s", lambda e: e.copy(wsl[0:8, 0:1024], stg[0:8, 0:1024]),
             reads=["io_stg"], writes=["f2_wsb"])
        for ch in range(2):
            ps_u = psC if ch == 0 else psD
            pn_u = "psC" if ch == 0 else "psD"
            k.mm(ps_u[0:1, :], ohc8b[:, b:b + 1], wsl[0:8, ch * 512:(ch + 1) * 512],
                 True, True, reads=["f2_oh8b", "f2_wsb"], writes=[pn_u])
            k.op("v", lambda e, ch=ch, ps_u=ps_u: e.tensor_add(
                stg[64:65, ch * 512:(ch + 1) * 512],
                stg[32:33, ch * 512:(ch + 1) * 512], ps_u[0:1, :]),
                reads=[pn_u, "io_stgb"], writes=["io_stgc"])
            k.op("s", lambda e, BO=BO, ch=ch: e.copy(
                brows[0:1, BO + 1088 + ch * 512:BO + 1088 + (ch + 1) * 512],
                stg[64:65, ch * 512:(ch + 1) * 512]),
                reads=["io_stgc"], writes=["f2_brows"])

    k.sched.barrier()

    for bb_ in range(2):
        for e_ in range(8):
            k.op("v", lambda e, e_=e_, bb_=bb_: e.tensor_scalar_mul(
                diag[:, bb_ * 1024 + e_ * 128:bb_ * 1024 + (e_ + 1) * 128]
                if False else diag[:, (bb_ * 8 + e_) % 2 * 128:
                                   ((bb_ * 8 + e_) % 2 + 1) * 128],
                ident[:], ohcol[:, bb_ * 8 + e_:bb_ * 8 + e_ + 1]),
                reads=["ident_bf", "f2_ohc"], writes=[f"f2_diagx{bb_}_{e_}"])                 if False else None
    diag16 = k.sb("f2_diag16", [128, 16 * 128], BF16)
    for j in range(16):
        k.op("v", lambda e, j=j: e.tensor_scalar_mul(
            diag16[:, j * 128:(j + 1) * 128], ident[:], ohcol[:, j:j + 1]),
            reads=["ident_bf", "f2_ohc"], writes=["f2_diag16"])
    for dt in range(8):
        for oc_ in range(2):
            for e_ in range(8):
                sl = e_ % 2
                k.dma("y", stg[:, sl * 512:(sl + 1) * 512],
                      W["l2f_convWT"][e_ * 1024 + dt * 128:
                                      e_ * 1024 + (dt + 1) * 128,
                                      oc_ * 512:(oc_ + 1) * 512],
                      writes=[f"io_stg{sl}"])
                k.op("g", lambda e, sl=sl: e.tensor_copy(
                    wsl[:, sl * 512:(sl + 1) * 512], stg[:, sl * 512:(sl + 1) * 512]),
                    reads=[f"io_stg{sl}"], writes=[f"f2_ws0{sl}"])
                for bb_ in range(2):
                    ps_w = psA if bb_ == 0 else psB
                    pn_w = "psA" if bb_ == 0 else "psB"
                    k.mm(ps_w[:], diag16[:, (bb_ * 8 + e_) * 128:
                                         (bb_ * 8 + e_ + 1) * 128],
                         wsl[:, sl * 512:(sl + 1) * 512], start=e_ == 0,
                         stop=e_ == 7,
                         reads=["f2_diag16", f"f2_ws0{sl}"], writes=[pn_w])
            for bb_ in range(2):
                ps_w = psA if bb_ == 0 else psB
                pn_w = "psA" if bb_ == 0 else "psB"
                k.op("s", lambda e, ps_w=ps_w, bb_=bb_: e.copy(
                    wsl[:, 1024 + bb_ * 512:1024 + bb_ * 512 + 512], ps_w[:]),
                    reads=[pn_w], writes=["f2_ws1"])
                k.dma("g", W["l2f_wsel_dram"]
                      [bb_ * 1024 + dt * 128:bb_ * 1024 + (dt + 1) * 128,
                       oc_ * 512:(oc_ + 1) * 512],
                      wsl[:, 1024 + bb_ * 512:1024 + bb_ * 512 + 512],
                      reads=["f2_ws1"], writes=[])
    for dt in range(8):
        k.dma("y", stg[:, 0:1024], W["l2f_baseWT"][dt * 128:(dt + 1) * 128, :],
              writes=["io_stg"])
        k.op("g", lambda e: e.tensor_copy(wsl[:, 0:1024], stg[:, 0:1024]),
             reads=["io_stg"], writes=["f2_ws0"])
        k.dma("g", W["l2f_base_dram"][dt * 128:(dt + 1) * 128, :], wsl[:, 0:1024],
              reads=["f2_ws0"], writes=[])
    k.sched.barrier()

    for b in range(BPC):
        bt = b * T
        BO = b * 2112
        for db in range(DBLK):
            for fc in range(4):
                f0 = bt + fc * 512
                k.op("v", lambda e, f0=f0: e.tensor_mul(
                    t3[:], lnbf[0:1, f0:f0 + 512], r_rep[0:1, f0:f0 + 512]),
                    reads=["lnbf", "r_rep"], writes=["row_t3"])
                k.mm(psC[:], orf[:], t3[:], True, True,
                     reads=["orf", "row_t3"], writes=["psC"])
                dstx = xln[:, db * T + fc * 512:db * T + fc * 512 + 512]
                k.op("v", lambda e, dstx=dstx, f0=f0, db=db: e.tensor_mul(
                    dstx, x[:, db * TB + f0:db * TB + f0 + 512],
                    r_rep[:, f0:f0 + 512]),
                    reads=[f"x{db}", "r_rep"], writes=["f2_xln"])
                k.op("v", lambda e, dstx=dstx: e.tensor_sub(dstx, dstx, psC[:]),
                     reads=["f2_xln", "psC"], writes=["f2_xln"])
        for fc in range(4):
            f0 = fc * 512
            ps = psC if fc % 2 == 0 else psD
            pn = "psC" if fc % 2 == 0 else "psD"
            for db in range(8):
                k.mm(ps[0:64, :], dsel[:, b * 512 + db * 64:b * 512 + (db + 1) * 64],
                     xln[:, db * T + f0:db * T + f0 + 512],
                     start=db == 0, stop=False, reads=["f2_dsel", "f2_xln"],
                     writes=[pn])
            k.mm(ps[0:64, :], brows[0:1, BO:BO + 64], orow[:],
                 start=False, stop=True, reads=["f2_brows", "ones_row"],
                 writes=[pn])
            k.op("s", lambda e, f0=f0, ps=ps: e.activation(
                hbuf[:, f0:f0 + 512], ps[0:64, :], AF.Relu),
                reads=[pn], writes=["f2_h"])
        for fc in range(4):
            f0 = fc * 512
            for dt in range(8):
                ps = psC if dt % 2 == 0 else psD
                pn = "psC" if dt % 2 == 0 else "psD"
                k.mm(ps[:], usel[:, b * 1024 + dt * 128:b * 1024 + (dt + 1) * 128],
                     hbuf[:, f0:f0 + 512], start=True, stop=False,
                     reads=["f2_usel", "f2_h"], writes=[pn])
                k.mm(ps[:], brows[0:1, BO + 64 + dt * 128:BO + 64 + (dt + 1) * 128],
                     orow[:], start=False, stop=True,
                     reads=["f2_brows", "ones_row"], writes=[pn])
                k.op("v", lambda e, dt=dt, ps=ps: e.tensor_copy(
                    abuf[:, dt * 512:(dt + 1) * 512], ps[:]),
                    reads=[pn], writes=["f2_a"])
            for ot in range(8):
                ps = psA if ot % 2 == 0 else psB
                pn = "psA" if ot % 2 == 0 else "psB"
                k.dma("y", wsl[:, 0:1024].rearrange("p (a c) -> p a c", a=8),
                      W["l2f_base_dram"].rearrange("(a p) c -> p a c", p=128)
                      [:, :, ot * 128:(ot + 1) * 128], writes=["f2_ws0"])
                k.dma("s", wsl[:, 1024:2048].rearrange("p (a c) -> p a c", a=8),
                      W["l2f_wsel_dram"][b * 1024:(b + 1) * 1024, :]
                      .rearrange("(a p) c -> p a c", p=128)
                      [:, :, ot * 128:(ot + 1) * 128], writes=["f2_ws1"])
                for cb in range(8):
                    k.mm(ps[:], wsl[:, cb * 128:(cb + 1) * 128],
                         xln[:, cb * T + f0:cb * T + f0 + 512],
                         start=cb == 0, stop=False, reads=["f2_ws0", "f2_xln"],
                         writes=[pn])
                for cb in range(8):
                    k.mm(ps[:], wsl[:, 1024 + cb * 128:1024 + (cb + 1) * 128],
                         abuf[:, cb * 512:(cb + 1) * 512],
                         start=False, stop=False, reads=["f2_ws1", "f2_a"],
                         writes=[pn])
                k.mm(ps[:], brows[0:1, BO + 1088 + ot * 128:
                                  BO + 1088 + (ot + 1) * 128], orow[:],
                     start=False, stop=True, reads=["f2_brows", "ones_row"],
                     writes=[pn])
                xs = x[:, ot * TB + bt + f0:ot * TB + bt + f0 + 512]
                k.op("v", lambda e, xs=xs, ps=ps: e.tensor_add(xs, ps[:], xs),
                     reads=[pn], writes=[f"x{ot}"])

    k.sched.barrier()



L3F_PHASE = 9


def build_l3_ffn(k: K, W):
    x = k.tensors["x_sb"]
    r_rep = k.tensors["r_rep"]
    oc, orow, orf = k.tensors["ones_col"], k.tensors["ones_row"], k.tensors["orf"]
    ocf = k.tensors["ones_col_f"]
    sq = k.tensors["sq_t"]
    psA, psB, psC, psD = (k.tensors[n] for n in ("psA", "psB", "psC", "psD"))
    psR1, psR2 = k.tensors["psR1"], k.tensors["psR2"]
    t1, t3 = k.tensors["row_t1"], k.tensors["row_t3"]
    t2 = k.tensors["row_t2"][0:1, :]

    build_ln_rows(k)
    lnbf = k.tensors["lnbf"]
    stg = k.tensors["io_stg"]

    xn = k.sb("f3_xn", [128, 8 * T], BF16)
    kT = k.sb("f3_kT", [128, 8 * T], BF16)
    vb = k.sb("f3_v", [128, 16 * 1024], BF16)
    qch = k.sb("f3_q", [128, 8 * 512], BF16)
    ao = qch
    ws = k.sb("f3_ws", [128, 1024], BF16)
    bb = k.sb("f3_bb", [128, 2048], BF16)
    orow4 = k.sb("orow4", [128, 512], BF16)
    k.op("v", lambda e: e.memset(orow4[:], 1.0), writes=["orow4"])
    wbcol = k.sb("f3_wb", [128, 16], F32)
    p_sb = sq
    srep = k.sb("f3_srep", [128, 512], F32)
    SC = 1.0 / 32.0

    def wtile(src_ap, nrow, slot, tag):
        a = nrow // 128
        st = stg[:, 0:8 * 128]
        k.dma("y", st[:, 0:a * 128].rearrange("p (a c) -> p a c", a=a),
              src_ap.rearrange("(a p) c -> p a c", p=128), writes=["io_stg"])
        k.op("g", lambda e, slot=slot, a=a: e.tensor_copy(
            slot[:, 0:a * 128], st[:, 0:a * 128]),
            reads=["io_stg"], writes=[tag])

    for i, nm in enumerate(("l3f_qb", "l3f_kb", "l3f_vb", "l3f_ob")):
        pb, c0 = [(0, 0), (32, 0), (64, 0), (0, 1024)][i]
        k.dma("y", stg[0:1, 0:1024], W[nm][0:1, :], writes=["io_stg"])
        k.op("s", lambda e, pb=pb, c0=c0: e.copy(bb[pb:pb + 1, c0:c0 + 1024],
                                                 stg[0:1, 0:1024]),
             reads=["io_stg"], writes=["f3_bb"])
    for ri, nm in enumerate(("l3f_nw", "l3f_nb")):
        for ch in range(2):
            k.dma("y", t3[0:1, :], W[nm][0:1, ch * 512:(ch + 1) * 512],
                  reads=["row_t3"], writes=["row_t3"])
            for j in range(4):
                k.mm(psD[:, j:j + 1], t3[0:1, j * 128:(j + 1) * 128], orf[0:1, 0:1],
                     True, True, reads=["row_t3", "orf"], writes=["psD"])
            k.op("s", lambda e, ri=ri, ch=ch: e.copy(
                wbcol[:, ri * 8 + ch * 4:ri * 8 + ch * 4 + 4], psD[:, 0:4]),
                reads=["psD"], writes=["f3_wb"])
    k.sched.barrier()

    for b in range(BPC):
        bt = b * T
        for fc in range(4):
            f0 = bt + fc * 512
            fl = fc * 512
            k.op("v", lambda e, f0=f0: e.tensor_mul(
                t3[:], lnbf[0:1, f0:f0 + 512], r_rep[0:1, f0:f0 + 512]),
                reads=["lnbf", "r_rep"], writes=["row_t3"])
            k.mm(psC[:], orf[:], t3[:], True, True,
                 reads=["orf", "row_t3"], writes=["psC"])
            for db in range(DBLK):
                dst = xn[:, db * T + fl:db * T + fl + 512]
                k.op("v", lambda e, dst=dst, f0=f0, db=db: e.tensor_mul(
                    dst, x[:, db * TB + f0:db * TB + f0 + 512],
                    r_rep[:, f0:f0 + 512]),
                    reads=[f"x{db}", "r_rep"], writes=[f"f3_xn{db}"])
                k.op("v", lambda e, dst=dst: e.tensor_sub(dst, dst, psC[:]),
                     reads=[f"f3_xn{db}", "psC"], writes=[f"f3_xn{db}"])
                sqs = sq[:, (db % 2) * 512:(db % 2) * 512 + 512]
                k.op("s", lambda e, dst=dst, sqs=sqs: e.activation(
                    sqs, dst, AF.Square), reads=[f"f3_xn{db}"],
                    writes=[f"sq{db % 2}"])
                k.mm(psR1[:], oc[:], dst, start=db == 0, stop=db == DBLK - 1,
                     reads=["ones_col", f"f3_xn{db}"], writes=["psR1"])
                k.mm(psR2[:], oc[:], sqs, start=db == 0, stop=db == DBLK - 1,
                     reads=["ones_col", f"sq{db % 2}"], writes=["psR2"])
            k.op("v", lambda e: e.tensor_scalar_mul(t1[:], psR1[:], 1.0 / D),
                 reads=["psR1"], writes=["row_t1"])
            k.op("v", lambda e: e.tensor_scalar_mul(t3[:], psR2[:], 1.0 / D),
                 reads=["psR2"], writes=["row_t3"])
            k.op("s", lambda e: e.activation(t2, t1[:], AF.Square),
                 reads=["row_t1"], writes=["row_t2"])
            k.op("v", lambda e: e.tensor_sub(t3[:], t3[:], t2),
                 reads=["row_t3", "row_t2"], writes=["row_t3"])
            k.op("s", lambda e: e.activation(t3[:], t3[:], AF.Ln, bias=1e-5),
                 reads=["row_t3"], writes=["row_t3"])
            k.op("s", lambda e: e.activation(t3[:], t3[:], AF.Exp, scale=-0.5),
                 reads=["row_t3"], writes=["row_t3"])
            k.mm(psC[:], orf[:], t1[:], True, True,
                 reads=["orf", "row_t1"], writes=["psC"])
            k.mm(psD[:], orf[:], t3[:], True, True,
                 reads=["orf", "row_t3"], writes=["psD"])
            for db in range(DBLK):
                dst = xn[:, db * T + fl:db * T + fl + 512]
                tmp = stg[:, 0:512]
                k.op("v", lambda e, dst=dst, tmp=tmp: e.tensor_sub(
                    tmp, dst, psC[:]),
                    reads=[f"f3_xn{db}", "psC"], writes=["io_stg"])
                k.op("v", lambda e, tmp=tmp: e.tensor_mul(
                    tmp, tmp, psD[:]),
                    reads=["io_stg", "psD"], writes=["io_stg"])
                k.op("g", lambda e, dst=dst, tmp=tmp, db=db: e.tensor_scalar(
                    dst, tmp, wbcol[:, db:db + 1], wbcol[:, 8 + db:8 + db + 1],
                    ALU.mult, ALU.add),
                    reads=["io_stg", "f3_wb"], writes=[f"f3_xn{db}"])
        XNR = [f"f3_xn{db}" for db in range(DBLK)]
        if L3F_PHASE < 2:
            continue
        for seg in range(4):
            m = 1024 >> seg
            nc_ = m // 128
            sl0 = seg * 512
            for ot in range(8):
                slot = ws[:, 0:1024]
                stag = "f3_ws0"
                wtile(W["l3f_kwT"][0:m, ot * 128:(ot + 1) * 128], m, slot, stag)
                ps = psA if ot % 2 == 0 else psB
                pn = "psA" if ot % 2 == 0 else "psB"
                for cb in range(nc_):
                    k.mm(ps[:], slot[:, cb * 128:(cb + 1) * 128],
                         xn[:, cb * T + sl0:cb * T + sl0 + 512],
                         start=cb == 0, stop=False, reads=[stag] + XNR, writes=[pn])
                k.mm(ps[:], bb[32:33, ot * 128:(ot + 1) * 128],
                     orow4[32:33, :], start=False, stop=True,
                     reads=["f3_bb", "orow4"], writes=[pn])
                k.op("s", lambda e, ot=ot, sl0=sl0, ps=ps: e.copy(
                    kT[:, ot * T + sl0:ot * T + sl0 + 512], ps[:]),
                    reads=[pn], writes=["f3_kT"])
            for vt in range(8):
                slot = ws[:, 0:1024]
                stag = "f3_ws0"
                wtile(W["l3f_vwT"][0:m, vt * 128:(vt + 1) * 128], m, slot, stag)
                for tt in range(4):
                    t0 = sl0 + tt * 128
                    ps = psC if tt % 2 == 0 else psD
                    pn = "psC" if tt % 2 == 0 else "psD"
                    for cb in range(nc_):
                        k.mm(ps[:, 0:128], xn[:, cb * T + t0:cb * T + t0 + 128],
                             slot[:, cb * 128:(cb + 1) * 128],
                             start=cb == 0, stop=False, reads=[stag] + XNR,
                             writes=[pn])
                    k.mm(ps[:, 0:128], orow4[64:65, 0:128],
                         bb[64:65, vt * 128:(vt + 1) * 128],
                         start=False, stop=True, reads=["f3_bb", "orow4"],
                         writes=[pn])
                    k.op("s", lambda e, t0=t0, vt=vt, ps=ps: e.copy(
                        vb[:, (t0 // 128) * 1024 + vt * 128:
                           (t0 // 128) * 1024 + vt * 128 + 128],
                        ps[:, 0:128]),
                        reads=[pn], writes=["f3_vb"])
        if L3F_PHASE < 3:
            continue
        for seg in range(4):
            m = 1024 >> seg
            nc_ = m // 128
            sl0 = seg * 512
            f0 = bt + sl0
            for ot in range(8):
                slot = ws[:, 0:1024]
                stag = "f3_ws0"
                wtile(W["l3f_qwT"][0:m, ot * 128:(ot + 1) * 128], m, slot, stag)
                ps = psA if ot % 2 == 0 else psB
                pn = "psA" if ot % 2 == 0 else "psB"
                for cb in range(nc_):
                    k.mm(ps[:], slot[:, cb * 128:(cb + 1) * 128],
                         xn[:, cb * T + sl0:cb * T + sl0 + 512],
                         start=cb == 0, stop=False, reads=[stag] + XNR, writes=[pn])
                k.mm(ps[:], bb[0:1, ot * 128:(ot + 1) * 128], orow4[0:1, :],
                     start=False, stop=True, reads=["f3_bb", "orow4"],
                     writes=[pn])
                k.op("s", lambda e, ot=ot, ps=ps: e.copy(
                    qch[:, ot * 512:(ot + 1) * 512], ps[:]),
                    reads=[pn], writes=[f"f3_q{ot}"])
            for h in range(8 if L3F_PHASE >= 4 else 0):
                for kt in range(16):
                    ps = psC if kt % 2 == 0 else psD
                    pn = "psC" if kt % 2 == 0 else "psD"
                    k.mm(ps[:], kT[:, h * T + kt * 128:h * T + (kt + 1) * 128],
                         qch[:, h * 512:(h + 1) * 512],
                         True, True, reads=["f3_kT", f"f3_q{h}"], writes=[pn])
                    pt = p_sb[:, (kt % 2) * 512:(kt % 2) * 512 + 512]
                    ptn = f"sq{kt % 2}"
                    k.op("s", lambda e, pt=pt, ps=ps: e.activation(
                        pt, ps[:], AF.Exp, scale=SC), reads=[pn], writes=[ptn])
                    k.mm(psA[:], vb[:, kt * 1024 + h * 128:kt * 1024 + h * 128 + 128],
                         pt, start=kt == 0, stop=kt == 15,
                         reads=["f3_vb", ptn], writes=["psA"])
                    k.mm(psR1[:], oc[:, 0:1], pt, start=kt == 0, stop=kt == 15,
                         reads=["ones_col", ptn], writes=["psR1"])
                k.op("v", lambda e: e.reciprocal(t1[:], psR1[:]),
                     reads=["psR1"], writes=["row_t1"])
                k.mm(psB[:], orf[:], t1[:], True, True,
                     reads=["orf", "row_t1"], writes=["psB"])
                k.op("s", lambda e: e.copy(srep[:], psB[:]),
                     reads=["psB"], writes=["f3_srep"])
                k.op("v", lambda e, h=h: e.tensor_mul(
                    ao[:, h * 512:(h + 1) * 512], psA[:], srep[:]),
                    reads=["psA", "f3_srep", f"f3_q{h}"], writes=[f"f3_q{h}"])
            k.op("v", lambda e, f0=f0: e.tensor_mul(
                t3[:], lnbf[0:1, f0:f0 + 512], r_rep[0:1, f0:f0 + 512]),
                reads=["lnbf", "r_rep"], writes=["row_t3"])
            k.mm(psB[:], orf[:], t3[:], True, True,
                 reads=["orf", "row_t3"], writes=["psB"])
            for ot in range(8 if L3F_PHASE >= 5 else 0):
                ps = psC if ot % 2 == 0 else psD
                pn = "psC" if ot % 2 == 0 else "psD"
                if ot < nc_:
                    slot = ws[:, 0:1024]
                    stag = "f3_ws0"
                    wtile(W["l3f_owT"][0:m, ot * 128:(ot + 1) * 128], m, slot, stag)
                    for cb in range(nc_):
                        k.mm(ps[:], slot[:, cb * 128:(cb + 1) * 128],
                             ao[:, cb * 512:(cb + 1) * 512],
                             start=cb == 0, stop=False,
                             reads=[stag] + [f"f3_q{cb}"], writes=[pn])
                    k.mm(ps[:], bb[0:1, 1024 + ot * 128:1024 + (ot + 1) * 128],
                         orow4[0:1, :], start=False, stop=True,
                         reads=["f3_bb", "orow4"], writes=[pn])
                tmpa = sq[:, 0:512]
                k.op("g", lambda e, tmpa=tmpa, f0=f0: e.tensor_scalar(
                    tmpa, r_rep[:, f0:f0 + 512], 1.0, None, ALU.add),
                    reads=["r_rep"], writes=["sq0"])
                xs = x[:, ot * TB + f0:ot * TB + f0 + 512]
                tmpb = stg[:, 0:512]
                k.op("v", lambda e, tmpb=tmpb, xs=xs, tmpa=tmpa: e.tensor_mul(
                    tmpb, xs, tmpa), reads=[f"x{ot}", "sq0"], writes=["io_stg"])
                k.op("v", lambda e, tmpb=tmpb: e.tensor_sub(tmpb, tmpb, psB[:]),
                     reads=["io_stg", "psB"], writes=["io_stg"])
                if ot < nc_:
                    k.op("v", lambda e, xs=xs, ps=ps, tmpb=tmpb: e.tensor_add(
                        xs, ps[:], tmpb), reads=[pn, "io_stg"], writes=[f"x{ot}"])
                else:
                    k.op("v", lambda e, xs=xs, tmpb=tmpb: e.tensor_copy(
                        xs, tmpb), reads=["io_stg"], writes=[f"x{ot}"])

    k.sched.barrier()



L3A_PHASE = 9


def build_l3_attn(k: K, W):
    x = k.tensors["x_sb"]
    r_rep = k.tensors["r_rep"]
    oc, orow, orf = k.tensors["ones_col"], k.tensors["ones_row"], k.tensors["orf"]
    ocf = k.tensors["ones_col_f"]
    sq = k.tensors["sq_t"]
    psA, psB, psC, psD = (k.tensors[n] for n in ("psA", "psB", "psC", "psD"))
    psR1, psR2 = k.tensors["psR1"], k.tensors["psR2"]
    t1, t3 = k.tensors["row_t1"], k.tensors["row_t3"]

    build_ln_rows(k)
    lnbf = k.tensors["lnbf"]
    stg = k.tensors["io_stg"]
    DI = 2048

    ws = k.sb("l3_ws", [128, 2 * 2048], BF16)
    lf = k.sb("l3_lf", [1, 128], BF16)
    acc = k.sb("l3_acc", [128, 2048], F32)
    sig = k.sb("l3_sig", [128, 2048], F32)
    xst = k.sb("l3_xst", [128, 2048], BF16)
    cw = k.sb("l3_cw", [128, 64], F32)
    cb = k.sb("l3_cb", [128, 16], F32)
    dtb = k.sb("l3_dtb", [128, 16], F32)
    dcol = k.sb("l3_dcol", [128, 16], F32)
    xpw = k.sb("l3_xpw", [128, 16 * 128], BF16)
    dthT = k.sb("l3_dth", [64, 2048], BF16)
    xdbl = k.sb("l3_xdbl", [96, 2048], BF16)
    xdbl32 = k.sb("l3_xdbl32", [96, 2048], F32)
    srp = k.sb("l3_srp", [128, 2048], F32)
    r3 = k.sb("l3_r3", [1, 2 * 2048 + 512], BF16)
    ut = k.sb("l3_u", [128, 2 * 2048], F32)
    xzp = ut[:, 0:2052]
    wt = k.sb("l3_w", [128, 2 * 2048], BF16)
    ya = k.sb("l3_ya", [128, 2 * 2048], F32)
    ab = sig[:, 0:1024]
    st_ = acc
    yfc = k.sb("l3_yfc", [128, 8 * 512], BF16)
    GD = 2

    k.dma("g", cw[:].rearrange("p (a c) -> p a c", a=16),
          W["l3a_convw"].rearrange("(a p) c -> p a c", p=128), writes=["l3_cw"])
    k.dma("g", cb[:].rearrange("p (a c) -> p a c", a=16),
          W["l3a_convb"].rearrange("(a p) c -> p a c", p=128), writes=["l3_cb"],
          allow_slow_non_contiguous=True)
    k.dma("g", dtb[:].rearrange("p (a c) -> p a c", a=16),
          W["l3a_dtb"].rearrange("(a p) c -> p a c", p=128), writes=["l3_dtb"],
          allow_slow_non_contiguous=True)
    k.dma("g", dcol[:].rearrange("p (a c) -> p a c", a=16),
          W["l3a_D"].rearrange("(a p) c -> p a c", p=128), writes=["l3_dcol"],
          allow_slow_non_contiguous=True)
    k.op("v", lambda e: e.memset(xpw[:], 0.0), writes=["l3_xpw"])
    for hh in range(2):
        k.dma("y", stg[:, 0:768].rearrange("p (a c) -> p a c", a=8),
              W["l3a_xpT"].rearrange("(a p) c -> p a c", p=128)[:, hh * 8:(hh + 1) * 8, :],
              writes=["io_stg"])
        k.op("g", lambda e, hh=hh: e.tensor_copy(
            xpw[:, hh * 8 * 128:hh * 8 * 128 + 768]
            .rearrange("p (a c) -> p a c", c=96),
            stg[:, 0:768].rearrange("p (a c) -> p a c", c=96)),
            reads=["io_stg"], writes=["l3_xpw"]) if False else None
        for j in range(8):
            k.op("g", lambda e, hh=hh, j=j: e.tensor_copy(
                xpw[:, (hh * 8 + j) * 128:(hh * 8 + j) * 128 + 96],
                stg[:, j * 96:(j + 1) * 96]),
                reads=["io_stg"], writes=["l3_xpw"])
    k.dma("y", stg[0:64, 0:1024], W["l3a_dthT"][:, 0:1024], writes=["io_stg"])
    k.op("g", lambda e: e.tensor_copy(dthT[:, 0:1024], stg[0:64, 0:1024]),
         reads=["io_stg"], writes=["l3_dth"])
    k.dma("y", stg[0:64, 0:1024], W["l3a_dthT"][:, 1024:2048], writes=["io_stg"])
    k.op("g", lambda e: e.tensor_copy(dthT[:, 1024:2048], stg[0:64, 0:1024]),
         reads=["io_stg"], writes=["l3_dth"])
    for rr_ in range(16):
        k.dma("y", stg[:, 0:1024], W["l3a_opT"][rr_ * 128:(rr_ + 1) * 128, :],
              writes=["io_stg"])
        k.op("g", lambda e: e.tensor_copy(ws[:, 0:1024], stg[:, 0:1024]),
             reads=["io_stg"], writes=["l3_ws0"])
        k.dma("g", W["l3a_opT_bf"][rr_ * 128:(rr_ + 1) * 128, :], ws[:, 0:1024],
              reads=["l3_ws0"], writes=[])
    k.sched.barrier()

    def wtile8(col0, slot, tag, wname="l3a_ipT"):
        k.dma("y", stg[:, 0:1024].rearrange("p (a c) -> p a c", a=8),
              W[wname].rearrange("(a p) c -> p a c", p=128)[:, :, col0:col0 + 128],
              writes=["io_stg"])
        k.op("g", lambda e, slot=slot: e.tensor_copy(slot[:, 0:1024],
                                                     stg[:, 0:1024]),
             reads=["io_stg"], writes=[tag])

    def inproj_tile(b, col0, dst_f32, dstn, slot, stag, need_rs):
        bt = b * T
        wtile8(col0, slot, stag)
        if need_rs:
            for cbk in range(8):
                k.mm(psR1[:, 0:128], oc[:], slot[:, cbk * 128:(cbk + 1) * 128],
                     start=cbk == 0, stop=cbk == 7, reads=["ones_col", stag],
                     writes=["psR1"])
            k.op("s", lambda e: e.activation(
                lf[0:1, 0:128], psR1[:, 0:128], AF.Copy, scale=-1.0),
                reads=["psR1"], writes=["l3_lf"])
        for ch in range(4):
            f0 = bt + ch * 512
            ps = psA if ch % 2 == 0 else psB
            pn = "psA" if ch % 2 == 0 else "psB"
            for cbk in range(DBLK):
                k.mm(ps[:], slot[:, cbk * 128:(cbk + 1) * 128],
                     x[:, cbk * TB + f0:cbk * TB + f0 + 512],
                     start=cbk == 0, stop=False, reads=[stag, f"x{cbk}"],
                     writes=[pn])
            k.mm(ps[:], lf[0:1, 0:128], lnbf[0:1, f0:f0 + 512],
                 start=False, stop=True, reads=["l3_lf", "lnbf"], writes=[pn])
            k.op("v", lambda e, ch=ch, ps=ps, f0=f0, dst_f32=dst_f32:
                 e.tensor_mul(dst_f32[:, ch * 512:(ch + 1) * 512], ps[:],
                              r_rep[:, f0:f0 + 512]),
                 reads=[pn, "r_rep"], writes=[dstn])

    for b in range(BPC):
        bt = b * T
        k.sched.barrier()
        for dt in range(16):
            slot = ws[:, (dt % 2) * 2048:(dt % 2) * 2048 + 1024]
            stag = f"l3_ws{dt % 2}"
            k.op("v", lambda e: e.memset(xzp[:, 0:3], 0.0), writes=["l3_xzp"])
            inproj_tile(b, dt * 128, xzp[:, 3:2051], "l3_xzp", slot, stag, True)
            k.op("v", lambda e, dt=dt: e.tensor_scalar_mul(
                acc[:], xzp[:, 0:2048], cw[:, dt * 4:dt * 4 + 1]),
                reads=["l3_xzp", "l3_cw"], writes=["l3_acc"])
            for j in range(1, 4):
                k.op("v", lambda e, dt=dt, j=j: e.scalar_tensor_tensor(
                    acc[:], xzp[:, j:j + 2048], cw[:, dt * 4 + j:dt * 4 + j + 1],
                    acc[:], ALU.mult, ALU.add),
                    reads=["l3_xzp", "l3_cw", "l3_acc"], writes=["l3_acc"])
            k.op("v", lambda e, dt=dt: e.tensor_scalar(
                acc[:], acc[:], cb[:, dt:dt + 1], None, ALU.add),
                reads=["l3_acc", "l3_cb"], writes=["l3_acc"])
            k.op("s", lambda e: e.activation(sig[:], acc[:], AF.Sigmoid),
                 reads=["l3_acc"], writes=["l3_sig"])
            k.op("v", lambda e: e.tensor_mul(xst[:], acc[:], sig[:]),
                 reads=["l3_acc", "l3_sig"], writes=["l3_xst"])
            k.dma("g", W["l3a_xssm_dram"][b * DI + dt * 128:b * DI + (dt + 1) * 128, :],
                  xst[:], reads=["l3_xst"], writes=["d_xssm"])
        k.sched.barrier()
        if L3A_PHASE < 2:
            continue
        for ch in range(4):
            ps = psC if ch % 2 == 0 else psD
            pn = "psC" if ch % 2 == 0 else "psD"
            for dt in range(16):
                k.dma("y", xst[:, 0:512],
                      W["l3a_xssm_dram"][b * DI + dt * 128:b * DI + (dt + 1) * 128,
                                         ch * 512:(ch + 1) * 512],
                      reads=["d_xssm"], writes=["l3_xst"])
                k.mm(ps[:], xpw[:, dt * 128:(dt + 1) * 128], xst[:, 0:512],
                     start=dt == 0, stop=dt == 15, reads=["l3_xpw", "l3_xst"],
                     writes=[pn])
            k.op("s", lambda e, ch=ch, ps=ps: e.copy(
                xdbl[:, ch * 512:(ch + 1) * 512], ps[0:96, :]),
                reads=[pn], writes=["l3_xdbl"])
            k.op("s", lambda e, ch=ch, ps=ps: e.copy(
                xdbl32[:, ch * 512:(ch + 1) * 512], ps[0:96, :]),
                reads=[pn], writes=["l3_xdbl32"])
        if L3A_PHASE < 3:
            continue
        k.op("s", lambda e: e.activation(acc[0:16, :], xdbl32[64:80, :], AF.Square),
             reads=["l3_xdbl32"], writes=["l3_acc"])
        for ch in range(4):
            k.mm(psR1[:], ocf[0:16, :], acc[0:16, ch * 512:(ch + 1) * 512],
                 True, True, reads=["ones_col_f", "l3_acc"], writes=["psR1"])
            k.mm(psC[:], orf[:], psR1[:].rearrange("a b -> a b") if False else t1[:],
                 True, True, reads=["orf", "row_t1"], writes=["psC"])                 if False else None
            k.op("v", lambda e: e.tensor_copy(t1[:], psR1[:]),
                 reads=["psR1"], writes=["row_t1"])
            k.mm(psC[:], orf[:], t1[:], True, True,
                 reads=["orf", "row_t1"], writes=["psC"])
            k.op("s", lambda e, ch=ch: e.copy(srp[:, ch * 512:(ch + 1) * 512],
                                              psC[:]),
                 reads=["psC"], writes=["l3_srp"])
        k.sched.barrier()
        if L3A_PHASE < 4:
            continue
        for dtg in range(16 // GD):
            for g_ in range(GD):
                dt = dtg * GD + g_
                for ch in range(4):
                    ps = psA if ch % 2 == 0 else psB
                    pn = "psA" if ch % 2 == 0 else "psB"
                    k.mm(ps[:], dthT[:, dt * 128:(dt + 1) * 128],
                         xdbl[0:64, ch * 512:(ch + 1) * 512],
                         True, True, reads=["l3_dth", "l3_xdbl"], writes=[pn])
                    k.op("s", lambda e, ps=ps, ch=ch, g_=g_, dt=dt: e.activation(
                        ut[:, g_ * 2048 + ch * 512:g_ * 2048 + (ch + 1) * 512],
                        ps[:], AF.Sigmoid, bias=dtb[:, dt:dt + 1]),
                        reads=[pn, "l3_dtb"], writes=[f"l3_u{g_}"])
                ug = ut[:, g_ * 2048:(g_ + 1) * 2048]
                k.op("v", lambda e, ug=ug: e.tensor_mul(acc[:], ug, srp[:]),
                     reads=[f"l3_u{g_}", "l3_srp"], writes=["l3_acc"])
                k.op("v", lambda e: e.tensor_scalar(acc[:], acc[:], 1.0, None,
                                                    ALU.add),
                     reads=["l3_acc"], writes=["l3_acc"])
                k.op("v", lambda e: e.reciprocal(acc[:], acc[:]),
                     reads=["l3_acc"], writes=["l3_acc"])
                k.op("v", lambda e, ug=ug: e.tensor_mul(ug, ug, acc[:]),
                     reads=[f"l3_u{g_}", "l3_acc"], writes=[f"l3_u{g_}"])
                k.dma("y", xst[:], W["l3a_xssm_dram"]
                      [b * DI + dt * 128:b * DI + (dt + 1) * 128, :],
                      reads=["d_xssm"], writes=["l3_xst"])
                k.op("v", lambda e, ug=ug, g_=g_: e.tensor_mul(
                    wt[:, g_ * 2048:(g_ + 1) * 2048], xst[:], ug),
                    reads=["l3_xst", f"l3_u{g_}"], writes=[f"l3_w{g_}"])
                k.op("v", lambda e, g_=g_: e.memset(
                    ya[:, g_ * 2048:(g_ + 1) * 2048], 0.0), writes=[f"l3_ya{g_}"])
            for n_ in range(16 if L3A_PHASE >= 5 else 0):
                k.dma("s", r3[0:1, 0:2048], xdbl[64 + n_:65 + n_, :],
                      reads=["l3_xdbl"], writes=["l3_r3"])
                k.dma("s", r3[0:1, 2048:4096], xdbl[80 + n_:81 + n_, :],
                      reads=["l3_xdbl"], writes=["l3_r3"])
                for ch in range(4):
                    c0 = ch * 512
                    k.op("s", lambda e, c0=c0: e.activation(
                        r3[0:1, 4096:4608], r3[0:1, c0:c0 + 512], AF.Square),
                        reads=["l3_r3"], writes=["l3_r3kk"])
                    k.mm(psB[:], orow[0:1, 0:128], r3[0:1, 4096:4608],
                         True, True, reads=["ones_row", "l3_r3kk"], writes=["psB"])
                    k.mm(psC[:], orow[0:1, 0:128], r3[0:1, c0:c0 + 512],
                         True, True, reads=["ones_row", "l3_r3"], writes=["psC"])
                    k.mm(psD[:], orow[0:1, 0:128], r3[0:1, 2048 + c0:2048 + c0 + 512],
                         True, True, reads=["ones_row", "l3_r3"], writes=["psD"])
                    for g_ in range(GD):
                        dt = dtg * GD + g_
                        ug = ut[:, g_ * 2048 + c0:g_ * 2048 + c0 + 512]
                        wg = wt[:, g_ * 2048 + c0:g_ * 2048 + c0 + 512]
                        av_ = ab[:, 0:512]
                        bv_ = ab[:, 512:1024]
                        k.op("v", lambda e, ug=ug, av_=av_: e.scalar_tensor_tensor(
                            av_, ug, -1.0, psB[:], ALU.mult, ALU.mult),
                            reads=[f"l3_u{g_}", "psB"], writes=["l3_ab0"])
                        k.op("s", lambda e, av_=av_: e.activation(
                            av_, av_, AF.Copy, bias=1.0),
                            reads=["l3_ab0"], writes=["l3_ab0"])
                        k.op("v", lambda e, wg=wg, bv_=bv_: e.tensor_tensor(
                            bv_, wg, psC[:], ALU.mult),
                            reads=[f"l3_w{g_}", "psC"], writes=["l3_ab1"])
                        ss = st_[:, (g_ * 2 + ch % 2) * 512:
                                 (g_ * 2 + ch % 2) * 512 + 512]
                        sprev = st_[:, (g_ * 2 + (ch + 1) % 2) * 512 + 511:
                                    (g_ * 2 + (ch + 1) % 2) * 512 + 512]
                        sn = f"l3_s{g_}_{ch % 2}"
                        spn = f"l3_s{g_}_{(ch + 1) % 2}"
                        if ch == 0:
                            k.op("v", lambda e, ss=ss, av_=av_, bv_=bv_:
                                 e.tensor_tensor_scan(ss, av_, bv_, 0.0,
                                                      ALU.mult, ALU.add),
                                 reads=["l3_ab0", "l3_ab1"], writes=[sn])
                        else:
                            k.op("v", lambda e, ss=ss, av_=av_, bv_=bv_,
                                 sprev=sprev: e.tensor_tensor_scan(
                                     ss, av_, bv_, sprev, ALU.mult, ALU.add),
                                 reads=["l3_ab0", "l3_ab1", spn], writes=[sn])
                        k.op("v", lambda e, ss=ss, av_=av_: e.tensor_tensor(
                            av_, ss, psD[:], ALU.mult),
                            reads=[sn, "psD"], writes=["l3_ab0"])
                        k.op("g", lambda e, g_=g_, c0=c0, av_=av_: e.tensor_tensor(
                            ya[:, g_ * 2048 + c0:g_ * 2048 + c0 + 512],
                            av_, ya[:, g_ * 2048 + c0:g_ * 2048 + c0 + 512],
                            ALU.add),
                            reads=["l3_ab0", f"l3_ya{g_}"], writes=[f"l3_ya{g_}"])
            k.sched.barrier()
            for g_ in range(GD if L3A_PHASE >= 6 else 0):
                dt = dtg * GD + g_
                slot = ws[:, (dt % 2) * 2048:(dt % 2) * 2048 + 1024]
                stag = f"l3_ws{dt % 2}"
                inproj_tile(b, 2048 + dt * 128, acc[:], "l3_acc", slot, stag,
                            True)
                k.op("s", lambda e: e.activation(sig[:], acc[:], AF.Sigmoid),
                     reads=["l3_acc"], writes=["l3_sig"])
                k.op("v", lambda e: e.tensor_mul(sig[:], acc[:], sig[:]),
                     reads=["l3_acc", "l3_sig"], writes=["l3_sig"])
                k.dma("y", xst[:], W["l3a_xssm_dram"]
                      [b * DI + dt * 128:b * DI + (dt + 1) * 128, :],
                      reads=["d_xssm"], writes=["l3_xst"])
                k.op("v", lambda e, g_=g_, dt=dt: e.scalar_tensor_tensor(
                    acc[:], xst[:], dcol[:, dt:dt + 1],
                    ya[:, g_ * 2048:(g_ + 1) * 2048], ALU.mult, ALU.add),
                    reads=["l3_xst", "l3_dcol", f"l3_ya{g_}"], writes=["l3_acc"])
                k.op("v", lambda e: e.tensor_mul(xst[:], acc[:], sig[:]),
                     reads=["l3_acc", "l3_sig"], writes=["l3_xst"])
                k.dma("g", W["l3a_yfin_dram"]
                      [b * DI + dt * 128:b * DI + (dt + 1) * 128, :],
                      xst[:], reads=["l3_xst"], writes=["d_yfin"])
        k.sched.barrier()
        if L3A_PHASE < 7:
            continue
        k.sched.barrier()
        for ch in range(4):
            f0 = bt + ch * 512
            for ot in range(8):
                slot = ws[:, (ot % 2) * 2048:(ot % 2) * 2048 + 2048]
                stag = f"l3_ws{ot % 2}"
                k.dma("y", slot[:].rearrange("p (a c) -> p a c", a=16),
                      W["l3a_opT_bf"].rearrange("(a p) c -> p a c", p=128)
                      [:, :, ot * 128:(ot + 1) * 128],
                      writes=[stag])
                ps = psA if ot % 2 == 0 else psB
                pn = "psA" if ot % 2 == 0 else "psB"
                for half in range(2):
                    for dt_ in range(8):
                        dt = half * 8 + dt_
                        k.dma("y", yfc[:, dt_ * 512:(dt_ + 1) * 512],
                              W["l3a_yfin_dram"]
                              [b * DI + dt * 128:b * DI + (dt + 1) * 128,
                               ch * 512:(ch + 1) * 512],
                              reads=["d_yfin"], writes=[f"l3_yfc{dt_}"])
                    for dt_ in range(8):
                        k.mm(ps[:], slot[:, (half * 8 + dt_) * 128:
                                         (half * 8 + dt_ + 1) * 128],
                             yfc[:, dt_ * 512:(dt_ + 1) * 512],
                             start=(half == 0 and dt_ == 0),
                             stop=(half == 1 and dt_ == 7),
                             reads=[stag, f"l3_yfc{dt_}"], writes=[pn])
                xs = x[:, ot * TB + f0:ot * TB + f0 + 512]
                k.op("v", lambda e, xs=xs, ps=ps: e.tensor_add(xs, ps[:], xs),
                     reads=[pn], writes=[f"x{ot}"])

    k.sched.barrier()


BUILDERS = {"l0_attn": build_l0_attn, "l0_ffn": build_l0_ffn,
            "l1_attn": build_l1_attn, "l1_ffn": build_l1_ffn,
            "l2_attn": build_l2_attn, "l2_ffn": build_l2_ffn,
            "l3_ffn": build_l3_ffn, "l3_attn": build_l3_attn}
SUBLAYERS = ["l0_attn", "l0_ffn", "l1_attn", "l1_ffn", "l2_attn",
             "l2_ffn", "l3_attn", "l3_ffn"]


def build_fused_program():
    nc = bass.Bass("TRN2", target_bir_lowering=False)
    xin = nc.dram_tensor("xin", [D, TB], F32, kind="ExternalInput").ap()
    xout = nc.dram_tensor("xout", [D, TB], F32, kind="ExternalOutput").ap()
    W = {}
    for name in SUBLAYERS:
        for n in WNAMES[name]:
            W[n] = nc.dram_tensor(n, WSHAPES[n], F32, kind="ExternalInput").ap()
        for n, (shp, dt) in WSCRATCH.get(name, {}).items():
            W[n] = nc.dram_tensor(n, shp, dt, kind="Internal").ap()
    W["x32"] = xin
    x32_scratch = nc.dram_tensor("x32s", [D, TB], F32, kind="Internal").ap()
    with ExitStack() as ctx:
        k = K(nc, ctx)
        build_common(k)
        build_io_load(k, xin)
        for name in SUBLAYERS:
            if name == "l1_ffn":
                stgx = k.tensors["io_stg"]
                for db in range(DBLK):
                    for hh in range(4):
                        k.op("s", lambda e, db=db, hh=hh: e.copy(
                            stgx[:, 0:1024],
                            k.tensors["x_sb"][:, db * TB + hh * 1024:
                                              db * TB + (hh + 1) * 1024]),
                            reads=[f"x{db}"], writes=["io_stg"])
                        k.dma("y", x32_scratch[db * 128:(db + 1) * 128,
                                               hh * 1024:(hh + 1) * 1024],
                              stgx[:, 0:1024], reads=["io_stg"], writes=["d_x32"])
                k.sched.barrier()
                W["x32"] = x32_scratch
            k.push_scope()
            BUILDERS[name](k, W)
            k.pop_scope()
        build_io_store(k, xout)
        sems = {}
        for e in ENGS:
            sems[("c", e)] = ctx.enter_context(nc.semaphore(f"sem_{e}"))
        for q in DMA_QUEUES:
            sems[("d", q)] = ctx.enter_context(nc.semaphore(f"dsem_{q}"))
        k.sched.realize(nc, sems)
    return nc


def build_sublayer_program(name, debug=False):
    nc = bass.Bass("TRN2", target_bir_lowering=False)
    xin = nc.dram_tensor("xin", [D, TB], F32, kind="ExternalInput").ap()
    xout = nc.dram_tensor("xout", [D, TB], F32, kind="ExternalOutput").ap()
    dbg = nc.dram_tensor("dbg", [128, 1024], F32, kind="ExternalOutput").ap() \
        if debug else None
    wnames = WNAMES[name]
    W = {n: nc.dram_tensor(n, WSHAPES[n], F32, kind="ExternalInput").ap()
         for n in wnames}
    for n, (shp, dt) in WSCRATCH.get(name, {}).items():
        W[n] = nc.dram_tensor(n, shp, dt, kind="Internal").ap()
    W["x32"] = xin
    with ExitStack() as ctx:
        k = K(nc, ctx)
        build_common(k)
        build_io_load(k, xin)
        BUILDERS[name](k, W)
        if debug:
            dstg = k.tensors["io_stg"]
            for nm, col in (("f2_lrow", 0), ("f2_mcol", 64), ("f2_ohc", 96),
                            ("f2_brows", 128), ("l3_xdbl32", 0),
                            ("l3_srp", 384), ("l3_u", 512)):
                if nm in k.tensors:
                    h = k.tensors[nm]
                    p, f = h.shape
                    n = min(f, 384 if col < 384 else 128)
                    k.op("s", lambda e, h=h, p=p, n=n: e.copy(
                        dstg[0:p, 0:n], h[:, 0:n]), reads=[nm], writes=["io_stg"])
                    k.dma("g", dbg[0:p, col:col + n], dstg[0:p, 0:n],
                          reads=["io_stg"], writes=[])
            if "l3a_yfin_dram" in W:
                k.dma("g", dstg[:, 0:384],
                      W["l3a_opT_bf"][0:128, 0:384], writes=["io_stg"])
                k.dma("g", dbg[:, 640:1024], dstg[:, 0:384],
                      reads=["io_stg"], writes=[])
            if "l2f_wsel_dram" in W:
                k.dma("y", dstg[:, 0:384], W["l2f_wsel_dram"][0:128, 0:384],
                      writes=["io_stg"]) if False else None
                k.op("s", lambda e: e.copy(dstg[:, 0:384],
                                           k.tensors["f2_ws"][:, 1024:1408]),
                     reads=["f2_ws1"], writes=["io_stg"])
                k.dma("g", dbg[:, 512:896], dstg[:, 0:384],
                      reads=["io_stg"], writes=[])
        build_io_store(k, xout)
        sems = {}
        for e in ENGS:
            sems[("c", e)] = ctx.enter_context(nc.semaphore(f"sem_{e}"))
        for q in DMA_QUEUES:
            sems[("d", q)] = ctx.enter_context(nc.semaphore(f"dsem_{q}"))
        k.sched.realize(nc, sems)
    return nc, wnames


FUSED = True


def kernel(**inputs):
    W = prep_weights(inputs)
    xs = shard_x(inputs["L0_attn_x"])
    if FUSED:
        nc = build_fused_program()
        wall = {n: W[n] for name in SUBLAYERS for n in WNAMES[name]}
        in_maps = [dict({"xin": xs[c]}, **wall) for c in range(NCORES)]
        res = run_bass_kernel_spmd(nc, in_maps, core_ids=list(range(NCORES)))
        xs = [res.results[c]["xout"] for c in range(NCORES)]
        return unshard_x(xs)
    for name in SUBLAYERS:
        nc, wnames = build_sublayer_program(name)
        in_maps = [dict({"xin": xs[c]}, **{n: W[n] for n in wnames})
                   for c in range(NCORES)]
        res = run_bass_kernel_spmd(nc, in_maps, core_ids=list(range(NCORES)))
        xs = [res.results[c]["xout"] for c in range(NCORES)]
    return unshard_x(xs)


# version 08 (final)
# speedup vs baseline: 1.2308x; 1.2308x over previous
import math
from contextlib import ExitStack

import numpy as np
import ml_dtypes

import concourse.bass as bass
import concourse.mybir as mybir
from concourse.bass_utils import run_bass_kernel_spmd

F32 = mybir.dt.float32
BF16 = mybir.dt.bfloat16
AF = mybir.ActivationFunctionType
ALU = mybir.AluOpType
AX = mybir.AxisListType.X

NCORES = 8
BPC = 2
T = 2048
D = 1024
TB = BPC * T
DBLK = D // 128

ENGS = ("t", "v", "s", "g", "y")
SECTION = {"t": "tensor", "v": "vector", "s": "scalar", "g": "gpsimd", "y": "sync"}
DMA_QUEUES = ("y", "g", "s", "v")
SELF_EDGES = True


class Buf:
    __slots__ = ("w", "r", "name")

    def __init__(self, name=""):
        self.w = None
        self.r = set()
        self.name = name


class Op:
    __slots__ = ("eng", "waits", "emit", "incs")

    def __init__(self, eng, waits, emit, incs):
        self.eng, self.waits, self.emit, self.incs = eng, waits, emit, incs


class Sched:
    def __init__(self):
        self.ops = []
        self.count = {e: 0 for e in ENGS}
        self.dcount = {q: 0 for q in DMA_QUEUES}
        self.known = {e: {} for e in ENGS}
        self.pending = {e: set() for e in ENGS}

    def _waits(self, eng, tags):
        ws = []
        for key, val in sorted(tags):
            if key == ("c", eng) and (eng == "t" or not SELF_EDGES):
                continue
            if self.known[eng].get(key, 0) >= val:
                continue
            ws.append((key, val))
            self.known[eng][key] = val
        return ws

    def _tags(self, reads, writes):
        tags = set()
        for b in reads:
            if b.w:
                tags.add(b.w)
        for b in writes:
            if b.w:
                tags.add(b.w)
            tags.update(b.r)
        return tags

    def _sound_dma(self, tags):
        out = set()
        for key, val in tags:
            if key[0] == "d":
                out.add((key, self.dcount[key[1]]))
            else:
                out.add((key, val))
        return out

    def op(self, eng, emit, reads=(), writes=()):
        tags = self._sound_dma(self._tags(reads, writes)) | self.pending[eng]
        self.pending[eng] = set()
        ws = self._waits(eng, tags)
        self.count[eng] += 1
        tag = (("c", eng), self.count[eng])
        self.ops.append(Op(eng, ws, emit, [(("c", eng), 1)]))
        for b in writes:
            b.w = tag
            b.r = set()
        for b in reads:
            b.r.add(tag)
        return tag

    def dma(self, q, emit, reads=(), writes=()):
        tags = self._sound_dma(self._tags(reads, writes)) | self.pending[q]
        self.pending[q] = set()
        ws = self._waits(q, tags)
        self.dcount[q] += 16
        tag = (("d", q), self.dcount[q])
        self.ops.append(Op(q, ws, emit, [(("d", q), 16)]))
        for b in writes:
            b.w = tag
            b.r = set()
        for b in reads:
            b.r.add(tag)
        return tag

    def barrier(self):
        tags = set()
        for e in ENGS:
            if self.count[e]:
                tags.add((("c", e), self.count[e]))
        for q in DMA_QUEUES:
            if self.dcount[q]:
                tags.add((("d", q), self.dcount[q]))
        for e in ENGS:
            self.pending[e] |= tags

    def flush_waits(self, eng):
        tags = self.pending[eng]
        self.pending[eng] = set()
        ws = self._waits(eng, tags)
        if ws:
            self.ops.append(Op(eng, ws, None, []))

    def realize(self, nc, sems):
        with nc.Block() as block:
            for eng in ENGS:
                ops = [o for o in self.ops if o.eng == eng]
                if not ops:
                    continue
                dec = getattr(block, SECTION[eng])

                def body(e, ops=ops):
                    for o in ops:
                        for key, val in o.waits:
                            e.wait_ge(sems[key], val)
                        if o.emit is None:
                            continue
                        inst = o.emit(e)
                        for key, amt in o.incs:
                            inst.then_inc(sems[key], amt)

                dec(body)


class K:
    def __init__(self, nc, ctx):
        self.nc = nc
        self.ctx = ctx
        self.sched = Sched()
        self.bufs = {}
        self.tensors = {}
        self.scopes = []
        self.uid = 0

    def push_scope(self):
        self.scopes.append((ExitStack(), []))

    def pop_scope(self):
        ctx, names = self.scopes.pop()
        ctx.close()
        for n in names:
            self.tensors.pop(n, None)

    def sb(self, name, shape, dtype):
        if name in self.tensors:
            return self.tensors[name]
        self.uid += 1
        pname = f"{name}__{self.uid}"
        if self.scopes:
            ctx, names = self.scopes[-1]
            h = ctx.enter_context(self.nc.sbuf_tensor(pname, shape, dtype))
            names.append(name)
        else:
            h = self.ctx.enter_context(self.nc.sbuf_tensor(pname, shape, dtype))
        self.tensors[name] = h
        return h

    def ps(self, name, shape, dtype=F32):
        h = self.ctx.enter_context(self.nc.psum_tensor(name, shape, dtype))
        self.tensors[name] = h
        return h

    def nbuf(self, name):
        b = self.bufs.get(name)
        if b is None:
            b = self.bufs[name] = Buf(name)
        return b

    def op(self, eng, emit, reads=(), writes=()):
        return self.sched.op(eng, emit, [self.nbuf(r) for r in reads],
                             [self.nbuf(w) for w in writes])

    def dma(self, q, out_ap, in_ap, reads=(), writes=(), **kw):
        return self.sched.dma(
            q, lambda e: e.dma_start(out=out_ap, in_=in_ap, **kw),
            [self.nbuf(r) for r in reads], [self.nbuf(w) for w in writes])

    def mm(self, ps_ap, lhsT_ap, rhs_ap, start, stop, reads=(), writes=()):
        return self.op(
            "t", lambda e: e.matmul(ps_ap, lhsT_ap, rhs_ap, start=start, stop=stop),
            reads=reads, writes=writes)



def _t(a):
    return np.ascontiguousarray(np.asarray(a).T).astype(np.float32, copy=False)


def _row(a):
    return np.ascontiguousarray(np.asarray(a).reshape(1, -1)).astype(np.float32, copy=False)


def prep_weights(inp):
    w = {}
    assert np.asarray(inp["L3_ffn_router_prob"]).shape[2] == 4
    w["l0a_wqT"] = _t(inp["L0_attn_Wq"])
    w["l0a_wkT"] = _t(inp["L0_attn_Wk"])
    w["l0a_wvT"] = _t(inp["L0_attn_Wv"])
    w["l0a_woT"] = _t(inp["L0_attn_Wo"])
    w["l0a_bq"] = _row(inp["L0_attn_bq"])
    w["l0a_bk"] = _row(inp["L0_attn_bk"])
    w["l0a_bv"] = _row(inp["L0_attn_bv"])
    w["l0a_bo"] = _row(inp["L0_attn_bo"])
    w["l0a_U"] = np.ascontiguousarray(np.asarray(inp["L0_attn_U"], np.float32))
    w["l0a_V"] = np.ascontiguousarray(np.asarray(inp["L0_attn_V"], np.float32))
    w["l0f_wfcT"] = _t(inp["L0_ffn_W_fc"])
    w["l0f_wprojT"] = _t(inp["L0_ffn_W_proj"])
    w["l0f_lnw"] = _row(inp["L0_ffn_ln_w"])
    w["l0f_lnb"] = _row(inp["L0_ffn_ln_b"])
    w["l1a_watt"] = np.ascontiguousarray(np.asarray(inp["L1_attn_w_attn"], np.float32))
    w["l1a_batt"] = _row(inp["L1_attn_b_attn"])
    w["l1a_wproj"] = np.ascontiguousarray(np.asarray(inp["L1_attn_w_proj"], np.float32))
    w["l1a_bproj"] = _row(inp["L1_attn_b_proj"])
    w["l1f_w1s"] = np.ascontiguousarray(
        np.asarray(inp["L1_ffn_W1"], np.float32).transpose(1, 0, 2).reshape(D, D))
    w["l1f_b1"] = _row(inp["L1_ffn_b1"])
    w["l1f_w2s"] = np.ascontiguousarray(
        np.asarray(inp["L1_ffn_W2"], np.float32).reshape(D, D))
    w["l1f_b2"] = np.ascontiguousarray(np.asarray(inp["L1_ffn_b2"], np.float32))
    w["l1f_wr"] = np.ascontiguousarray(np.asarray(inp["L1_ffn_Wr"], np.float32))
    w["l1f_br"] = _row(inp["L1_ffn_br"])
    s = np.zeros((16, D), np.float32)
    for m in range(D):
        s[m // 64, m] = 1.0
    w["l1f_sel"] = s
    perm = np.arange(3072)
    for sec in range(2):
        for hh in range(16):
            base = sec * 1024 + hh * 64
            perm[base:base + 32] = base + np.arange(0, 64, 2)
            perm[base + 32:base + 64] = base + np.arange(1, 64, 2)
    wq = np.asarray(inp["L2_attn_w_qkv"], np.float32)
    w["l2a_wqkvT"] = np.ascontiguousarray(wq[perm].T)
    bq = np.asarray(inp["L2_attn_b_qkv"], np.float32)[perm]
    w["l2a_bqkv"] = np.ascontiguousarray(bq.reshape(1, -1))
    w["l2a_woutT"] = _t(inp["L2_attn_w_out"])
    w["l2a_bout"] = _row(inp["L2_attn_b_out"])
    inv_freq = (1.0 / np.float32(10000.0) **
                (np.arange(0, 64, 2, dtype=np.float32) / np.float32(64.0))
                ).astype(np.float32)
    pos = np.arange(T, dtype=np.float32)
    freqs = (pos[None, :] * inv_freq[:, None]).astype(np.float32)
    sin = np.sin(freqs).astype(np.float32)
    cos = np.cos(freqs).astype(np.float32)
    w["l2a_cos4"] = np.ascontiguousarray(np.tile(cos, (4, 1)))
    w["l2a_ssin4"] = np.ascontiguousarray(
        np.concatenate([-sin, sin, -sin, sin], axis=0))
    w["l2f_baseWT"] = _t(inp["L2_ffn_base_W"])
    w["l2f_base_b"] = _row(inp["L2_ffn_base_b"])
    w["l2f_muT"] = _t(inp["L2_ffn_disc_mu"])
    w["l2f_sigT"] = _t(inp["L2_ffn_disc_sigma"])
    w["l2f_downWT"] = np.ascontiguousarray(
        np.asarray(inp["L2_ffn_down_W"], np.float32).transpose(1, 0, 2)
        .reshape(D, 8 * 64))
    w["l2f_down_b"] = np.ascontiguousarray(np.asarray(inp["L2_ffn_down_b"],
                                                      np.float32))
    w["l2f_upWs"] = np.ascontiguousarray(
        np.asarray(inp["L2_ffn_up_W"], np.float32).reshape(8 * 64, D))
    w["l2f_up_b"] = np.ascontiguousarray(np.asarray(inp["L2_ffn_up_b"],
                                                    np.float32))
    w["l2f_convWT"] = np.ascontiguousarray(
        np.asarray(inp["L2_ffn_conv_W"], np.float32).transpose(0, 2, 1)
        .reshape(8 * D, D))
    w["l2f_conv_b"] = np.ascontiguousarray(np.asarray(inp["L2_ffn_conv_b"],
                                                      np.float32))
    w["l3f_qwT"] = _t(inp["L3_ffn_q_w"])
    w["l3f_kwT"] = _t(inp["L3_ffn_k_w"])
    w["l3f_vwT"] = _t(inp["L3_ffn_v_w"])
    w["l3f_owT"] = _t(inp["L3_ffn_o_w"])
    w["l3f_qb"] = _row(inp["L3_ffn_q_b"])
    w["l3f_kb"] = _row(inp["L3_ffn_k_b"])
    w["l3f_vb"] = _row(inp["L3_ffn_v_b"])
    w["l3f_ob"] = _row(inp["L3_ffn_o_b"])
    w["l3f_nw"] = _row(inp["L3_ffn_norm_w"])
    w["l3f_nb"] = _row(inp["L3_ffn_norm_b"])
    w["l3a_ipT"] = _t(inp["L3_attn_in_proj_w"])
    w["l3a_convw"] = np.ascontiguousarray(
        np.asarray(inp["L3_attn_conv_w"], np.float32).reshape(2048, 4))
    w["l3a_convb"] = np.ascontiguousarray(
        np.asarray(inp["L3_attn_conv_b"], np.float32).reshape(2048, 1))
    w["l3a_dtb"] = np.ascontiguousarray(
        np.asarray(inp["L3_attn_dt_head_b"], np.float32).reshape(2048, 1))
    w["l3a_D"] = np.ascontiguousarray(
        np.asarray(inp["L3_attn_D"], np.float32).reshape(2048, 1))
    w["l3a_xpT"] = _t(inp["L3_attn_x_proj_w"])
    w["l3a_dthT"] = _t(inp["L3_attn_dt_head_w"])
    w["l3a_opT"] = _t(inp["L3_attn_out_proj_w"])
    return w


WSHAPES = {
    "l0a_wqT": [D, D], "l0a_wkT": [D, D], "l0a_wvT": [D, D], "l0a_woT": [D, D],
    "l0a_bq": [1, D], "l0a_bk": [1, D], "l0a_bv": [1, D], "l0a_bo": [1, D],
    "l0a_U": [64, 32], "l0a_V": [64, 32],
    "l0f_wfcT": [D, 4096], "l0f_wprojT": [4096, D],
    "l0f_lnw": [1, 4096], "l0f_lnb": [1, 4096],
    "l1a_watt": [D, 3072], "l1a_batt": [1, 3072],
    "l1a_wproj": [D, D], "l1a_bproj": [1, D],
    "l1f_w1s": [D, D], "l1f_b1": [1, D], "l1f_w2s": [D, D], "l1f_b2": [16, D],
    "l1f_wr": [D, 16], "l1f_br": [1, 16], "l1f_sel": [16, D],
    "l2a_wqkvT": [D, 3072], "l2a_bqkv": [1, 3072],
    "l2a_woutT": [D, D], "l2a_bout": [1, D],
    "l2a_cos4": [128, T], "l2a_ssin4": [128, T],
    "l2f_baseWT": [D, D], "l2f_base_b": [1, D], "l2f_muT": [D, 8],
    "l2f_sigT": [D, 8], "l2f_downWT": [D, 512], "l2f_down_b": [8, 64],
    "l2f_upWs": [512, D], "l2f_up_b": [8, D], "l2f_convWT": [8 * D, D],
    "l2f_conv_b": [8, D],
    "l3f_qwT": [D, D], "l3f_kwT": [D, D], "l3f_vwT": [D, D], "l3f_owT": [D, D],
    "l3f_qb": [1, D], "l3f_kb": [1, D], "l3f_vb": [1, D], "l3f_ob": [1, D],
    "l3f_nw": [1, D], "l3f_nb": [1, D],
    "l3a_ipT": [D, 4096], "l3a_convw": [2048, 4], "l3a_convb": [2048, 1],
    "l3a_dtb": [2048, 1], "l3a_D": [2048, 1], "l3a_xpT": [2048, 96],
    "l3a_dthT": [64, 2048], "l3a_opT": [2048, D],
}

WSCRATCH = {
    "l1_attn": {"l1a_watt_bf": ([D, 3072], BF16), "l1a_wproj_bf": ([D, D], BF16)},
    "l2_attn": {"l2a_wqkv_bf": ([D, 3072], BF16), "l2a_wout_bf": ([D, D], BF16)},
    "l0_ffn": {"l0f_wfcq_dram": ([D, 4096], BF16)},
    "l2_ffn": {"l2f_wsel_dram": ([2 * D, D], BF16),
               "l2f_base_dram": ([D, D], BF16)},
    "l3_attn": {"l3a_xssm_dram": ([2 * 2048, T], BF16),
                "l3a_yfin_dram": ([2 * 2048, T], BF16),
                "l3a_opT_bf": ([2048, D], BF16)},
}

WNAMES = {
    "l0_attn": ["l0a_wqT", "l0a_wkT", "l0a_wvT", "l0a_woT",
                "l0a_bq", "l0a_bk", "l0a_bv", "l0a_bo", "l0a_U", "l0a_V"],
    "l0_ffn": ["l0f_wfcT", "l0f_wprojT", "l0f_lnw", "l0f_lnb"],
    "l1_attn": ["l1a_watt", "l1a_batt", "l1a_wproj", "l1a_bproj"],
    "l1_ffn": ["l1f_w1s", "l1f_b1", "l1f_w2s", "l1f_b2", "l1f_wr", "l1f_br",
               "l1f_sel"],
    "l2_attn": ["l2a_wqkvT", "l2a_bqkv", "l2a_woutT", "l2a_bout",
                "l2a_cos4", "l2a_ssin4"],
    "l2_ffn": ["l2f_baseWT", "l2f_base_b", "l2f_muT", "l2f_sigT", "l2f_downWT",
               "l2f_down_b", "l2f_upWs", "l2f_up_b", "l2f_convWT", "l2f_conv_b"],
    "l3_ffn": ["l3f_qwT", "l3f_kwT", "l3f_vwT", "l3f_owT", "l3f_qb", "l3f_kb",
               "l3f_vb", "l3f_ob", "l3f_nw", "l3f_nb"],
    "l3_attn": ["l3a_ipT", "l3a_convw", "l3a_convb", "l3a_dtb", "l3a_D",
                "l3a_xpT", "l3a_dthT", "l3a_opT"],
}


def shard_x(x):
    shards = []
    for c in range(NCORES):
        xs = np.asarray(x[c * BPC:(c + 1) * BPC])
        xs = np.ascontiguousarray(xs.transpose(2, 0, 1).reshape(D, TB))
        shards.append(xs.astype(np.float32, copy=False))
    return shards


def unshard_x(shards):
    out = np.empty((NCORES * BPC, T, D), np.float32)
    for c, xs in enumerate(shards):
        out[c * BPC:(c + 1) * BPC] = np.asarray(xs).reshape(D, BPC, T).transpose(1, 2, 0)
    return out



ROW_MU, ROW_R, ROW_RECIP = 32, 0, 64


def build_common(k: K, consts=(1e-5,)):
    for i, cv in enumerate(consts):
        ct = k.sb(f"const_{i}", [128, 1], F32)
        k.op("v", lambda e, ct=ct, cv=cv: e.memset(ct[:], cv), writes=[f"const_{i}"])
        k.nc.const_aps.register(F32, cv, ct[:, 0:1])
    k.sched.barrier()
    ones_col = k.sb("ones_col", [128, 1], BF16)
    ones_row = k.sb("ones_row", [1, 512], BF16)
    orf = k.sb("orf", [1, 128], F32)
    ocf = k.sb("ones_col_f", [128, 1], F32)
    k.op("v", lambda e: e.memset(ocf[:], 1.0), writes=["ones_col_f"])
    k.op("v", lambda e: e.memset(ones_col[:], 1.0), writes=["ones_col"])
    k.op("v", lambda e: e.memset(ones_row[:], 1.0), writes=["ones_row"])
    k.op("v", lambda e: e.memset(orf[:], 1.0), writes=["orf"])
    k.sb("row_t1", [1, 512], F32)
    k.sb("row_t2", [32, 512], F32)
    k.sb("row_t3", [1, 512], F32)
    k.sb("lnbf", [1, TB], BF16)
    k.sb("r_rep", [128, TB], BF16)
    k.sb("sq_t", [128, 1024], BF16)
    k.ps("psA", [128, 512])
    k.ps("psB", [128, 512])
    k.ps("psC", [128, 512])
    k.ps("psD", [128, 512])
    k.ps("psE1", [128, 512])
    k.ps("psE2", [128, 512])
    k.ps("psR1", [1, 512])
    k.ps("psR2", [1, 512])


def rowsl(k, base, t0, n):
    return k.tensors["rowbank"][base:base + 1, t0:t0 + n]


def build_io_load(k: K, xin_ap):
    x_sb = k.sb("x_sb", [128, DBLK * TB], BF16)
    stg = k.sb("io_stg", [128, 1024], F32)
    for db in range(DBLK):
        for hh in range(4):
            k.dma("y", stg[:, 0:1024],
                  xin_ap[db * 128:(db + 1) * 128, hh * 1024:(hh + 1) * 1024],
                  reads=[], writes=["io_stg"])
            k.op("g" if db % 2 else "s",
                 lambda e, db=db, hh=hh: (e.tensor_copy if hasattr(e, "tensor_copy")
                                          else e.copy)(
                     x_sb[:, db * TB + hh * 1024:db * TB + (hh + 1) * 1024],
                     stg[:, 0:1024]),
                 reads=["io_stg"], writes=[f"x{db}"])
    return x_sb


def build_io_store(k: K, xout_ap):
    x = k.tensors["x_sb"]
    stg = k.tensors["io_stg"]
    for db in range(DBLK):
        for hh in range(4):
            k.op("s", lambda e, db=db, hh=hh: e.copy(
                stg[:, 0:1024],
                x[:, db * TB + hh * 1024:db * TB + (hh + 1) * 1024]),
                reads=[f"x{db}"], writes=["io_stg"])
            k.dma("y", xout_ap[db * 128:(db + 1) * 128, hh * 1024:(hh + 1) * 1024],
                  stg[:, 0:1024], reads=["io_stg"], writes=[])
    k.sched.pending["y"].add((("d", "y"), k.sched.dcount["y"]))
    k.sched.flush_waits("y")


def build_ln_rows(k: K):
    x = k.tensors["x_sb"]
    lnbf = k.tensors["lnbf"]
    r_rep = k.tensors["r_rep"]
    sq = k.tensors["sq_t"]
    oc = k.tensors["ones_col"]
    orf = k.tensors["orf"]
    t1, t3 = k.tensors["row_t1"], k.tensors["row_t3"]
    t2 = k.tensors["row_t2"][0:1, :]
    psR1, psR2 = k.tensors["psR1"], k.tensors["psR2"]
    psA, psB = k.tensors["psA"], k.tensors["psB"]
    for fc in range(TB // 512):
        f0 = fc * 512
        for db in range(DBLK):
            xs = x[:, db * TB + f0: db * TB + f0 + 512]
            sqs = sq[:, (db % 2) * 512:(db % 2) * 512 + 512]
            k.op("s", lambda e, xs=xs, sqs=sqs: e.activation(sqs, xs, AF.Square),
                 reads=[f"x{db}"], writes=[f"sq{db % 2}"])
            k.mm(psR1[:], oc[:], xs, start=db == 0, stop=db == DBLK - 1,
                 reads=["ones_col", f"x{db}"], writes=["psR1"])
            k.mm(psR2[:], oc[:], sqs, start=db == 0, stop=db == DBLK - 1,
                 reads=["ones_col", f"sq{db % 2}"], writes=["psR2"])
        k.op("v", lambda e: e.tensor_scalar_mul(t1[:], psR1[:], 1.0 / D),
             reads=["psR1"], writes=["row_t1"])
        k.op("s", lambda e, f0=f0: e.copy(lnbf[0:1, f0:f0 + 512], t1[:]),
             reads=["row_t1"], writes=["lnbf"])
        k.op("v", lambda e: e.tensor_scalar_mul(t3[:], psR2[:], 1.0 / D),
             reads=["psR2"], writes=["row_t3"])
        k.op("s", lambda e: e.activation(t2, t1[:], AF.Square),
             reads=["row_t1"], writes=["row_t2"])
        k.op("v", lambda e: e.tensor_sub(t3[:], t3[:], t2),
             reads=["row_t3", "row_t2"], writes=["row_t3"])
        if "recipbf" in k.tensors:
            rb = k.tensors["recipbf"]
            k.op("s", lambda e: e.activation(t2, t3[:], AF.Sqrt, bias=1e-5),
                 reads=["row_t3"], writes=["row_t2"])
            k.op("s", lambda e, f0=f0, rb=rb: e.copy(rb[0:1, f0:f0 + 512], t2),
                 reads=["row_t2"], writes=["recipbf"])
        k.op("s", lambda e: e.activation(t3[:], t3[:], AF.Ln, bias=1e-5),
             reads=["row_t3"], writes=["row_t3"])
        k.op("s", lambda e: e.activation(t3[:], t3[:], AF.Exp, scale=-0.5),
             reads=["row_t3"], writes=["row_t3"])
        ps = psA if fc % 2 == 0 else psB
        pn = "psA" if fc % 2 == 0 else "psB"
        k.mm(ps[:], orf[:], t3[:], True, True, reads=["orf", "row_t3"], writes=[pn])
        k.op("s", lambda e, f0=f0, ps=ps: e.copy(r_rep[:, f0:f0 + 512], ps[:]),
             reads=[pn], writes=["r_rep"])
        if "rcol" in k.tensors:
            rc = k.tensors["rcol"]
            psO = psA if fc % 2 else psB
            pnO = "psA" if fc % 2 else "psB"
            for j in range(4):
                k.mm(psO[:, j:j + 1], t3[0:1, j * 128:(j + 1) * 128], orf[0:1, 0:1],
                     True, True, reads=["row_t3", "orf"], writes=[pnO])
            k.op("s", lambda e, fc=fc, psO=psO: e.copy(rc[:, fc * 4:fc * 4 + 4],
                                                       psO[:, 0:4]),
                 reads=[pnO], writes=["rcol"])



def build_l0_attn(k: K, W):
    x = k.tensors["x_sb"]
    r_rep = k.tensors["r_rep"]
    lnbf = k.tensors["lnbf"]
    oc, orow, orf = k.tensors["ones_col"], k.tensors["ones_row"], k.tensors["orf"]
    psA, psB, psC, psD = (k.tensors[n] for n in ("psA", "psB", "psC", "psD"))
    psE = [k.tensors["psE1"][0:33, :], k.tensors["psE2"][0:33, :]]
    psR1 = k.tensors["psR1"]

    k.sb("recipbf", [1, TB], BF16)
    build_ln_rows(k)

    uv = k.sb("l0_uv", [128, 64], BF16)
    stg = k.sb("l0_wstg", [128, 2 * 1024], F32)
    wtb = k.sb("l0_wtb", [128, 3 * 1024], BF16)
    lf = k.sb("l0_lf", [1, 3 * 1024], BF16)
    bias_bf = k.sb("l0_bias_bf", [1, 3 * 1024], BF16)
    bb = k.sb("l0_bias", [128, 1024], F32)
    wo_p = k.sb("l0_wop", [128, 4 * 1024], BF16)
    bo_bf = k.sb("l0_bo_bf", [1, 1024], BF16)
    qkv_hp = k.sb("l0_qkv_hp", [128, 3 * T], BF16)
    zqT = k.sb("l0_zqT", [128, 2 * T], BF16)
    zkT = k.sb("l0_zkT", [128, 2 * T], BF16)
    zv = k.sb("l0_zv", [128, 4 * 16 * 33], BF16)
    yr = k.sb("l0_yr", [128, 4 * T], BF16)
    p_sb = k.sb("l0_p", [128, 2 * 512], BF16)
    srow = k.tensors["row_t1"]
    srep = k.tensors["row_t2"][0:32, :]

    k.dma("g", stg[0:64, 0:32], W["l0a_V"][:, :], writes=["l0stg_0"])
    k.dma("g", stg[0:64, 32:64], W["l0a_U"][:, :], writes=["l0stg_0"])
    k.dma("g", stg[64:128, 0:32], W["l0a_V"][:, :], writes=["l0stg_0"])
    k.dma("g", stg[64:128, 32:64], W["l0a_U"][:, :], writes=["l0stg_0"])
    k.op("s", lambda e: e.copy(uv[:, 0:64], stg[0:128, 0:64]),
         reads=["l0stg_0"], writes=["l0_uv"])
    for i, nm in enumerate(("bq", "bk", "bv", "bo")):
        k.dma("g", bb[32 * i:32 * i + 1, :], W["l0a_" + nm][:, :],
              writes=["l0_bias"])
    for i in range(3):
        k.op("s", lambda e, i=i: e.copy(bias_bf[0:1, i * 1024:(i + 1) * 1024],
                                        bb[32 * i:32 * i + 1, :]),
             reads=["l0_bias"], writes=["l0_bias_bf"])
    k.op("s", lambda e: e.copy(bo_bf[:], bb[96:97, :]),
         reads=["l0_bias"], writes=["l0_bo_bf"])

    ALLS = [f"l0stg_{i}" for i in range(2)]
    ALLW = [f"l0wt_{i}" for i in range(2)]
    for ch in range(4):
        f0 = ch * 256
        k.dma("y", stg[:, 0:2048].rearrange("p (a c) -> p a c", a=8),
              W["l0a_woT"].rearrange("(a p) c -> p a c", p=128)[:, :, f0:f0 + 256],
              reads=[], writes=ALLS)
        k.op("g", lambda e: e.tensor_copy(wtb[:, 0:2048], stg[:, 0:2048]),
             reads=ALLS, writes=ALLW)
        for h in range(16):
            hg, hi = h // 4, h % 4
            rb, base = h // 2, 64 * (h % 2)
            ps = psA if h % 2 == 0 else psB
            pn = "psA" if h % 2 == 0 else "psB"
            k.mm(ps[0:32, 0:256], uv[base:base + 64, 32:64],
                 wtb[base:base + 64, rb * 256:(rb + 1) * 256],
                 True, True, reads=["l0_uv"] + ALLW, writes=[pn])
            k.op("s", lambda e, hg=hg, hi=hi, f0=f0, ps=ps:
                 e.copy(wo_p[32 * hi:32 * hi + 32,
                             hg * 1024 + f0:hg * 1024 + f0 + 256], ps[0:32, 0:256]),
                 reads=[pn], writes=["l0_wop"])

    SC = 1.0 / math.sqrt(32.0)
    for b in range(BPC):
        for quarter in range(4):
            for hp in range(2):
                hp_g = quarter * 2 + hp
                for wi, wn in enumerate(("wq", "wk", "wv")):
                    st = stg[:, (wi % 2) * 1024:(wi % 2 + 1) * 1024]
                    wt = wtb[:, wi * 1024:(wi + 1) * 1024]
                    stn, wtn = f"l0stg_{wi % 2}", f"l0wt_{wi}"
                    k.dma("y", st.rearrange("p (a c) -> p a c", a=8),
                          W["l0a_" + wn + "T"].rearrange("(a p) c -> p a c", p=128)
                          [:, :, hp_g * 128:(hp_g + 1) * 128],
                          reads=[], writes=[stn])
                    k.op("g", lambda e, wt=wt, st=st: e.tensor_copy(wt, st),
                         reads=[stn], writes=[wtn])
                    for db in range(DBLK):
                        k.mm(psR1[:, 0:128], oc[:], wt[:, db * 128:(db + 1) * 128],
                             start=db == 0, stop=db == DBLK - 1,
                             reads=["ones_col", wtn], writes=["psR1"])
                    k.op("s", lambda e, wi=wi, hp_g=hp_g:
                         e.activation(lf[0:1, wi * 1024 + hp_g * 128:
                                         wi * 1024 + (hp_g + 1) * 128],
                                      psR1[:, 0:128], AF.Copy, scale=-1.0),
                         reads=["psR1"], writes=["l0_lf"])
                    for fc in range(T // 512):
                        f0 = b * T + fc * 512
                        ps = psA if fc % 2 == 0 else psB
                        pn = "psA" if fc % 2 == 0 else "psB"
                        for db in range(DBLK):
                            k.mm(ps[:], wt[:, db * 128:(db + 1) * 128],
                                 x[:, db * TB + f0:db * TB + f0 + 512],
                                 start=db == 0, stop=False,
                                 reads=[wtn, f"x{db}"], writes=[pn])
                        k.mm(ps[:], lf[0:1, wi * 1024 + hp_g * 128:
                                       wi * 1024 + (hp_g + 1) * 128],
                             lnbf[0:1, f0:f0 + 512], start=False, stop=False,
                             reads=["l0_lf", "lnbf"], writes=[pn])
                        k.mm(ps[:], bias_bf[0:1, wi * 1024 + hp_g * 128:
                                            wi * 1024 + (hp_g + 1) * 128],
                             k.tensors["recipbf"][0:1, f0:f0 + 512],
                             start=False, stop=True,
                             reads=["l0_bias_bf", "recipbf"], writes=[pn])
                        dst = qkv_hp[:, wi * T + fc * 512: wi * T + fc * 512 + 512]
                        k.op("v", lambda e, dst=dst, ps=ps, f0=f0:
                             e.tensor_mul(dst, ps[:], r_rep[:, f0:f0 + 512]),
                             reads=[pn, "r_rep"], writes=[f"l0qkv_{wi}"])
                for hh in range(2):
                    h_loc = hp * 2 + hh
                    base = 64 * hh
                    rowz = 64 * (h_loc % 2)
                    colg = h_loc // 2
                    for wi in range(2):
                        ps = psA if wi == 0 else psB
                        pn = "psA" if wi == 0 else "psB"
                        zdst = zqT if wi == 0 else zkT
                        zdn = "l0_zqT" if wi == 0 else "l0_zkT"
                        for fc in range(T // 512):
                            k.mm(ps[0:32, :], uv[base:base + 64, 0:32],
                                 qkv_hp[base:base + 64,
                                        wi * T + fc * 512:wi * T + fc * 512 + 512],
                                 True, True, reads=["l0_uv", f"l0qkv_{wi}"],
                                 writes=[pn])
                            k.op("s", lambda e, zdst=zdst, rowz=rowz, colg=colg,
                                 fc=fc, ps=ps:
                                 e.copy(zdst[rowz:rowz + 32,
                                             colg * T + fc * 512:colg * T + fc * 512 + 512],
                                        ps[0:32, :]),
                                 reads=[pn], writes=[zdn])
                    for kt in range(T // 128):
                        k.mm(psC[:, (kt % 4) * 128:(kt % 4) * 128 + 32],
                             qkv_hp[base:base + 64, 2 * T + kt * 128:2 * T + (kt + 1) * 128],
                             uv[base:base + 64, 0:32], True, True,
                             reads=["l0qkv_2", "l0_uv"], writes=["psC"])
                        if kt % 4 == 3:
                            kt0 = kt - 3
                            k.op("s", lambda e, h_loc=h_loc, kt0=kt0:
                                 e.copy(zv[:, h_loc * 16 * 33 + kt0 * 33:
                                           h_loc * 16 * 33 + (kt0 + 4) * 33]
                                        .rearrange("p (a c) -> p a c", a=4)[:, :, 0:32],
                                        psC[:].rearrange("p (a c) -> p a c", a=4)
                                        [:, :, 0:32]),
                                 reads=["psC"], writes=["l0_zv"])
            k.op("v", lambda e: e.memset(
                zv[:].rearrange("p (a c) -> p a c", c=33)[:, :, 32:33], 1.0),
                writes=["l0_zv"])

            for h_loc in range(4):
                h = quarter * 4 + h_loc
                hg, hi = h // 4, h % 4
                rowz = 64 * (h_loc % 2)
                colg = h_loc // 2
                for qc in range(T // 512):
                    pse = psE[qc % 2]
                    pen = "psE1" if qc % 2 == 0 else "psE2"
                    for kt in range(T // 128):
                        pss = psC if kt % 2 == 0 else psD
                        psn = "psC" if kt % 2 == 0 else "psD"
                        k.mm(pss[:],
                             zkT[rowz:rowz + 32,
                                 colg * T + kt * 128:colg * T + (kt + 1) * 128],
                             zqT[rowz:rowz + 32,
                                 colg * T + qc * 512:colg * T + qc * 512 + 512],
                             True, True, reads=["l0_zqT", "l0_zkT"], writes=[psn])
                        pt = p_sb[:, (kt % 2) * 512:(kt % 2) * 512 + 512]
                        ptn = f"l0p{kt % 2}"
                        k.op("s", lambda e, pt=pt, pss=pss:
                             e.activation(pt, pss[:], AF.Exp, scale=SC),
                             reads=[psn], writes=[ptn])
                        k.mm(pse[:],
                             zv[:, h_loc * 16 * 33 + kt * 33:h_loc * 16 * 33 + (kt + 1) * 33],
                             pt, start=kt == 0, stop=kt == 15,
                             reads=["l0_zv", ptn], writes=[pen])
                    k.op("v", lambda e, pse=pse: e.reciprocal(srow[:], pse[32:33, :]),
                         reads=[pen], writes=["row_t1"])
                    k.mm(psC[0:32, :], orf[0:1, 0:32], srow[:], True, True,
                         reads=["orf", "row_t1"], writes=["psC"])
                    k.op("s", lambda e: e.copy(srep[:], psC[0:32, :]),
                         reads=["psC"], writes=["row_t2"])
                    k.op("v", lambda e, hi=hi, hg=hg, qc=qc, pse=pse:
                         e.tensor_mul(yr[32 * hi:32 * hi + 32,
                                         hg * T + qc * 512:hg * T + qc * 512 + 512],
                                      pse[0:32, :], srep[:]),
                         reads=[pen, "row_t2"], writes=["l0_yr"])

        for db in range(DBLK):
            for fc in range(T // 512):
                f0 = fc * 512
                ps = psA if fc % 2 == 0 else psB
                pn = "psA" if fc % 2 == 0 else "psB"
                for hg in range(4):
                    k.mm(ps[:], wo_p[:, hg * 1024 + db * 128:hg * 1024 + (db + 1) * 128],
                         yr[:, hg * T + f0:hg * T + f0 + 512],
                         start=hg == 0, stop=False,
                         reads=["l0_wop", "l0_yr"], writes=[pn])
                k.mm(ps[:], bo_bf[0:1, db * 128:(db + 1) * 128], orow[:],
                     start=False, stop=True,
                     reads=["l0_bo_bf", "ones_row"], writes=[pn])
                xs = x[:, db * TB + b * T + f0: db * TB + b * T + f0 + 512]
                k.op("v", lambda e, xs=xs, ps=ps: e.tensor_add(xs, ps[:], xs),
                     reads=[pn], writes=[f"x{db}"])

    k.sched.barrier()





def build_l0_ffn(k: K, W):
    x = k.tensors["x_sb"]
    r_rep = k.tensors["r_rep"]
    lnbf = k.tensors["lnbf"]
    oc, orf = k.tensors["ones_col"], k.tensors["orf"]
    ocf = k.tensors["ones_col_f"]
    sq = k.tensors["sq_t"]
    psA, psB, psC, psD = (k.tensors[n] for n in ("psA", "psB", "psC", "psD"))
    psR1, psR2 = k.tensors["psR1"], k.tensors["psR2"]
    t1r, t3r = k.tensors["row_t1"], k.tensors["row_t3"]
    t2r = k.tensors["row_t2"][0:1, :]

    build_ln_rows(k)

    wstg = k.tensors["io_stg"]
    tmp = k.tensors["io_stg"]
    acc = k.sb("f0_acc", [128, 8], F32)
    k.sched.barrier()
    rep = k.sb("f0_rep", [128, 1024], F32)
    wb = k.sb("f0_wb", [128, 64], F32)
    lf = k.sb("f0_lf", [1, 4096], BF16)
    wqt = k.sb("f0_wqt", [128, 2 * 1024], BF16)
    wprojq = k.sb("f0_wprojq", [128, 32 * 1024], BF16)
    scr = k.sb("f0_scr", [128, 32 * 256], BF16)
    NW = 4096.0 * 1024.0

    k.op("v", lambda e: e.memset(acc[:, 0:2], 0.0), writes=["f0_acc01"])
    for wi in range(2):
        for ch in range(32):
            st = wstg[:, 0:1024]
            stn = "f0_wstg0"
            if wi == 0:
                src_ap = W["l0f_wfcT"][(ch // 4) * 128:(ch // 4 + 1) * 128,
                                       (ch % 4) * 1024:(ch % 4 + 1) * 1024]
            else:
                src_ap = W["l0f_wprojT"][ch * 128:(ch + 1) * 128, :]
            k.dma("y", st, src_ap, writes=[stn])
            k.op("v", lambda e, st=st: e.tensor_reduce(
                acc[:, 4:5], st, AX, ALU.add, apply_absolute_value=True),
                reads=[stn], writes=["f0_acc4"])
            k.op("v", lambda e, wi=wi: e.tensor_add(
                acc[:, wi:wi + 1], acc[:, wi:wi + 1], acc[:, 4:5]),
                reads=["f0_acc4"], writes=["f0_acc01"])
    for wi in range(2):
        k.mm(psR1[:, 0:1], ocf[:], acc[:, wi:wi + 1], True, True,
             reads=["ones_col_f", "f0_acc01"], writes=["psR1"])
        k.op("v", lambda e, wi=wi: e.tensor_scalar(
            t1r[0:1, wi:wi + 1], psR1[:, 0:1], 0.5 / NW, 0.5e-6, ALU.mult, ALU.add),
            reads=["psR1"], writes=["row_t1"])
    k.op("v", lambda e: e.tensor_scalar_mul(t1r[0:1, 2:4], t1r[0:1, 0:2], -1.0),
         reads=["row_t1"], writes=["row_t1"])
    for j, src_c in enumerate((0, 2, 1, 3)):
        k.mm(psA[:, j:j + 1], orf[:], t1r[0:1, src_c:src_c + 1], True, True,
             reads=["row_t1", "orf"], writes=["psA"])
    k.op("s", lambda e: e.copy(acc[:, 2:6], psA[:, 0:4]),
         reads=["psA"], writes=["f0_thr"])

    def quant_chunk(src_ap, dst, dstn, sl, thr_c, nthr_c):
        st = wstg[:, 0:1024]
        stn = "f0_wstg0"
        pos = scr[:, sl * 1024:sl * 1024 + 1024]
        posn = f"f0_pos{sl}"
        k.dma("y", st, src_ap, writes=[stn])
        k.op("g", lambda e, st=st, pos=pos, thr_c=thr_c: e.tensor_scalar(
            pos, st, acc[:, thr_c:thr_c + 1], None, ALU.is_gt),
            reads=[stn, "f0_thr"], writes=[posn])
        k.op("v", lambda e, st=st, pos=pos, dst=dst, nthr_c=nthr_c:
             e.scalar_tensor_tensor(dst, st, acc[:, nthr_c:nthr_c + 1], pos,
                                    ALU.is_lt, ALU.subtract),
             reads=[stn, posn, "f0_thr"], writes=[dstn])
        k.op("v", lambda e, dst=dst: e.tensor_scalar_mul(dst, dst, -1.0),
             reads=[dstn], writes=[dstn])

    ci = 0
    for hc in range(4):
        for dmb in range(8):
            sl = ci % 2
            dst = wqt[:, sl * 1024:(sl + 1) * 1024]
            dstn = f"f0_wqt{sl}"
            quant_chunk(W["l0f_wfcT"][dmb * 128:(dmb + 1) * 128,
                                      hc * 1024:(hc + 1) * 1024],
                        dst, dstn, sl, 2, 3)
            k.dma("g", W["l0f_wfcq_dram"][dmb * 128:(dmb + 1) * 128,
                                          hc * 1024:(hc + 1) * 1024], dst,
                  reads=[dstn], writes=[])
            for j in range(2):
                ps = psR1 if j == 0 else psR2
                pn = "psR1" if j == 0 else "psR2"
                k.mm(ps[:], oc[:], dst[:, j * 512:(j + 1) * 512],
                     start=dmb == 0, stop=dmb == 7,
                     reads=["ones_col", dstn], writes=[pn])
            if dmb == 7:
                for j in range(2):
                    ps = psR1 if j == 0 else psR2
                    pn = "psR1" if j == 0 else "psR2"
                    k.op("s", lambda e, hc=hc, j=j, ps=ps: e.activation(
                        lf[0:1, hc * 1024 + j * 512:hc * 1024 + j * 512 + 512],
                        ps[:], AF.Copy, scale=-1.0),
                        reads=[pn], writes=["f0_lf"])
            ci += 1
    for ch in range(32):
        quant_chunk(W["l0f_wprojT"][ch * 128:(ch + 1) * 128, :],
                    wprojq[:, ch * 1024:(ch + 1) * 1024], "f0_wprojq", ch % 2, 4, 5)

    for ri, wnm in enumerate(("l0f_lnw", "l0f_lnb")):
        for ch in range(8):
            k.dma("g", t3r[0:1, :], W[wnm][0:1, ch * 512:(ch + 1) * 512],
                  reads=["row_t3"], writes=["row_t3"])
            for j in range(4):
                k.mm(psB[:, j:j + 1], t3r[0:1, j * 128:(j + 1) * 128],
                     orf[0:1, 0:1], True, True,
                     reads=["row_t3", "orf"], writes=["psB"])
            k.op("s", lambda e, ri=ri, ch=ch: e.copy(
                wb[:, ri * 32 + ch * 4:ri * 32 + ch * 4 + 4], psB[:, 0:4]),
                reads=["psB"], writes=["f0_wb"])
    k.sched.barrier()

    CH = 256
    for fc in range(TB // CH):
        f0 = fc * CH
        for hb in range(32):
            ps = psA if hb % 2 == 0 else psB
            pn = "psA" if hb % 2 == 0 else "psB"
            wqs = wqt[:, (hb % 2) * 1024:(hb % 2) * 1024 + 1024]
            wqsn = f"f0_wqt{hb % 2}"
            if fc % 4 == 0 or True:
                k.dma("y", wqs.rearrange("p (a c) -> p a c", a=8),
                      W["l0f_wfcq_dram"].rearrange("(a p) c -> p a c", p=128)
                      [:, :, hb * 128:(hb + 1) * 128],
                      reads=[], writes=[wqsn])
            for db in range(DBLK):
                k.mm(ps[:, 0:CH], wqs[:, db * 128:(db + 1) * 128],
                     x[:, db * TB + f0:db * TB + f0 + CH],
                     start=db == 0, stop=False, reads=[wqsn, f"x{db}"], writes=[pn])
            k.mm(ps[:, 0:CH], lf[0:1, hb * 128:(hb + 1) * 128], lnbf[0:1, f0:f0 + CH],
                 start=False, stop=True, reads=["f0_lf", "lnbf"], writes=[pn])
            tm = tmp[:, (hb % 2) * CH:(hb % 2) * CH + CH]
            tmn = f"f0_tmp{hb % 2}"
            k.op("v", lambda e, tm=tm, ps=ps, f0=f0: e.tensor_mul(
                tm, ps[:, 0:CH], r_rep[:, f0:f0 + CH]), reads=[pn, "r_rep"],
                writes=[tmn])
            hg = scr[:, hb * CH:(hb + 1) * CH]
            hgn = f"f0_scr{hb}"
            k.op("s", lambda e, hg=hg, tm=tm: e.activation(hg, tm, AF.Gelu),
                 reads=[tmn], writes=[hgn])
            sqs = sq[:, (hb % 2) * CH:(hb % 2) * CH + CH]
            k.op("s", lambda e, sqs=sqs, hg=hg: e.activation(sqs, hg, AF.Square),
                 reads=[hgn], writes=[f"sq{hb % 2}"])
            k.mm(psR1[:, 0:CH], oc[:], hg, start=hb == 0, stop=hb == 31,
                 reads=["ones_col", hgn], writes=["psR1"])
            k.mm(psR2[:, 0:CH], oc[:], sqs, start=hb == 0, stop=hb == 31,
                 reads=["ones_col", f"sq{hb % 2}"], writes=["psR2"])
        H4 = 4096.0
        t1c, t3c, t2c = t1r[0:1, 0:CH], t3r[0:1, 0:CH], t2r[0:1, 0:CH]
        k.op("v", lambda e: e.tensor_scalar_mul(t1c, psR1[:, 0:CH], 1.0 / H4),
             reads=["psR1"], writes=["row_t1"])
        k.op("v", lambda e: e.tensor_scalar_mul(t3c, psR2[:, 0:CH], 1.0 / H4),
             reads=["psR2"], writes=["row_t3"])
        k.op("s", lambda e: e.activation(t2c, t1c, AF.Square),
             reads=["row_t1"], writes=["row_t2"])
        k.op("v", lambda e: e.tensor_sub(t3c, t3c, t2c),
             reads=["row_t3", "row_t2"], writes=["row_t3"])
        k.op("s", lambda e: e.activation(t3c, t3c, AF.Ln, bias=1e-5),
             reads=["row_t3"], writes=["row_t3"])
        k.op("s", lambda e: e.activation(t3c, t3c, AF.Exp, scale=-0.5),
             reads=["row_t3"], writes=["row_t3"])
        k.mm(psC[:, 0:CH], orf[:], t1c, True, True,
             reads=["orf", "row_t1"], writes=["psC"])
        k.op("s", lambda e: e.copy(rep[:, 0:CH], psC[:, 0:CH]),
             reads=["psC"], writes=["f0_rep"])
        k.mm(psD[:, 0:CH], orf[:], t3c, True, True,
             reads=["orf", "row_t3"], writes=["psD"])
        k.op("s", lambda e: e.copy(rep[:, 512:512 + CH], psD[:, 0:CH]),
             reads=["psD"], writes=["f0_rep"])
        for hb in range(32):
            hg = scr[:, hb * CH:(hb + 1) * CH]
            hgn = f"f0_scr{hb}"
            tm = tmp[:, 512 + (hb % 2) * CH:512 + (hb % 2) * CH + CH]
            tmn = f"f0_tmp2{hb % 2}"
            k.op("v", lambda e, tm=tm, hg=hg: e.tensor_sub(tm, hg, rep[:, 0:CH]),
                 reads=[hgn, "f0_rep"], writes=[tmn])
            k.op("v", lambda e, tm=tm: e.tensor_mul(tm, tm, rep[:, 512:512 + CH]),
                 reads=[tmn, "f0_rep"], writes=[tmn])
            k.op("g", lambda e, hg=hg, tm=tm, hb=hb: e.tensor_scalar(
                hg, tm, wb[:, hb:hb + 1], wb[:, 32 + hb:32 + hb + 1],
                ALU.mult, ALU.add),
                reads=[tmn, "f0_wb"], writes=[hgn])
        for db in range(DBLK):
            ps = psA if db % 2 == 0 else psB
            pn = "psA" if db % 2 == 0 else "psB"
            for hb in range(32):
                k.mm(ps[:, 0:CH],
                     wprojq[:, hb * 1024 + db * 128:hb * 1024 + (db + 1) * 128],
                     scr[:, hb * CH:(hb + 1) * CH],
                     start=hb == 0, stop=hb == 31,
                     reads=["f0_wprojq", f"f0_scr{hb}"], writes=[pn])
            xs = x[:, db * TB + f0:db * TB + f0 + CH]
            k.op("v", lambda e, xs=xs, ps=ps: e.tensor_add(xs, ps[:, 0:CH], xs),
                 reads=[pn], writes=[f"x{db}"])

    k.sched.barrier()



def build_l1_attn(k: K, W):
    x = k.tensors["x_sb"]
    r_rep = k.tensors["r_rep"]
    oc, orow, orf = k.tensors["ones_col"], k.tensors["ones_row"], k.tensors["orf"]
    sq = k.tensors["sq_t"]
    psA, psB, psC, psD = (k.tensors[n] for n in ("psA", "psB", "psC", "psD"))
    psR1 = k.tensors["psR1"]
    t1 = k.tensors["row_t1"]

    k.sb("recipbf", [1, TB], BF16)
    k.sb("rcol", [128, 32], F32)
    build_ln_rows(k)
    lnbf = k.tensors["lnbf"]
    recipbf = k.tensors["recipbf"]
    rcol = k.tensors["rcol"]

    kT = k.sb("l1_kT", [128, 8 * T], BF16)
    vb = k.sb("l1_v", [128, 16 * 1024], BF16)
    qch = k.sb("l1_q", [128, 8 * 512], BF16)
    P = k.sb("l1_p", [128, 16 * 512], BF16)
    ybuf = qch
    ws = k.sb("l1_ws", [128, 2 * 1024], BF16)
    lf = k.sb("l1_lf", [1, 3072], BF16)
    bbf = k.sb("l1_bb", [1, 3072], BF16)
    bf32 = k.tensors["io_stg"]
    bpcol = k.sb("l1_bpcol", [128, 8], F32)
    srep = k.sb("l1_srep", [128, 512], F32)
    stg = k.tensors["io_stg"]
    SC = 1.0 / 32.0
    t3 = k.tensors["row_t3"]

    def wtile(src_ap, slot, tag):
        k.dma("y", slot.rearrange("p (a c) -> p a c", a=8),
              src_ap.rearrange("(a p) c -> p a c", p=128), writes=[tag])

    for wi, (srcn, dstn, ncol) in enumerate(
            (("l1a_watt", "l1a_watt_bf", 3072), ("l1a_wproj", "l1a_wproj_bf", 1024))):
        for rb in range(8):
            for ch in range(ncol // 1024):
                k.dma("y", stg[:, 0:1024],
                      W[srcn][rb * 128:(rb + 1) * 128, ch * 1024:(ch + 1) * 1024],
                      writes=["io_stg"])
                k.op("g", lambda e: e.tensor_copy(ws[:, 0:1024], stg[:, 0:1024]),
                     reads=["io_stg"], writes=["l1_ws0"])
                k.dma("g", W[dstn][rb * 128:(rb + 1) * 128,
                                   ch * 1024:(ch + 1) * 1024], ws[:, 0:1024],
                      reads=["l1_ws0"], writes=["d_" + dstn])
    k.sched.barrier()

    k.dma("g", bf32[0:1, :], W["l1a_batt"][0:1, 0:1024], writes=["l1_bf32"])
    k.dma("g", bf32[32:33, :], W["l1a_batt"][0:1, 1024:2048], writes=["l1_bf32"])
    k.dma("g", bf32[64:65, :], W["l1a_batt"][0:1, 2048:3072], writes=["l1_bf32"])
    for i in range(3):
        k.op("s", lambda e, i=i: e.copy(bbf[0:1, i * 1024:(i + 1) * 1024],
                                        bf32[32 * i:32 * i + 1, :]),
             reads=["l1_bf32"], writes=["l1_bbf"])
    for hh in range(2):
        k.dma("g", t3[0:1, :], W["l1a_bproj"][0:1, hh * 512:(hh + 1) * 512],
              reads=["row_t3"], writes=["row_t3"])
        for j in range(4):
            k.mm(psD[:, j:j + 1], t3[0:1, j * 128:(j + 1) * 128], orf[0:1, 0:1],
                 True, True, reads=["row_t3", "orf"], writes=["psD"])
        k.op("s", lambda e, hh=hh: e.copy(bpcol[:, hh * 4:hh * 4 + 4], psD[:, 0:4]),
             reads=["psD"], writes=["l1_bpcol"])
    k.sched.barrier()
    for ot in range(24):
        slot = ws[:, (ot % 2) * 1024:(ot % 2) * 1024 + 1024]
        stag = f"l1_ws{ot % 2}"
        wtile(W["l1a_watt_bf"][:, ot * 128:(ot + 1) * 128], slot, stag)
        for cb in range(8):
            k.mm(psR1[:, 0:128], oc[:], slot[:, cb * 128:(cb + 1) * 128],
                 start=cb == 0, stop=cb == 7, reads=["ones_col", stag],
                 writes=["psR1"])
        k.op("s", lambda e, ot=ot: e.activation(lf[0:1, ot * 128:(ot + 1) * 128],
                                                psR1[:, 0:128], AF.Copy, scale=-1.0),
             reads=["psR1"], writes=["l1_lf"])
    for b in range(BPC):
        bt = b * T
        for ot in range(8):
            slot = ws[:, (ot % 2) * 1024:(ot % 2) * 1024 + 1024]
            stag = f"l1_ws{ot % 2}"
            wtile(W["l1a_watt_bf"][:, 1024 + ot * 128:1024 + (ot + 1) * 128], slot, stag)
            for fc in range(4):
                f0 = bt + fc * 512
                ps = psA if fc % 2 == 0 else psB
                pn = "psA" if fc % 2 == 0 else "psB"
                for cb in range(DBLK):
                    k.mm(ps[:], slot[:, cb * 128:(cb + 1) * 128],
                         x[:, cb * TB + f0:cb * TB + f0 + 512],
                         start=cb == 0, stop=False, reads=[stag, f"x{cb}"],
                         writes=[pn])
                k.mm(ps[:], lf[0:1, 1024 + ot * 128:1024 + (ot + 1) * 128],
                     lnbf[0:1, f0:f0 + 512], start=False, stop=False,
                     reads=["l1_lf", "lnbf"], writes=[pn])
                k.mm(ps[:], bbf[0:1, 1024 + ot * 128:1024 + (ot + 1) * 128],
                     recipbf[0:1, f0:f0 + 512], start=False, stop=True,
                     reads=["l1_bbf", "recipbf"], writes=[pn])
                k.op("v", lambda e, ot=ot, fc=fc, ps=ps, f0=f0: e.tensor_mul(
                    kT[:, ot * T + fc * 512:ot * T + fc * 512 + 512],
                    ps[:], r_rep[:, f0:f0 + 512]),
                    reads=[pn, "r_rep"], writes=["l1_kT"])
        for vt in range(8):
            slot = ws[:, (vt % 2) * 1024:(vt % 2) * 1024 + 1024]
            stag = f"l1_ws{vt % 2}"
            wtile(W["l1a_watt_bf"][:, 2048 + vt * 128:2048 + (vt + 1) * 128], slot, stag)
            for tt in range(16):
                t0 = bt + tt * 128
                ps = psC if tt % 2 == 0 else psD
                pn = "psC" if tt % 2 == 0 else "psD"
                for cb in range(DBLK):
                    k.mm(ps[:, 0:128], x[:, cb * TB + t0:cb * TB + t0 + 128],
                         slot[:, cb * 128:(cb + 1) * 128],
                         start=cb == 0, stop=False, reads=[stag, f"x{cb}"],
                         writes=[pn])
                k.mm(ps[:, 0:128], lnbf[0:1, t0:t0 + 128],
                     lf[0:1, 2048 + vt * 128:2048 + (vt + 1) * 128],
                     start=False, stop=False, reads=["l1_lf", "lnbf"], writes=[pn])
                k.mm(ps[:, 0:128], recipbf[0:1, t0:t0 + 128],
                     bbf[0:1, 2048 + vt * 128:2048 + (vt + 1) * 128],
                     start=False, stop=True, reads=["l1_bbf", "recipbf"], writes=[pn])
                k.op("v", lambda e, tt=tt, vt=vt, ps=ps, b=b: e.tensor_scalar_mul(
                    vb[:, tt * 1024 + vt * 128:tt * 1024 + vt * 128 + 128],
                    ps[:, 0:128], rcol[:, b * 16 + tt:b * 16 + tt + 1]),
                    reads=[pn, "rcol"], writes=["l1_vb"])
        for qc in range(4):
            f0 = bt + qc * 512
            for ot in range(8):
                slot = ws[:, (ot % 2) * 1024:(ot % 2) * 1024 + 1024]
                stag = f"l1_ws{ot % 2}"
                wtile(W["l1a_watt_bf"][:, ot * 128:(ot + 1) * 128], slot, stag)
                ps = psA if ot % 2 == 0 else psB
                pn = "psA" if ot % 2 == 0 else "psB"
                for cb in range(DBLK):
                    k.mm(ps[:], slot[:, cb * 128:(cb + 1) * 128],
                         x[:, cb * TB + f0:cb * TB + f0 + 512],
                         start=cb == 0, stop=False, reads=[stag, f"x{cb}"],
                         writes=[pn])
                k.mm(ps[:], lf[0:1, ot * 128:(ot + 1) * 128],
                     lnbf[0:1, f0:f0 + 512], start=False, stop=False,
                     reads=["l1_lf", "lnbf"], writes=[pn])
                k.mm(ps[:], bbf[0:1, ot * 128:(ot + 1) * 128],
                     recipbf[0:1, f0:f0 + 512], start=False, stop=True,
                     reads=["l1_bbf", "recipbf"], writes=[pn])
                k.op("v", lambda e, ot=ot, ps=ps, f0=f0: e.tensor_mul(
                    qch[:, ot * 512:(ot + 1) * 512], ps[:], r_rep[:, f0:f0 + 512]),
                    reads=[pn, "r_rep"], writes=["l1_q"])
            kmax = qc * 4 + 4
            for kt in range(kmax):
                ps = psC if kt % 2 == 0 else psD
                pn = "psC" if kt % 2 == 0 else "psD"
                for cb in range(8):
                    k.mm(ps[:], kT[:, cb * T + kt * 128:cb * T + (kt + 1) * 128],
                         qch[:, cb * 512:(cb + 1) * 512],
                         start=cb == 0, stop=cb == 7,
                         reads=["l1_kT", "l1_q"], writes=[pn])
                delta = kt * 128 - qc * 512
                pdst = P[:, kt * 512:(kt + 1) * 512]
                if delta < 0:
                    k.op("s", lambda e, pdst=pdst, ps=ps: e.activation(
                        pdst, ps[:], AF.Exp, scale=SC),
                        reads=[pn], writes=["l1_P"])
                else:
                    pt = sq[:, 0:512]
                    k.op("s", lambda e, pt=pt, ps=ps: e.activation(
                        pt, ps[:], AF.Exp, scale=SC), reads=[pn], writes=["sq0"])
                    k.op("g", lambda e, pdst=pdst, pt=pt, delta=delta:
                         e.affine_select(pdst, pt, [[1, 512]], ALU.is_ge, 0.0,
                                         base=-delta, channel_multiplier=-1),
                         reads=["sq0"], writes=["l1_P"])
            for kt in range(kmax):
                k.mm(psR1[:], oc[:, 0:1],
                     P[:, kt * 512:(kt + 1) * 512], start=kt == 0, stop=kt == kmax - 1,
                     reads=["ones_col", "l1_P"], writes=["psR1"])
            k.op("v", lambda e: e.reciprocal(t1[:], psR1[:]),
                 reads=["psR1"], writes=["row_t1"])
            k.mm(psC[:], orf[:], t1[:], True, True,
                 reads=["orf", "row_t1"], writes=["psC"])
            k.op("s", lambda e: e.copy(srep[:], psC[:]), reads=["psC"],
                 writes=["l1_srep"])
            for ct in range(8):
                ps = psA if ct % 2 == 0 else psB
                pn = "psA" if ct % 2 == 0 else "psB"
                for kt in range(kmax):
                    k.mm(ps[:], vb[:, kt * 1024 + ct * 128:kt * 1024 + ct * 128 + 128],
                         P[:, kt * 512:(kt + 1) * 512],
                         start=kt == 0, stop=kt == kmax - 1,
                         reads=["l1_vb", "l1_P"], writes=[pn])
                k.op("v", lambda e, ct=ct, ps=ps: e.tensor_mul(
                    ybuf[:, ct * 512:(ct + 1) * 512], ps[:], srep[:]),
                    reads=[pn, "l1_srep"], writes=["l1_y"])
            for db in range(DBLK):
                slot = ws[:, (db % 2) * 1024:(db % 2) * 1024 + 1024]
                stag = f"l1_ws{db % 2}"
                wtile(W["l1a_wproj_bf"][:, db * 128:(db + 1) * 128], slot, stag)
                ps = psC if db % 2 == 0 else psD
                pn = "psC" if db % 2 == 0 else "psD"
                for cb in range(8):
                    k.mm(ps[:], slot[:, cb * 128:(cb + 1) * 128],
                         ybuf[:, cb * 512:(cb + 1) * 512],
                         start=cb == 0, stop=cb == 7, reads=[stag, "l1_y"], writes=[pn])
                xs = x[:, db * TB + f0:db * TB + f0 + 512]
                k.op("v", lambda e, xs=xs, ps=ps, db=db: e.scalar_tensor_tensor(
                    xs, ps[:], bpcol[:, db:db + 1], xs, ALU.add, ALU.add),
                    reads=[pn, "l1_bpcol"], writes=[f"x{db}"])

    k.sched.barrier()



def build_l1_ffn(k: K, W):
    x = k.tensors["x_sb"]
    r_rep = k.tensors["r_rep"]
    oc, orow, orf = k.tensors["ones_col"], k.tensors["ones_row"], k.tensors["orf"]
    sq = k.tensors["sq_t"]
    psA, psB, psC, psD = (k.tensors[n] for n in ("psA", "psB", "psC", "psD"))
    psR1 = k.tensors["psR1"]
    t3 = k.tensors["row_t3"]

    k.sb("recipbf", [1, TB], BF16)
    build_ln_rows(k)
    lnbf = k.tensors["lnbf"]
    recipbf = k.tensors["recipbf"]
    stg = k.tensors["io_stg"]
    ocf = k.tensors["ones_col_f"]
    t1 = k.tensors["row_t1"]
    t2 = k.tensors["row_t2"][0:1, :]

    h = k.sb("f1_h", [128, 8 * 2048], BF16)
    x32 = k.sb("f1_x32", [128, 8 * 256], F32)
    sq32 = k.sb("f1_sq32", [128, 256], F32)
    wr32 = k.sb("f1_wr32", [128, 128], F32)
    rcol32 = k.sb("f1_rcol32", [128, 4], F32)
    r32 = k.sb("f1_r32", [1, 64], F32)
    w2 = k.sb("f1_w2", [128, 8 * 1024], BF16)
    ws = k.sb("f1_ws", [128, 2 * 1024], BF16)
    b2bf = k.sb("f1_b2", [16, 1024], BF16)
    sel = k.sb("f1_sel", [16, 1024], BF16)
    ident = k.sb("ident_bf", [128, 128], BF16)
    gT = k.sb("f1_gT", [16, TB], BF16)
    logits = k.sb("f1_logits", [128, 32 * 16], F32)
    gwork = k.sb("f1_gw", [128, 64], F32)
    gb = k.sb("f1_gb", [128, 16], BF16)
    rowsA = k.sb("f1_rowsA", [1, 2048], BF16)
    rowsB = k.sb("f1_rowsB", [1, 1024], BF16)
    negone = k.sb("negone_row", [1, 128], BF16)
    k.op("v", lambda e: e.memset(negone[:], -1.0), writes=["negone_row"])
    k.op("v", lambda e: e.memset(ident[:], 1.0), writes=["ident_bf"])
    k.op("g", lambda e: e.affine_select(ident[:], ident[:], [[1, 128]],
                                        ALU.is_equal, 0.0, base=0,
                                        channel_multiplier=-1),
         reads=["ident_bf"], writes=["ident_bf"])

    def wtile(src_ap, slot, tag):
        st = stg[:, 0:1024]
        k.dma("y", st.rearrange("p (a c) -> p a c", a=8),
              src_ap.rearrange("(a p) c -> p a c", p=128), writes=["io_stg"])
        k.op("g", lambda e, slot=slot: e.tensor_copy(slot, st[:, 0:1024]),
             reads=["io_stg"], writes=[tag])

    for rb in range(8):
        k.dma("y", stg[:, 0:1024], W["l1f_w2s"][rb * 128:(rb + 1) * 128, :],
              writes=["io_stg"])
        k.op("g", lambda e, rb=rb: e.tensor_copy(w2[:, rb * 1024:(rb + 1) * 1024],
                                                 stg[:, 0:1024]),
             reads=["io_stg"], writes=["f1_w2"])
    k.dma("y", wr32[:].rearrange("p (a c) -> p a c", a=8),
          W["l1f_wr"].rearrange("(a p) c -> p a c", p=128), writes=["f1_wr32"])
    k.dma("y", stg[0:16, 0:1024], W["l1f_b2"][:, :], writes=["io_stg"])
    k.op("s", lambda e: e.copy(b2bf[:], stg[0:16, 0:1024]),
         reads=["io_stg"], writes=["f1_b2"])
    k.dma("y", stg[0:16, 0:1024], W["l1f_sel"][:, :], writes=["io_stg"])
    k.op("s", lambda e: e.copy(sel[:], stg[0:16, 0:1024]),
         reads=["io_stg"], writes=["f1_sel"])
    k.dma("y", stg[0:1, 0:1024], W["l1f_b1"][0:1, :], writes=["io_stg"])
    k.op("s", lambda e: e.copy(rowsA[0:1, 1024:2048], stg[0:1, 0:1024]),
         reads=["io_stg"], writes=["f1_rowsA"])
    k.dma("y", r32[0:1, 16:32], W["l1f_br"][0:1, :], writes=["f1_r32"])
    for cb in range(8):
        k.mm(psR1[:, 0:16], ocf[:], wr32[:, cb * 16:(cb + 1) * 16],
             start=cb == 0, stop=cb == 7, reads=["ones_col_f", "f1_wr32"],
             writes=["psR1"])
    k.op("s", lambda e: e.activation(r32[0:1, 0:16], psR1[:, 0:16],
                                     AF.Copy, scale=-1.0),
         reads=["psR1"], writes=["f1_r32"])
    k.sched.barrier()

    BIG = 1.0e9
    for fc in range(16):
        f0 = fc * 256
        for db in range(DBLK):
            k.dma("y", x32[:, db * 256:(db + 1) * 256],
                  W["x32"][db * 128:(db + 1) * 128, f0:f0 + 256],
                  writes=[f"f1_x32_{db}"])
        for db in range(DBLK):
            xsl = x32[:, db * 256:(db + 1) * 256]
            k.op("s", lambda e, xsl=xsl: e.activation(sq32[:], xsl, AF.Square),
                 reads=[f"f1_x32_{db}"], writes=["f1_sq32"])
            k.mm(psR1[:, 0:256], ocf[:], xsl, start=db == 0, stop=db == DBLK - 1,
                 reads=["ones_col_f", f"f1_x32_{db}"], writes=["psR1"])
            k.mm(k.tensors["psR2"][:, 0:256], ocf[:], sq32[:], start=db == 0,
                 stop=db == DBLK - 1, reads=["ones_col_f", "f1_sq32"],
                 writes=["psR2"])
        t1c, t3c, t2c = t1[0:1, 0:256], t3[0:1, 0:256], t2[0:1, 0:256]
        k.op("v", lambda e: e.tensor_scalar_mul(t1c, psR1[:, 0:256], 1.0 / D),
             reads=["psR1"], writes=["row_t1"])
        k.op("v", lambda e: e.tensor_scalar_mul(t3c, k.tensors["psR2"][:, 0:256],
                                                1.0 / D),
             reads=["psR2"], writes=["row_t3"])
        k.op("s", lambda e: e.activation(t2c, t1c, AF.Square),
             reads=["row_t1"], writes=["row_t2"])
        k.op("v", lambda e: e.tensor_sub(t3c, t3c, t2c),
             reads=["row_t3", "row_t2"], writes=["row_t3"])
        k.op("s", lambda e: e.activation(t2c, t3c, AF.Sqrt, bias=1e-5),
             reads=["row_t3"], writes=["row_t2"])
        k.op("s", lambda e: e.activation(t3c, t3c, AF.Ln, bias=1e-5),
             reads=["row_t3"], writes=["row_t3"])
        k.op("s", lambda e: e.activation(t3c, t3c, AF.Exp, scale=-0.5),
             reads=["row_t3"], writes=["row_t3"])
        for j in range(2):
            k.mm(psD[:, j:j + 1], t3[0:1, j * 128:(j + 1) * 128], ocf[0:1, 0:1],
                 True, True, reads=["row_t3", "ones_col_f"], writes=["psD"])
        k.op("s", lambda e: e.copy(rcol32[:, 0:2], psD[:, 0:2]),
             reads=["psD"], writes=["f1_rcol32"])
        for ti in range(2):
            tt = fc * 2 + ti
            t0l = ti * 128
            lt = logits[:, tt * 16:(tt + 1) * 16]
            pslt = psC[:, 0:16]
            for db in range(DBLK):
                k.mm(pslt, x32[:, db * 256 + t0l:db * 256 + t0l + 128],
                     wr32[:, db * 16:(db + 1) * 16],
                     start=db == 0, stop=False,
                     reads=[f"f1_x32_{db}", "f1_wr32"], writes=["psC"])
            k.mm(pslt, t1[0:1, t0l:t0l + 128], r32[0:1, 0:16],
                 start=False, stop=False, reads=["row_t1", "f1_r32"], writes=["psC"])
            k.mm(pslt, t2[0:1, t0l:t0l + 128], r32[0:1, 16:32],
                 start=False, stop=True, reads=["row_t2", "f1_r32"], writes=["psC"])
            k.op("v", lambda e, lt=lt, ti=ti: e.tensor_scalar_mul(
                lt, psC[:, 0:16], rcol32[:, ti:ti + 1]),
                reads=["psC", "f1_rcol32"], writes=["f1_logits"])
    for tt in range(32):
        t0 = tt * 128
        lt = logits[:, tt * 16:(tt + 1) * 16]
        k.op("v", lambda e, lt=lt: e.tensor_reduce(gwork[:, 0:1], lt, AX, ALU.max),
             reads=["f1_logits"], writes=["f1_gw"])
        k.op("v", lambda e, lt=lt: e.tensor_scalar(gwork[:, 16:32], lt,
                                                   gwork[:, 0:1], None, ALU.is_ge),
             reads=["f1_logits", "f1_gw"], writes=["f1_gw"])
        k.op("v", lambda e, lt=lt: e.scalar_tensor_tensor(
            gwork[:, 32:48], gwork[:, 16:32], -BIG, lt, ALU.mult, ALU.add),
            reads=["f1_logits", "f1_gw"], writes=["f1_gw"])
        k.op("v", lambda e: e.tensor_reduce(gwork[:, 1:2], gwork[:, 32:48],
                                            AX, ALU.max),
             reads=["f1_gw"], writes=["f1_gw"])
        k.op("v", lambda e: e.tensor_scalar(gwork[:, 48:64], gwork[:, 32:48],
                                            gwork[:, 1:2], None, ALU.is_ge),
             reads=["f1_gw"], writes=["f1_gw"])
        k.op("v", lambda e: e.tensor_add(gb[:], gwork[:, 16:32], gwork[:, 48:64]),
             reads=["f1_gw"], writes=["f1_gb"])
        k.mm(psD[0:16, 0:128], gb[:], ident[:], True, True,
             reads=["f1_gb", "ident_bf"], writes=["psD"])
        k.op("s", lambda e, t0=t0: e.copy(gT[:, t0:t0 + 128], psD[0:16, 0:128]),
             reads=["psD"], writes=["f1_gT"])

    for half in range(2):
        bt = half * 2048
        for ehb in range(8):
            slot = ws[:, (ehb % 2) * 1024:(ehb % 2) * 1024 + 1024]
            stag = f"f1_ws{ehb % 2}"
            wtile(W["l1f_w1s"][:, ehb * 128:(ehb + 1) * 128], slot, stag)
            if half == 0:
                for cb in range(8):
                    k.mm(psR1[:, 0:128], oc[:], slot[:, cb * 128:(cb + 1) * 128],
                         start=cb == 0, stop=cb == 7, reads=["ones_col", stag],
                         writes=["psR1"])
                k.op("s", lambda e, ehb=ehb: e.activation(
                    rowsA[0:1, ehb * 128:(ehb + 1) * 128], psR1[:, 0:128],
                    AF.Copy, scale=-1.0), reads=["psR1"], writes=["f1_rowsA"])
            for fcl in range(4):
                f0 = bt + fcl * 512
                ps = psA if fcl % 2 == 0 else psB
                pn = "psA" if fcl % 2 == 0 else "psB"
                for cb in range(DBLK):
                    k.mm(ps[:], slot[:, cb * 128:(cb + 1) * 128],
                         x[:, cb * TB + f0:cb * TB + f0 + 512],
                         start=cb == 0, stop=False, reads=[stag, f"x{cb}"],
                         writes=[pn])
                k.mm(ps[:], rowsA[0:1, ehb * 128:(ehb + 1) * 128],
                     lnbf[0:1, f0:f0 + 512], start=False, stop=False,
                     reads=["f1_rowsA", "lnbf"], writes=[pn])
                k.mm(ps[:], rowsA[0:1, 1024 + ehb * 128:1024 + (ehb + 1) * 128],
                     recipbf[0:1, f0:f0 + 512], start=False, stop=True,
                     reads=["f1_rowsA", "recipbf"], writes=[pn])
                hs = h[:, ehb * 2048 + fcl * 512:ehb * 2048 + fcl * 512 + 512]
                hn = f"f1_h{ehb}"
                k.op("v", lambda e, hs=hs, ps=ps, f0=f0: e.scalar_tensor_tensor(
                    hs, ps[:], 0.0, r_rep[:, f0:f0 + 512], ALU.max, ALU.mult),
                    reads=[pn, "r_rep"], writes=[hn])
                psg = psC if fcl % 2 == 0 else psD
                pgn = "psC" if fcl % 2 == 0 else "psD"
                k.mm(psg[:], sel[:, ehb * 128:(ehb + 1) * 128], gT[:, f0:f0 + 512],
                     True, True, reads=["f1_sel", "f1_gT"], writes=[pgn])
                k.op("v", lambda e, hs=hs, psg=psg: e.tensor_mul(hs, psg[:], hs),
                     reads=[pgn, hn], writes=[hn])

        for fcl in range(4):
            f0 = bt + fcl * 512
            k.op("v", lambda e, f0=f0: e.tensor_mul(
                rowsB[0:1, 512:1024], lnbf[0:1, f0:f0 + 512],
                r_rep[0:1, f0:f0 + 512]),
                reads=["lnbf", "r_rep"], writes=["f1_rowsB"])
            for db in range(DBLK):
                ps = psA if db % 2 == 0 else psB
                pn = "psA" if db % 2 == 0 else "psB"
                for ehb in range(8):
                    k.mm(ps[:],
                         w2[:, ehb * 1024 + db * 128:ehb * 1024 + (db + 1) * 128],
                         h[:, ehb * 2048 + fcl * 512:ehb * 2048 + fcl * 512 + 512],
                         start=ehb == 0, stop=False,
                         reads=["f1_w2", f"f1_h{ehb}"], writes=[pn])
                k.mm(ps[:], b2bf[:, db * 128:(db + 1) * 128], gT[:, f0:f0 + 512],
                     start=False, stop=False, reads=["f1_b2", "f1_gT"], writes=[pn])
                k.mm(ps[:], negone[:], rowsB[0:1, 512:1024],
                     start=False, stop=True, reads=["negone_row", "f1_rowsB"],
                     writes=[pn])
                tmpa = sq[:, 0:512]
                k.op("g", lambda e, tmpa=tmpa, f0=f0: e.tensor_scalar(
                    tmpa, r_rep[:, f0:f0 + 512], 1.0, None, ALU.add),
                    reads=["r_rep"], writes=["sq0"])
                xs = x[:, db * TB + f0:db * TB + f0 + 512]
                tmpb = stg[:, 0:512]
                k.op("v", lambda e, tmpb=tmpb, xs=xs, tmpa=tmpa: e.tensor_mul(
                    tmpb, xs, tmpa), reads=[f"x{db}", "sq0"], writes=["io_stg"])
                k.op("v", lambda e, xs=xs, ps=ps, tmpb=tmpb: e.tensor_add(
                    xs, ps[:], tmpb), reads=[pn, "io_stg"], writes=[f"x{db}"])

    k.sched.barrier()



def build_l2_attn(k: K, W):
    x = k.tensors["x_sb"]
    r_rep = k.tensors["r_rep"]
    oc, orow, orf = k.tensors["ones_col"], k.tensors["ones_row"], k.tensors["orf"]
    sq = k.tensors["sq_t"]
    psA, psB, psC, psD = (k.tensors[n] for n in ("psA", "psB", "psC", "psD"))
    psE1 = k.tensors["psE1"]
    psR1 = k.tensors["psR1"]
    t1 = k.tensors["row_t1"]
    t3 = k.tensors["row_t3"]

    k.sb("recipbf", [1, TB], BF16)
    k.sb("rcol", [128, 32], F32)
    build_ln_rows(k)
    lnbf = k.tensors["lnbf"]
    recipbf = k.tensors["recipbf"]
    rcol = k.tensors["rcol"]
    stg = k.tensors["io_stg"]

    kT = k.sb("l2_kT", [128, 8 * T], BF16)
    vb = k.sb("l2_v", [128, 16 * 1040], BF16)
    qch = k.sb("l2_q", [128, 8 * 512], BF16)
    P = k.sb("l2_p", [128, 2 * 512], BF16)
    ybuf = qch
    ws = k.sb("l2_ws", [128, 2 * 1024], BF16)
    lf = k.sb("l2_lf", [1, 3072], BF16)
    bbf = k.sb("l2_bb", [1, 3072], BF16)
    bpcol = k.sb("l2_bpcol", [128, 8], F32)
    srep = k.sb("l2_srep", [128, 512], F32)
    sc_t = k.sb("l2_sincos", [128, 2 * 2048], BF16)
    rt = k.sb("l2_rt", [128, 512], F32)
    swp = k.sb("l2_swp", [128, 512], F32)
    SC = 1.0 / 8.0

    def wtile(src_ap, slot, tag):
        k.dma("y", slot.rearrange("p (a c) -> p a c", a=8),
              src_ap.rearrange("(a p) c -> p a c", p=128), writes=[tag])

    for srcn, dstn, ncol in (("l2a_wqkvT", "l2a_wqkv_bf", 3072),
                             ("l2a_woutT", "l2a_wout_bf", 1024)):
        for rb in range(8):
            for ch in range(ncol // 1024):
                k.dma("y", stg[:, 0:1024],
                      W[srcn][rb * 128:(rb + 1) * 128, ch * 1024:(ch + 1) * 1024],
                      writes=["io_stg"])
                k.op("g", lambda e: e.tensor_copy(ws[:, 0:1024], stg[:, 0:1024]),
                     reads=["io_stg"], writes=["l2_ws0"])
                k.dma("g", W[dstn][rb * 128:(rb + 1) * 128,
                                   ch * 1024:(ch + 1) * 1024], ws[:, 0:1024],
                      reads=["l2_ws0"], writes=["d_" + dstn])
    k.sched.barrier()

    for j, nm in enumerate(("l2a_cos4", "l2a_ssin4")):
        for hh in range(2):
            k.dma("y", stg[:, 0:1024], W[nm][:, hh * 1024:(hh + 1) * 1024],
                  writes=["io_stg"])
            k.op("s", lambda e, j=j, hh=hh: e.copy(
                sc_t[:, j * 2048 + hh * 1024:j * 2048 + (hh + 1) * 1024],
                stg[:, 0:1024]),
                reads=["io_stg"], writes=["l2_sc"])
    for i in range(3):
        k.dma("y", stg[0:1, 0:1024], W["l2a_bqkv"][0:1, i * 1024:(i + 1) * 1024],
              writes=["io_stg"])
        k.op("s", lambda e, i=i: e.copy(bbf[0:1, i * 1024:(i + 1) * 1024],
                                        stg[0:1, 0:1024]),
             reads=["io_stg"], writes=["l2_bbf"])
    for hh in range(2):
        k.dma("y", t3[0:1, :], W["l2a_bout"][0:1, hh * 512:(hh + 1) * 512],
              reads=["row_t3"], writes=["row_t3"])
        for j in range(4):
            k.mm(psD[:, j:j + 1], t3[0:1, j * 128:(j + 1) * 128], orf[0:1, 0:1],
                 True, True, reads=["row_t3", "orf"], writes=["psD"])
        k.op("s", lambda e, hh=hh: e.copy(bpcol[:, hh * 4:hh * 4 + 4], psD[:, 0:4]),
             reads=["psD"], writes=["l2_bpcol"])
    for ot in range(24):
        slot = ws[:, (ot % 2) * 1024:(ot % 2) * 1024 + 1024]
        stag = f"l2_ws{ot % 2}"
        wtile(W["l2a_wqkv_bf"][:, ot * 128:(ot + 1) * 128], slot, stag)
        for cb in range(8):
            k.mm(psR1[:, 0:128], oc[:], slot[:, cb * 128:(cb + 1) * 128],
                 start=cb == 0, stop=cb == 7, reads=["ones_col", stag],
                 writes=["psR1"])
        k.op("s", lambda e, ot=ot: e.activation(lf[0:1, ot * 128:(ot + 1) * 128],
                                                psR1[:, 0:128], AF.Copy, scale=-1.0),
             reads=["psR1"], writes=["l2_lf"])
    k.op("v", lambda e: e.memset(
        vb[:].rearrange("p (a c) -> p a c", c=65)[:, :, 64:65], 1.0),
        writes=["l2_vones"])
    k.sched.barrier()

    def qk_proj(sec, ot, f0, dst, dstn):
        slot = ws[:, (ot % 2) * 1024:(ot % 2) * 1024 + 1024]
        stag = f"l2_ws{ot % 2}"
        wcol = sec * 1024 + ot * 128
        wtile(W["l2a_wqkv_bf"][:, wcol:wcol + 128], slot, stag)
        ps = psA if ot % 2 == 0 else psB
        pn = "psA" if ot % 2 == 0 else "psB"
        for cb in range(DBLK):
            k.mm(ps[:], slot[:, cb * 128:(cb + 1) * 128],
                 x[:, cb * TB + f0:cb * TB + f0 + 512],
                 start=cb == 0, stop=False, reads=[stag, f"x{cb}"], writes=[pn])
        k.mm(ps[:], lf[0:1, wcol:wcol + 128], lnbf[0:1, f0:f0 + 512],
             start=False, stop=False, reads=["l2_lf", "lnbf"], writes=[pn])
        k.mm(ps[:], bbf[0:1, wcol:wcol + 128], recipbf[0:1, f0:f0 + 512],
             start=False, stop=True, reads=["l2_bbf", "recipbf"], writes=[pn])
        k.op("v", lambda e, ps=ps, f0=f0: e.tensor_mul(rt[:], ps[:],
                                                       r_rep[:, f0:f0 + 512]),
             reads=[pn, "r_rep"], writes=["l2_rt"])
        tl = f0 % T
        for j in range(4):
            sb_, db_ = 32 * (j ^ 1), 32 * j
            k.op("v", lambda e, sb_=sb_, db_=db_: e.tensor_copy(
                swp[db_:db_ + 32, :], rt[sb_:sb_ + 32, :]),
                reads=["l2_rt"], writes=["l2_swp"])
        k.op("v", lambda e, tl=tl: e.tensor_mul(
            swp[:], swp[:], sc_t[:, 2048 + tl:2048 + tl + 512]),
            reads=["l2_swp", "l2_sc"], writes=["l2_swp"])
        k.op("v", lambda e, tl=tl: e.tensor_mul(
            rt[:], rt[:], sc_t[:, tl:tl + 512]),
            reads=["l2_rt", "l2_sc"], writes=["l2_rt"])
        k.op("v", lambda e, dst=dst: e.tensor_add(dst, rt[:], swp[:]),
             reads=["l2_rt", "l2_swp"], writes=[dstn])

    for b in range(BPC):
        bt = b * T
        for ot in range(8):
            h0c = ot
            for fc in range(4):
                f0 = bt + fc * 512
                tl = fc * 512
                qk_proj(1, ot, f0, kT[:, h0c * T + tl:h0c * T + tl + 512],
                        "l2_kT")
        for vt in range(8):
            slot = ws[:, (vt % 2) * 1024:(vt % 2) * 1024 + 1024]
            stag = f"l2_ws{vt % 2}"
            wtile(W["l2a_wqkv_bf"][:, 2048 + vt * 128:2048 + (vt + 1) * 128],
                  slot, stag)
            for tt in range(16):
                t0 = bt + tt * 128
                ps = psC if tt % 2 == 0 else psD
                pn = "psC" if tt % 2 == 0 else "psD"
                for cb in range(DBLK):
                    k.mm(ps[:, 0:128], x[:, cb * TB + t0:cb * TB + t0 + 128],
                         slot[:, cb * 128:(cb + 1) * 128],
                         start=cb == 0, stop=False, reads=[stag, f"x{cb}"],
                         writes=[pn])
                k.mm(ps[:, 0:128], lnbf[0:1, t0:t0 + 128],
                     lf[0:1, 2048 + vt * 128:2048 + (vt + 1) * 128],
                     start=False, stop=False, reads=["l2_lf", "lnbf"], writes=[pn])
                k.mm(ps[:, 0:128], recipbf[0:1, t0:t0 + 128],
                     bbf[0:1, 2048 + vt * 128:2048 + (vt + 1) * 128],
                     start=False, stop=True, reads=["l2_bbf", "recipbf"],
                     writes=[pn])
                h0 = vt * 2
                for j in range(2):
                    k.op("v", lambda e, tt=tt, h0=h0, j=j, ps=ps, b=b:
                         e.tensor_scalar_mul(
                             vb[:, tt * 1040 + (h0 + j) * 65:
                                tt * 1040 + (h0 + j) * 65 + 64],
                             ps[:, j * 64:j * 64 + 64],
                             rcol[:, b * 16 + tt:b * 16 + tt + 1]),
                         reads=[pn, "rcol"], writes=["l2_vb"])
        for qc in range(4):
            f0 = bt + qc * 512
            tl = qc * 512
            for ot in range(8):
                h0c = ot
                qk_proj(0, ot, f0, qch[:, h0c * 512:h0c * 512 + 512], "l2_q")
            for h in range(16):
                cbk = h // 2
                rb = 64 * (h % 2)
                kmax = qc * 4 + 4
                for kt in range(kmax):
                    ps = psC if kt % 2 == 0 else psD
                    pn = "psC" if kt % 2 == 0 else "psD"
                    k.mm(ps[:], kT[rb:rb + 64, cbk * T + kt * 128:
                                   cbk * T + (kt + 1) * 128],
                         qch[rb:rb + 64, cbk * 512:(cbk + 1) * 512],
                         True, True, reads=["l2_kT", "l2_q"], writes=[pn])
                    delta = kt * 128 - qc * 512
                    pdst = P[:, (kt % 2) * 512:(kt % 2) * 512 + 512]
                    pdn = f"l2_P{kt % 2}"
                    if delta < 0:
                        k.op("s", lambda e, pdst=pdst, ps=ps: e.activation(
                            pdst, ps[:], AF.Exp, scale=SC),
                            reads=[pn], writes=[pdn])
                    else:
                        pt = sq[:, 512:1024]
                        k.op("s", lambda e, pt=pt, ps=ps: e.activation(
                            pt, ps[:], AF.Exp, scale=SC), reads=[pn],
                            writes=["sq1"])
                        k.op("g", lambda e, pdst=pdst, pt=pt, delta=delta:
                             e.affine_select(pdst, pt, [[1, 512]], ALU.is_ge, 0.0,
                                             base=-delta, channel_multiplier=-1),
                             reads=["sq1"], writes=[pdn])
                    k.mm(psE1[0:65, :], vb[:, kt * 1040 + h * 65:kt * 1040 + h * 65 + 65],
                         pdst, start=kt == 0, stop=kt == kmax - 1,
                         reads=["l2_vb", "l2_vones", pdn], writes=["psE1"])
                k.op("v", lambda e: e.reciprocal(t1[:], psE1[64:65, :]),
                     reads=["psE1"], writes=["row_t1"])
                k.mm(psC[0:64, :], orf[0:1, 0:64], t1[:], True, True,
                     reads=["orf", "row_t1"], writes=["psC"])
                k.op("s", lambda e: e.copy(srep[0:64, :], psC[0:64, :]),
                     reads=["psC"], writes=["l2_srep"])
                k.op("v", lambda e, cbk=cbk, rb=rb: e.tensor_mul(
                    ybuf[rb:rb + 64, cbk * 512:(cbk + 1) * 512],
                    psE1[0:64, :], srep[0:64, :]),
                    reads=["psE1", "l2_srep"], writes=["l2_y"])
            for db in range(DBLK):
                slot = ws[:, (db % 2) * 1024:(db % 2) * 1024 + 1024]
                stag = f"l2_ws{db % 2}"
                wtile(W["l2a_wout_bf"][:, db * 128:(db + 1) * 128], slot, stag)
                ps = psC if db % 2 == 0 else psD
                pn = "psC" if db % 2 == 0 else "psD"
                for cb in range(8):
                    k.mm(ps[:], slot[:, cb * 128:(cb + 1) * 128],
                         ybuf[:, cb * 512:(cb + 1) * 512],
                         start=cb == 0, stop=cb == 7, reads=[stag, "l2_y"],
                         writes=[pn])
                xs = x[:, db * TB + f0:db * TB + f0 + 512]
                k.op("v", lambda e, xs=xs, ps=ps, db=db: e.scalar_tensor_tensor(
                    xs, ps[:], bpcol[:, db:db + 1], xs, ALU.add, ALU.add),
                    reads=[pn, "l2_bpcol"], writes=[f"x{db}"])

    k.sched.barrier()



def build_l2_ffn(k: K, W):
    x = k.tensors["x_sb"]
    r_rep = k.tensors["r_rep"]
    oc, orow, orf = k.tensors["ones_col"], k.tensors["ones_row"], k.tensors["orf"]
    ocf = k.tensors["ones_col_f"]
    sq = k.tensors["sq_t"]
    psA, psB, psC, psD = (k.tensors[n] for n in ("psA", "psB", "psC", "psD"))
    psR1, psR2 = k.tensors["psR1"], k.tensors["psR2"]
    t3 = k.tensors["row_t3"]

    build_ln_rows(k)
    lnbf = k.tensors["lnbf"]
    stg = k.tensors["io_stg"]

    ident = k.sb("ident_bf", [128, 128], BF16)
    k.op("v", lambda e: e.memset(ident[:], 1.0), writes=["ident_bf"])
    k.op("g", lambda e: e.affine_select(ident[:], ident[:], [[1, 128]],
                                        ALU.is_equal, 0.0, base=0,
                                        channel_multiplier=-1),
         reads=["ident_bf"], writes=["ident_bf"])

    xln = k.sb("f2_xln", [128, 8 * T], BF16)
    wsl = k.sb("f2_ws", [128, 2 * 1024], BF16)
    hbuf = k.sb("f2_h", [64, T], BF16)
    abuf = k.sb("f2_a", [128, 8 * 512], BF16)
    dsel = k.sb("f2_dsel", [128, 2 * 512], BF16)
    usel = k.sb("f2_usel", [64, 2 * 1024], BF16)
    diag = k.sb("f2_diag", [128, 2 * 128], BF16)
    mcol = k.sb("f2_mcol", [128, 16], F32)
    dvals = k.sb("f2_dv", [128, 256], F32)
    ohcol = k.sb("f2_ohc", [128, 16], F32)
    ohc8 = k.sb("f2_oh8", [8, 2], F32)
    ohc8b = k.sb("f2_oh8b", [8, 2], BF16)
    lrow = k.sb("f2_lrow", [1, 64], F32)
    brows = k.sb("f2_brows", [1, 2 * 2112], BF16)

    k.dma("y", stg[:, 0:64].rearrange("p (a c) -> p a c", a=8),
          W["l2f_sigT"].rearrange("(a p) c -> p a c", p=128), writes=["io_stg"])
    k.op("v", lambda e: e.tensor_mul(dvals[:, 192:256], stg[:, 0:64], stg[:, 0:64]),
         reads=["io_stg"], writes=["f2_dv"])
    k.op("v", lambda e: e.reciprocal(dvals[:, 0:64], dvals[:, 192:256]),
         reads=["f2_dv"], writes=["f2_dv"])
    k.dma("y", stg[:, 0:64].rearrange("p (a c) -> p a c", a=8),
          W["l2f_muT"].rearrange("(a p) c -> p a c", p=128), writes=["io_stg"])
    k.op("v", lambda e: e.tensor_mul(dvals[:, 64:128], stg[:, 0:64], dvals[:, 0:64]),
         reads=["io_stg", "f2_dv"], writes=["f2_dv"])
    k.op("v", lambda e: e.tensor_mul(dvals[:, 128:192], stg[:, 0:64],
                                     dvals[:, 64:128]),
         reads=["io_stg", "f2_dv"], writes=["f2_dv"])
    for db in range(8):
        k.mm(psR1[:, 0:8], ocf[:], dvals[:, 128 + db * 8:128 + (db + 1) * 8],
             start=db == 0, stop=db == 7, reads=["ones_col_f", "f2_dv"],
             writes=["psR1"])
    k.op("s", lambda e: e.copy(lrow[0:1, 0:8], psR1[:, 0:8]),
         reads=["psR1"], writes=["f2_lrow"])

    for b in range(BPC):
        bt = b * T
        k.op("v", lambda e: e.memset(mcol[:], 0.0), writes=["f2_mcol"])
        for db in range(DBLK):
            for fc in range(4):
                f0 = bt + fc * 512
                k.op("v", lambda e, f0=f0: e.tensor_mul(
                    t3[:], lnbf[0:1, f0:f0 + 512], r_rep[0:1, f0:f0 + 512]),
                    reads=["lnbf", "r_rep"], writes=["row_t3"])
                k.mm(psC[:], orf[:], t3[:], True, True,
                     reads=["orf", "row_t3"], writes=["psC"])
                tmp = sq[:, 0:512]
                k.op("v", lambda e, f0=f0, tmp=tmp, db=db: e.tensor_mul(
                    tmp, x[:, db * TB + f0:db * TB + f0 + 512],
                    r_rep[:, f0:f0 + 512]),
                    reads=[f"x{db}", "r_rep"], writes=["sq0"])
                k.op("v", lambda e, tmp=tmp: e.tensor_sub(tmp, tmp, psC[:]),
                     reads=["sq0", "psC"], writes=["sq0"])
                k.op("v", lambda e, tmp=tmp: e.tensor_reduce(
                    dvals[:, 248:249], tmp, AX, ALU.add),
                    reads=["sq0"], writes=["f2_dv249"])
                k.op("v", lambda e, db=db: e.tensor_add(
                    mcol[:, 2 * db + 1:2 * db + 2], mcol[:, 2 * db + 1:2 * db + 2],
                    dvals[:, 248:249]),
                    reads=["f2_dv249"], writes=["f2_mcol"])
                sq2 = sq[:, 512:1024]
                k.op("s", lambda e, tmp=tmp, sq2=sq2: e.activation(
                    sq2, tmp, AF.Square), reads=["sq0"], writes=["sq1"])
                k.op("v", lambda e, sq2=sq2: e.tensor_reduce(
                    dvals[:, 249:250], sq2, AX, ALU.add),
                    reads=["sq1"], writes=["f2_dv249"])
                k.op("v", lambda e, db=db: e.tensor_add(
                    mcol[:, 2 * db:2 * db + 1], mcol[:, 2 * db:2 * db + 1],
                    dvals[:, 249:250]),
                    reads=["f2_dv249"], writes=["f2_mcol"])
        k.op("v", lambda e: e.tensor_scalar_mul(mcol[:], mcol[:], 1.0 / T),
             reads=["f2_mcol"], writes=["f2_mcol"])
        for db in range(8):
            k.mm(psR1[:, 0:8], mcol[:, 2 * db:2 * db + 1],
                 dvals[:, db * 8:(db + 1) * 8],
                 start=db == 0, stop=db == 7, reads=["f2_mcol", "f2_dv"],
                 writes=["psR1"])
            k.mm(psR2[:, 0:8], mcol[:, 2 * db + 1:2 * db + 2],
                 dvals[:, 64 + db * 8:64 + (db + 1) * 8],
                 start=db == 0, stop=db == 7, reads=["f2_mcol", "f2_dv"],
                 writes=["psR2"])
        k.op("v", lambda e: e.scalar_tensor_tensor(
            lrow[0:1, 8:16], psR2[:, 0:8], -2.0, lrow[0:1, 0:8],
            ALU.mult, ALU.add),
            reads=["psR2", "f2_lrow"], writes=["f2_lrow"])
        k.op("v", lambda e: e.tensor_add(lrow[0:1, 16:24], lrow[0:1, 8:16],
                                         psR1[:, 0:8]),
             reads=["psR1", "f2_lrow"], writes=["f2_lrow"])
        k.op("v", lambda e: e.tensor_scalar_mul(lrow[0:1, 16:24],
                                                lrow[0:1, 16:24], 1.0 / D),
             reads=["f2_lrow"], writes=["f2_lrow"])
        k.op("v", lambda e: e.tensor_reduce(lrow[0:1, 32:33], lrow[0:1, 16:24],
                                            AX, ALU.min),
             reads=["f2_lrow"], writes=["f2_lrow"])
        k.op("v", lambda e: e.tensor_scalar(lrow[0:1, 40:48], lrow[0:1, 16:24],
                                            lrow[0:1, 32:33], None, ALU.is_le),
             reads=["f2_lrow"], writes=["f2_lrow"])
        for e_ in range(8):
            k.mm(psC[:, 0:1], orf[:], lrow[0:1, 40 + e_:41 + e_], True, True,
                 reads=["orf", "f2_lrow"], writes=["psC"])
            k.op("s", lambda e, e_=e_, b=b: e.copy(
                ohcol[:, b * 8 + e_:b * 8 + e_ + 1], psC[:, 0:1]),
                reads=["psC"], writes=["f2_ohc"])
        k.mm(psC[0:8, 0:1], lrow[0:1, 40:48], orf[0:1, 0:1], True, True,
             reads=["f2_lrow", "orf"], writes=["psC"])
        k.op("s", lambda e, b=b: e.copy(ohc8[:, b:b + 1], psC[0:8, 0:1]),
             reads=["psC"], writes=["f2_oh8"])
        k.op("s", lambda e, b=b: e.copy(ohc8b[:, b:b + 1], psC[0:8, 0:1]),
             reads=["psC"], writes=["f2_oh8b"])
        for e_ in range(8):
            k.op("v", lambda e, e_=e_, b=b: e.tensor_scalar_mul(
                diag[:, 0:128], ident[:], ohcol[:, b * 8 + e_:b * 8 + e_ + 1]),
                reads=["ident_bf", "f2_ohc"], writes=["f2_diag"])
            k.dma("y", stg[:, 0:512].rearrange("p (a c) -> p a c", a=8),
                  W["l2f_downWT"].rearrange("(a p) c -> p a c", p=128)
                  [:, :, e_ * 64:(e_ + 1) * 64], writes=["io_stg"])
            k.op("g", lambda e: e.tensor_copy(wsl[:, 0:512], stg[:, 0:512]),
                 reads=["io_stg"], writes=["f2_ws0"])
            for db in range(8):
                k.mm(psA[:, db * 64:(db + 1) * 64], diag[:, 0:128],
                     wsl[:, db * 64:(db + 1) * 64],
                     start=e_ == 0, stop=e_ == 7,
                     reads=["f2_diag", "f2_ws0"], writes=["psA"])
            k.dma("y", stg[0:64, 0:1024],
                  W["l2f_upWs"][e_ * 64:(e_ + 1) * 64, :], writes=["io_stg"])
            k.op("g", lambda e: e.tensor_copy(wsl[0:64, 1024:2048],
                                              stg[0:64, 0:1024]),
                 reads=["io_stg"], writes=["f2_ws1"])
            for ch in range(2):
                ps_u = psB if ch == 0 else psD
                pn_u = "psB" if ch == 0 else "psD"
                k.mm(ps_u[0:64, :], diag[0:64, 0:64],
                     wsl[0:64, 1024 + ch * 512:1024 + (ch + 1) * 512],
                     start=e_ == 0, stop=e_ == 7,
                     reads=["f2_diag", "f2_ws1"], writes=[pn_u])
        k.op("s", lambda e, b=b: e.copy(dsel[:, b * 512:(b + 1) * 512], psA[:]),
             reads=["psA"], writes=["f2_dsel"])
        k.op("s", lambda e, b=b: e.copy(usel[:, b * 1024:b * 1024 + 512],
                                        psB[0:64, :]),
             reads=["psB"], writes=["f2_usel"])
        k.op("s", lambda e, b=b: e.copy(usel[:, b * 1024 + 512:b * 1024 + 1024],
                                        psD[0:64, :]),
             reads=["psD"], writes=["f2_usel"])
        BO = b * 2112
        k.dma("y", stg[0:8, 0:64], W["l2f_down_b"][:, :], writes=["io_stg"])
        k.op("s", lambda e: e.copy(wsl[0:8, 0:64], stg[0:8, 0:64]),
             reads=["io_stg"], writes=["f2_wsb"])
        k.mm(psC[0:1, 0:64], ohc8b[:, b:b + 1], wsl[0:8, 0:64], True, True,
             reads=["f2_oh8b", "f2_wsb"], writes=["psC"])
        k.op("s", lambda e, BO=BO: e.copy(brows[0:1, BO:BO + 64], psC[0:1, 0:64]),
             reads=["psC"], writes=["f2_brows"])
        k.dma("y", stg[0:8, 0:1024], W["l2f_up_b"][:, :], writes=["io_stg"])
        k.op("s", lambda e: e.copy(wsl[0:8, 0:1024], stg[0:8, 0:1024]),
             reads=["io_stg"], writes=["f2_wsb"])
        for ch in range(2):
            ps_u = psC if ch == 0 else psD
            pn_u = "psC" if ch == 0 else "psD"
            k.mm(ps_u[0:1, :], ohc8b[:, b:b + 1], wsl[0:8, ch * 512:(ch + 1) * 512],
                 True, True, reads=["f2_oh8b", "f2_wsb"], writes=[pn_u])
            k.op("s", lambda e, BO=BO, ch=ch, ps_u=ps_u: e.copy(
                brows[0:1, BO + 64 + ch * 512:BO + 64 + (ch + 1) * 512],
                ps_u[0:1, :]),
                reads=[pn_u], writes=["f2_brows"])
        k.dma("y", stg[0:8, 0:1024], W["l2f_conv_b"][:, :], writes=["io_stg"])
        k.dma("y", stg[32:33, 0:1024], W["l2f_base_b"][0:1, :], writes=["io_stgb"])
        k.op("s", lambda e: e.copy(wsl[0:8, 0:1024], stg[0:8, 0:1024]),
             reads=["io_stg"], writes=["f2_wsb"])
        for ch in range(2):
            ps_u = psC if ch == 0 else psD
            pn_u = "psC" if ch == 0 else "psD"
            k.mm(ps_u[0:1, :], ohc8b[:, b:b + 1], wsl[0:8, ch * 512:(ch + 1) * 512],
                 True, True, reads=["f2_oh8b", "f2_wsb"], writes=[pn_u])
            k.op("v", lambda e, ch=ch, ps_u=ps_u: e.tensor_add(
                stg[64:65, ch * 512:(ch + 1) * 512],
                stg[32:33, ch * 512:(ch + 1) * 512], ps_u[0:1, :]),
                reads=[pn_u, "io_stgb"], writes=["io_stgc"])
            k.op("s", lambda e, BO=BO, ch=ch: e.copy(
                brows[0:1, BO + 1088 + ch * 512:BO + 1088 + (ch + 1) * 512],
                stg[64:65, ch * 512:(ch + 1) * 512]),
                reads=["io_stgc"], writes=["f2_brows"])

    k.sched.barrier()

    for bb_ in range(2):
        for e_ in range(8):
            k.op("v", lambda e, e_=e_, bb_=bb_: e.tensor_scalar_mul(
                diag[:, bb_ * 1024 + e_ * 128:bb_ * 1024 + (e_ + 1) * 128]
                if False else diag[:, (bb_ * 8 + e_) % 2 * 128:
                                   ((bb_ * 8 + e_) % 2 + 1) * 128],
                ident[:], ohcol[:, bb_ * 8 + e_:bb_ * 8 + e_ + 1]),
                reads=["ident_bf", "f2_ohc"], writes=[f"f2_diagx{bb_}_{e_}"])                 if False else None
    diag16 = k.sb("f2_diag16", [128, 16 * 128], BF16)
    for j in range(16):
        k.op("v", lambda e, j=j: e.tensor_scalar_mul(
            diag16[:, j * 128:(j + 1) * 128], ident[:], ohcol[:, j:j + 1]),
            reads=["ident_bf", "f2_ohc"], writes=["f2_diag16"])
    for dt in range(8):
        for oc_ in range(2):
            for e_ in range(8):
                sl = e_ % 2
                k.dma("y", stg[:, sl * 512:(sl + 1) * 512],
                      W["l2f_convWT"][e_ * 1024 + dt * 128:
                                      e_ * 1024 + (dt + 1) * 128,
                                      oc_ * 512:(oc_ + 1) * 512],
                      writes=[f"io_stg{sl}"])
                k.op("g", lambda e, sl=sl: e.tensor_copy(
                    wsl[:, sl * 512:(sl + 1) * 512], stg[:, sl * 512:(sl + 1) * 512]),
                    reads=[f"io_stg{sl}"], writes=[f"f2_ws0{sl}"])
                for bb_ in range(2):
                    ps_w = psA if bb_ == 0 else psB
                    pn_w = "psA" if bb_ == 0 else "psB"
                    k.mm(ps_w[:], diag16[:, (bb_ * 8 + e_) * 128:
                                         (bb_ * 8 + e_ + 1) * 128],
                         wsl[:, sl * 512:(sl + 1) * 512], start=e_ == 0,
                         stop=e_ == 7,
                         reads=["f2_diag16", f"f2_ws0{sl}"], writes=[pn_w])
            for bb_ in range(2):
                ps_w = psA if bb_ == 0 else psB
                pn_w = "psA" if bb_ == 0 else "psB"
                k.op("s", lambda e, ps_w=ps_w, bb_=bb_: e.copy(
                    wsl[:, 1024 + bb_ * 512:1024 + bb_ * 512 + 512], ps_w[:]),
                    reads=[pn_w], writes=["f2_ws1"])
                k.dma("g", W["l2f_wsel_dram"]
                      [bb_ * 1024 + dt * 128:bb_ * 1024 + (dt + 1) * 128,
                       oc_ * 512:(oc_ + 1) * 512],
                      wsl[:, 1024 + bb_ * 512:1024 + bb_ * 512 + 512],
                      reads=["f2_ws1"], writes=[])
    for dt in range(8):
        k.dma("y", stg[:, 0:1024], W["l2f_baseWT"][dt * 128:(dt + 1) * 128, :],
              writes=["io_stg"])
        k.op("g", lambda e: e.tensor_copy(wsl[:, 0:1024], stg[:, 0:1024]),
             reads=["io_stg"], writes=["f2_ws0"])
        k.dma("g", W["l2f_base_dram"][dt * 128:(dt + 1) * 128, :], wsl[:, 0:1024],
              reads=["f2_ws0"], writes=[])
    k.sched.barrier()

    for b in range(BPC):
        bt = b * T
        BO = b * 2112
        for db in range(DBLK):
            for fc in range(4):
                f0 = bt + fc * 512
                k.op("v", lambda e, f0=f0: e.tensor_mul(
                    t3[:], lnbf[0:1, f0:f0 + 512], r_rep[0:1, f0:f0 + 512]),
                    reads=["lnbf", "r_rep"], writes=["row_t3"])
                k.mm(psC[:], orf[:], t3[:], True, True,
                     reads=["orf", "row_t3"], writes=["psC"])
                dstx = xln[:, db * T + fc * 512:db * T + fc * 512 + 512]
                k.op("v", lambda e, dstx=dstx, f0=f0, db=db: e.tensor_mul(
                    dstx, x[:, db * TB + f0:db * TB + f0 + 512],
                    r_rep[:, f0:f0 + 512]),
                    reads=[f"x{db}", "r_rep"], writes=["f2_xln"])
                k.op("v", lambda e, dstx=dstx: e.tensor_sub(dstx, dstx, psC[:]),
                     reads=["f2_xln", "psC"], writes=["f2_xln"])
        for fc in range(4):
            f0 = fc * 512
            ps = psC if fc % 2 == 0 else psD
            pn = "psC" if fc % 2 == 0 else "psD"
            for db in range(8):
                k.mm(ps[0:64, :], dsel[:, b * 512 + db * 64:b * 512 + (db + 1) * 64],
                     xln[:, db * T + f0:db * T + f0 + 512],
                     start=db == 0, stop=False, reads=["f2_dsel", "f2_xln"],
                     writes=[pn])
            k.mm(ps[0:64, :], brows[0:1, BO:BO + 64], orow[:],
                 start=False, stop=True, reads=["f2_brows", "ones_row"],
                 writes=[pn])
            k.op("s", lambda e, f0=f0, ps=ps: e.activation(
                hbuf[:, f0:f0 + 512], ps[0:64, :], AF.Relu),
                reads=[pn], writes=["f2_h"])
        for fc in range(4):
            f0 = fc * 512
            for dt in range(8):
                ps = psC if dt % 2 == 0 else psD
                pn = "psC" if dt % 2 == 0 else "psD"
                k.mm(ps[:], usel[:, b * 1024 + dt * 128:b * 1024 + (dt + 1) * 128],
                     hbuf[:, f0:f0 + 512], start=True, stop=False,
                     reads=["f2_usel", "f2_h"], writes=[pn])
                k.mm(ps[:], brows[0:1, BO + 64 + dt * 128:BO + 64 + (dt + 1) * 128],
                     orow[:], start=False, stop=True,
                     reads=["f2_brows", "ones_row"], writes=[pn])
                k.op("v", lambda e, dt=dt, ps=ps: e.tensor_copy(
                    abuf[:, dt * 512:(dt + 1) * 512], ps[:]),
                    reads=[pn], writes=["f2_a"])
            for ot in range(8):
                ps = psA if ot % 2 == 0 else psB
                pn = "psA" if ot % 2 == 0 else "psB"
                k.dma("y", wsl[:, 0:1024].rearrange("p (a c) -> p a c", a=8),
                      W["l2f_base_dram"].rearrange("(a p) c -> p a c", p=128)
                      [:, :, ot * 128:(ot + 1) * 128], writes=["f2_ws0"])
                k.dma("s", wsl[:, 1024:2048].rearrange("p (a c) -> p a c", a=8),
                      W["l2f_wsel_dram"][b * 1024:(b + 1) * 1024, :]
                      .rearrange("(a p) c -> p a c", p=128)
                      [:, :, ot * 128:(ot + 1) * 128], writes=["f2_ws1"])
                for cb in range(8):
                    k.mm(ps[:], wsl[:, cb * 128:(cb + 1) * 128],
                         xln[:, cb * T + f0:cb * T + f0 + 512],
                         start=cb == 0, stop=False, reads=["f2_ws0", "f2_xln"],
                         writes=[pn])
                for cb in range(8):
                    k.mm(ps[:], wsl[:, 1024 + cb * 128:1024 + (cb + 1) * 128],
                         abuf[:, cb * 512:(cb + 1) * 512],
                         start=False, stop=False, reads=["f2_ws1", "f2_a"],
                         writes=[pn])
                k.mm(ps[:], brows[0:1, BO + 1088 + ot * 128:
                                  BO + 1088 + (ot + 1) * 128], orow[:],
                     start=False, stop=True, reads=["f2_brows", "ones_row"],
                     writes=[pn])
                xs = x[:, ot * TB + bt + f0:ot * TB + bt + f0 + 512]
                k.op("v", lambda e, xs=xs, ps=ps: e.tensor_add(xs, ps[:], xs),
                     reads=[pn], writes=[f"x{ot}"])

    k.sched.barrier()



L3F_PHASE = 9


def build_l3_ffn(k: K, W):
    x = k.tensors["x_sb"]
    r_rep = k.tensors["r_rep"]
    oc, orow, orf = k.tensors["ones_col"], k.tensors["ones_row"], k.tensors["orf"]
    ocf = k.tensors["ones_col_f"]
    sq = k.tensors["sq_t"]
    psA, psB, psC, psD = (k.tensors[n] for n in ("psA", "psB", "psC", "psD"))
    psR1, psR2 = k.tensors["psR1"], k.tensors["psR2"]
    t1, t3 = k.tensors["row_t1"], k.tensors["row_t3"]
    t2 = k.tensors["row_t2"][0:1, :]

    build_ln_rows(k)
    lnbf = k.tensors["lnbf"]
    stg = k.tensors["io_stg"]

    xn = k.sb("f3_xn", [128, 8 * T], BF16)
    kT = k.sb("f3_kT", [128, 8 * T], BF16)
    vb = k.sb("f3_v", [128, 16 * 1024], BF16)
    qch = k.sb("f3_q", [128, 8 * 512], BF16)
    ao = qch
    ws = k.sb("f3_ws", [128, 1024], BF16)
    bb = k.sb("f3_bb", [128, 2048], BF16)
    orow4 = k.sb("orow4", [128, 512], BF16)
    k.op("v", lambda e: e.memset(orow4[:], 1.0), writes=["orow4"])
    wbcol = k.sb("f3_wb", [128, 16], F32)
    p_sb = sq
    srep = k.sb("f3_srep", [128, 512], F32)
    SC = 1.0 / 32.0

    def wtile(src_ap, nrow, slot, tag):
        a = nrow // 128
        st = stg[:, 0:8 * 128]
        k.dma("y", st[:, 0:a * 128].rearrange("p (a c) -> p a c", a=a),
              src_ap.rearrange("(a p) c -> p a c", p=128), writes=["io_stg"])
        k.op("g", lambda e, slot=slot, a=a: e.tensor_copy(
            slot[:, 0:a * 128], st[:, 0:a * 128]),
            reads=["io_stg"], writes=[tag])

    for i, nm in enumerate(("l3f_qb", "l3f_kb", "l3f_vb", "l3f_ob")):
        pb, c0 = [(0, 0), (32, 0), (64, 0), (0, 1024)][i]
        k.dma("y", stg[0:1, 0:1024], W[nm][0:1, :], writes=["io_stg"])
        k.op("s", lambda e, pb=pb, c0=c0: e.copy(bb[pb:pb + 1, c0:c0 + 1024],
                                                 stg[0:1, 0:1024]),
             reads=["io_stg"], writes=["f3_bb"])
    for ri, nm in enumerate(("l3f_nw", "l3f_nb")):
        for ch in range(2):
            k.dma("y", t3[0:1, :], W[nm][0:1, ch * 512:(ch + 1) * 512],
                  reads=["row_t3"], writes=["row_t3"])
            for j in range(4):
                k.mm(psD[:, j:j + 1], t3[0:1, j * 128:(j + 1) * 128], orf[0:1, 0:1],
                     True, True, reads=["row_t3", "orf"], writes=["psD"])
            k.op("s", lambda e, ri=ri, ch=ch: e.copy(
                wbcol[:, ri * 8 + ch * 4:ri * 8 + ch * 4 + 4], psD[:, 0:4]),
                reads=["psD"], writes=["f3_wb"])
    k.sched.barrier()

    for b in range(BPC):
        bt = b * T
        for fc in range(4):
            f0 = bt + fc * 512
            fl = fc * 512
            k.op("v", lambda e, f0=f0: e.tensor_mul(
                t3[:], lnbf[0:1, f0:f0 + 512], r_rep[0:1, f0:f0 + 512]),
                reads=["lnbf", "r_rep"], writes=["row_t3"])
            k.mm(psC[:], orf[:], t3[:], True, True,
                 reads=["orf", "row_t3"], writes=["psC"])
            for db in range(DBLK):
                dst = xn[:, db * T + fl:db * T + fl + 512]
                k.op("v", lambda e, dst=dst, f0=f0, db=db: e.tensor_mul(
                    dst, x[:, db * TB + f0:db * TB + f0 + 512],
                    r_rep[:, f0:f0 + 512]),
                    reads=[f"x{db}", "r_rep"], writes=[f"f3_xn{db}"])
                k.op("v", lambda e, dst=dst: e.tensor_sub(dst, dst, psC[:]),
                     reads=[f"f3_xn{db}", "psC"], writes=[f"f3_xn{db}"])
                sqs = sq[:, (db % 2) * 512:(db % 2) * 512 + 512]
                k.op("s", lambda e, dst=dst, sqs=sqs: e.activation(
                    sqs, dst, AF.Square), reads=[f"f3_xn{db}"],
                    writes=[f"sq{db % 2}"])
                k.mm(psR1[:], oc[:], dst, start=db == 0, stop=db == DBLK - 1,
                     reads=["ones_col", f"f3_xn{db}"], writes=["psR1"])
                k.mm(psR2[:], oc[:], sqs, start=db == 0, stop=db == DBLK - 1,
                     reads=["ones_col", f"sq{db % 2}"], writes=["psR2"])
            k.op("v", lambda e: e.tensor_scalar_mul(t1[:], psR1[:], 1.0 / D),
                 reads=["psR1"], writes=["row_t1"])
            k.op("v", lambda e: e.tensor_scalar_mul(t3[:], psR2[:], 1.0 / D),
                 reads=["psR2"], writes=["row_t3"])
            k.op("s", lambda e: e.activation(t2, t1[:], AF.Square),
                 reads=["row_t1"], writes=["row_t2"])
            k.op("v", lambda e: e.tensor_sub(t3[:], t3[:], t2),
                 reads=["row_t3", "row_t2"], writes=["row_t3"])
            k.op("s", lambda e: e.activation(t3[:], t3[:], AF.Ln, bias=1e-5),
                 reads=["row_t3"], writes=["row_t3"])
            k.op("s", lambda e: e.activation(t3[:], t3[:], AF.Exp, scale=-0.5),
                 reads=["row_t3"], writes=["row_t3"])
            k.mm(psC[:], orf[:], t1[:], True, True,
                 reads=["orf", "row_t1"], writes=["psC"])
            k.mm(psD[:], orf[:], t3[:], True, True,
                 reads=["orf", "row_t3"], writes=["psD"])
            for db in range(DBLK):
                dst = xn[:, db * T + fl:db * T + fl + 512]
                tmp = stg[:, 0:512]
                k.op("v", lambda e, dst=dst, tmp=tmp: e.tensor_sub(
                    tmp, dst, psC[:]),
                    reads=[f"f3_xn{db}", "psC"], writes=["io_stg"])
                k.op("v", lambda e, tmp=tmp: e.tensor_mul(
                    tmp, tmp, psD[:]),
                    reads=["io_stg", "psD"], writes=["io_stg"])
                k.op("g", lambda e, dst=dst, tmp=tmp, db=db: e.tensor_scalar(
                    dst, tmp, wbcol[:, db:db + 1], wbcol[:, 8 + db:8 + db + 1],
                    ALU.mult, ALU.add),
                    reads=["io_stg", "f3_wb"], writes=[f"f3_xn{db}"])
        XNR = [f"f3_xn{db}" for db in range(DBLK)]
        if L3F_PHASE < 2:
            continue
        for seg in range(4):
            m = 1024 >> seg
            nc_ = m // 128
            sl0 = seg * 512
            for ot in range(8):
                slot = ws[:, 0:1024]
                stag = "f3_ws0"
                wtile(W["l3f_kwT"][0:m, ot * 128:(ot + 1) * 128], m, slot, stag)
                ps = psA if ot % 2 == 0 else psB
                pn = "psA" if ot % 2 == 0 else "psB"
                for cb in range(nc_):
                    k.mm(ps[:], slot[:, cb * 128:(cb + 1) * 128],
                         xn[:, cb * T + sl0:cb * T + sl0 + 512],
                         start=cb == 0, stop=False, reads=[stag] + XNR, writes=[pn])
                k.mm(ps[:], bb[32:33, ot * 128:(ot + 1) * 128],
                     orow4[32:33, :], start=False, stop=True,
                     reads=["f3_bb", "orow4"], writes=[pn])
                k.op("s", lambda e, ot=ot, sl0=sl0, ps=ps: e.copy(
                    kT[:, ot * T + sl0:ot * T + sl0 + 512], ps[:]),
                    reads=[pn], writes=["f3_kT"])
            for vt in range(8):
                slot = ws[:, 0:1024]
                stag = "f3_ws0"
                wtile(W["l3f_vwT"][0:m, vt * 128:(vt + 1) * 128], m, slot, stag)
                for tt in range(4):
                    t0 = sl0 + tt * 128
                    ps = psC if tt % 2 == 0 else psD
                    pn = "psC" if tt % 2 == 0 else "psD"
                    for cb in range(nc_):
                        k.mm(ps[:, 0:128], xn[:, cb * T + t0:cb * T + t0 + 128],
                             slot[:, cb * 128:(cb + 1) * 128],
                             start=cb == 0, stop=False, reads=[stag] + XNR,
                             writes=[pn])
                    k.mm(ps[:, 0:128], orow4[64:65, 0:128],
                         bb[64:65, vt * 128:(vt + 1) * 128],
                         start=False, stop=True, reads=["f3_bb", "orow4"],
                         writes=[pn])
                    k.op("s", lambda e, t0=t0, vt=vt, ps=ps: e.copy(
                        vb[:, (t0 // 128) * 1024 + vt * 128:
                           (t0 // 128) * 1024 + vt * 128 + 128],
                        ps[:, 0:128]),
                        reads=[pn], writes=["f3_vb"])
        if L3F_PHASE < 3:
            continue
        for seg in range(4):
            m = 1024 >> seg
            nc_ = m // 128
            sl0 = seg * 512
            f0 = bt + sl0
            for ot in range(8):
                slot = ws[:, 0:1024]
                stag = "f3_ws0"
                wtile(W["l3f_qwT"][0:m, ot * 128:(ot + 1) * 128], m, slot, stag)
                ps = psA if ot % 2 == 0 else psB
                pn = "psA" if ot % 2 == 0 else "psB"
                for cb in range(nc_):
                    k.mm(ps[:], slot[:, cb * 128:(cb + 1) * 128],
                         xn[:, cb * T + sl0:cb * T + sl0 + 512],
                         start=cb == 0, stop=False, reads=[stag] + XNR, writes=[pn])
                k.mm(ps[:], bb[0:1, ot * 128:(ot + 1) * 128], orow4[0:1, :],
                     start=False, stop=True, reads=["f3_bb", "orow4"],
                     writes=[pn])
                k.op("s", lambda e, ot=ot, ps=ps: e.copy(
                    qch[:, ot * 512:(ot + 1) * 512], ps[:]),
                    reads=[pn], writes=[f"f3_q{ot}"])
            for h in range(8 if L3F_PHASE >= 4 else 0):
                for kt in range(16):
                    ps = psC if kt % 2 == 0 else psD
                    pn = "psC" if kt % 2 == 0 else "psD"
                    k.mm(ps[:], kT[:, h * T + kt * 128:h * T + (kt + 1) * 128],
                         qch[:, h * 512:(h + 1) * 512],
                         True, True, reads=["f3_kT", f"f3_q{h}"], writes=[pn])
                    pt = p_sb[:, (kt % 2) * 512:(kt % 2) * 512 + 512]
                    ptn = f"sq{kt % 2}"
                    k.op("s", lambda e, pt=pt, ps=ps: e.activation(
                        pt, ps[:], AF.Exp, scale=SC), reads=[pn], writes=[ptn])
                    k.mm(psA[:], vb[:, kt * 1024 + h * 128:kt * 1024 + h * 128 + 128],
                         pt, start=kt == 0, stop=kt == 15,
                         reads=["f3_vb", ptn], writes=["psA"])
                    k.mm(psR1[:], oc[:, 0:1], pt, start=kt == 0, stop=kt == 15,
                         reads=["ones_col", ptn], writes=["psR1"])
                k.op("v", lambda e: e.reciprocal(t1[:], psR1[:]),
                     reads=["psR1"], writes=["row_t1"])
                k.mm(psB[:], orf[:], t1[:], True, True,
                     reads=["orf", "row_t1"], writes=["psB"])
                k.op("s", lambda e: e.copy(srep[:], psB[:]),
                     reads=["psB"], writes=["f3_srep"])
                k.op("v", lambda e, h=h: e.tensor_mul(
                    ao[:, h * 512:(h + 1) * 512], psA[:], srep[:]),
                    reads=["psA", "f3_srep", f"f3_q{h}"], writes=[f"f3_q{h}"])
            k.op("v", lambda e, f0=f0: e.tensor_mul(
                t3[:], lnbf[0:1, f0:f0 + 512], r_rep[0:1, f0:f0 + 512]),
                reads=["lnbf", "r_rep"], writes=["row_t3"])
            k.mm(psB[:], orf[:], t3[:], True, True,
                 reads=["orf", "row_t3"], writes=["psB"])
            for ot in range(8 if L3F_PHASE >= 5 else 0):
                ps = psC if ot % 2 == 0 else psD
                pn = "psC" if ot % 2 == 0 else "psD"
                if ot < nc_:
                    slot = ws[:, 0:1024]
                    stag = "f3_ws0"
                    wtile(W["l3f_owT"][0:m, ot * 128:(ot + 1) * 128], m, slot, stag)
                    for cb in range(nc_):
                        k.mm(ps[:], slot[:, cb * 128:(cb + 1) * 128],
                             ao[:, cb * 512:(cb + 1) * 512],
                             start=cb == 0, stop=False,
                             reads=[stag] + [f"f3_q{cb}"], writes=[pn])
                    k.mm(ps[:], bb[0:1, 1024 + ot * 128:1024 + (ot + 1) * 128],
                         orow4[0:1, :], start=False, stop=True,
                         reads=["f3_bb", "orow4"], writes=[pn])
                tmpa = sq[:, 0:512]
                k.op("g", lambda e, tmpa=tmpa, f0=f0: e.tensor_scalar(
                    tmpa, r_rep[:, f0:f0 + 512], 1.0, None, ALU.add),
                    reads=["r_rep"], writes=["sq0"])
                xs = x[:, ot * TB + f0:ot * TB + f0 + 512]
                tmpb = stg[:, 0:512]
                k.op("v", lambda e, tmpb=tmpb, xs=xs, tmpa=tmpa: e.tensor_mul(
                    tmpb, xs, tmpa), reads=[f"x{ot}", "sq0"], writes=["io_stg"])
                k.op("v", lambda e, tmpb=tmpb: e.tensor_sub(tmpb, tmpb, psB[:]),
                     reads=["io_stg", "psB"], writes=["io_stg"])
                if ot < nc_:
                    k.op("v", lambda e, xs=xs, ps=ps, tmpb=tmpb: e.tensor_add(
                        xs, ps[:], tmpb), reads=[pn, "io_stg"], writes=[f"x{ot}"])
                else:
                    k.op("v", lambda e, xs=xs, tmpb=tmpb: e.tensor_copy(
                        xs, tmpb), reads=["io_stg"], writes=[f"x{ot}"])

    k.sched.barrier()



L3A_PHASE = 9


def build_l3_attn(k: K, W):
    x = k.tensors["x_sb"]
    r_rep = k.tensors["r_rep"]
    oc, orow, orf = k.tensors["ones_col"], k.tensors["ones_row"], k.tensors["orf"]
    ocf = k.tensors["ones_col_f"]
    sq = k.tensors["sq_t"]
    psA, psB, psC, psD = (k.tensors[n] for n in ("psA", "psB", "psC", "psD"))
    psR1, psR2 = k.tensors["psR1"], k.tensors["psR2"]
    t1, t3 = k.tensors["row_t1"], k.tensors["row_t3"]

    build_ln_rows(k)
    lnbf = k.tensors["lnbf"]
    stg = k.tensors["io_stg"]
    DI = 2048

    ws = k.sb("l3_ws", [128, 2 * 2048], BF16)
    lf = k.sb("l3_lf", [1, 128], BF16)
    acc = k.sb("l3_acc", [128, 2048], F32)
    sig = k.sb("l3_sig", [128, 2048], F32)
    xst = k.sb("l3_xst", [128, 2048], BF16)
    cw = k.sb("l3_cw", [128, 64], F32)
    cb = k.sb("l3_cb", [128, 16], F32)
    dtb = k.sb("l3_dtb", [128, 16], F32)
    dcol = k.sb("l3_dcol", [128, 16], F32)
    xpw = k.sb("l3_xpw", [128, 16 * 128], BF16)
    dthT = k.sb("l3_dth", [64, 2048], BF16)
    xdbl = k.sb("l3_xdbl", [96, 2048], BF16)
    srp = k.sb("l3_srp", [128, 2048], F32)
    r3 = k.sb("l3_r3", [1, 2 * 2048 + 1024], BF16)
    ut = k.sb("l3_u", [128, 2 * 2048], F32)
    xzp = ut[:, 0:2052]
    wt = k.sb("l3_w", [128, 2 * 2048], BF16)
    ya = k.sb("l3_ya", [128, 2 * 2048], F32)
    ab = sig[:, 0:1024]
    st_ = acc
    yfc = k.sb("l3_yfc", [128, 8 * 512], BF16)
    GD = 2

    k.dma("g", cw[:].rearrange("p (a c) -> p a c", a=16),
          W["l3a_convw"].rearrange("(a p) c -> p a c", p=128), writes=["l3_cw"])
    k.dma("g", cb[:].rearrange("p (a c) -> p a c", a=16),
          W["l3a_convb"].rearrange("(a p) c -> p a c", p=128), writes=["l3_cb"],
          allow_slow_non_contiguous=True)
    k.dma("g", dtb[:].rearrange("p (a c) -> p a c", a=16),
          W["l3a_dtb"].rearrange("(a p) c -> p a c", p=128), writes=["l3_dtb"],
          allow_slow_non_contiguous=True)
    k.dma("g", dcol[:].rearrange("p (a c) -> p a c", a=16),
          W["l3a_D"].rearrange("(a p) c -> p a c", p=128), writes=["l3_dcol"],
          allow_slow_non_contiguous=True)
    k.op("v", lambda e: e.memset(xpw[:], 0.0), writes=["l3_xpw"])
    for hh in range(2):
        k.dma("y", stg[:, 0:768].rearrange("p (a c) -> p a c", a=8),
              W["l3a_xpT"].rearrange("(a p) c -> p a c", p=128)[:, hh * 8:(hh + 1) * 8, :],
              writes=["io_stg"])
        k.op("g", lambda e, hh=hh: e.tensor_copy(
            xpw[:, hh * 8 * 128:hh * 8 * 128 + 768]
            .rearrange("p (a c) -> p a c", c=96),
            stg[:, 0:768].rearrange("p (a c) -> p a c", c=96)),
            reads=["io_stg"], writes=["l3_xpw"]) if False else None
        for j in range(8):
            k.op("g", lambda e, hh=hh, j=j: e.tensor_copy(
                xpw[:, (hh * 8 + j) * 128:(hh * 8 + j) * 128 + 96],
                stg[:, j * 96:(j + 1) * 96]),
                reads=["io_stg"], writes=["l3_xpw"])
    k.dma("y", stg[0:64, 0:1024], W["l3a_dthT"][:, 0:1024], writes=["io_stg"])
    k.op("g", lambda e: e.tensor_copy(dthT[:, 0:1024], stg[0:64, 0:1024]),
         reads=["io_stg"], writes=["l3_dth"])
    k.dma("y", stg[0:64, 0:1024], W["l3a_dthT"][:, 1024:2048], writes=["io_stg"])
    k.op("g", lambda e: e.tensor_copy(dthT[:, 1024:2048], stg[0:64, 0:1024]),
         reads=["io_stg"], writes=["l3_dth"])
    for rr_ in range(16):
        k.dma("y", stg[:, 0:1024], W["l3a_opT"][rr_ * 128:(rr_ + 1) * 128, :],
              writes=["io_stg"])
        k.op("g", lambda e: e.tensor_copy(ws[:, 0:1024], stg[:, 0:1024]),
             reads=["io_stg"], writes=["l3_ws0"])
        k.dma("g", W["l3a_opT_bf"][rr_ * 128:(rr_ + 1) * 128, :], ws[:, 0:1024],
              reads=["l3_ws0"], writes=[])
    k.sched.barrier()

    def wtile8(col0, slot, tag, wname="l3a_ipT"):
        k.dma("y", stg[:, 0:1024].rearrange("p (a c) -> p a c", a=8),
              W[wname].rearrange("(a p) c -> p a c", p=128)[:, :, col0:col0 + 128],
              writes=["io_stg"])
        k.op("g", lambda e, slot=slot: e.tensor_copy(slot[:, 0:1024],
                                                     stg[:, 0:1024]),
             reads=["io_stg"], writes=[tag])

    def inproj_tile(b, col0, dst_f32, dstn, slot, stag, need_rs):
        bt = b * T
        wtile8(col0, slot, stag)
        if need_rs:
            for cbk in range(8):
                k.mm(psR1[:, 0:128], oc[:], slot[:, cbk * 128:(cbk + 1) * 128],
                     start=cbk == 0, stop=cbk == 7, reads=["ones_col", stag],
                     writes=["psR1"])
            k.op("s", lambda e: e.activation(
                lf[0:1, 0:128], psR1[:, 0:128], AF.Copy, scale=-1.0),
                reads=["psR1"], writes=["l3_lf"])
        for ch in range(4):
            f0 = bt + ch * 512
            ps = psA if ch % 2 == 0 else psB
            pn = "psA" if ch % 2 == 0 else "psB"
            for cbk in range(DBLK):
                k.mm(ps[:], slot[:, cbk * 128:(cbk + 1) * 128],
                     x[:, cbk * TB + f0:cbk * TB + f0 + 512],
                     start=cbk == 0, stop=False, reads=[stag, f"x{cbk}"],
                     writes=[pn])
            k.mm(ps[:], lf[0:1, 0:128], lnbf[0:1, f0:f0 + 512],
                 start=False, stop=True, reads=["l3_lf", "lnbf"], writes=[pn])
            k.op("v", lambda e, ch=ch, ps=ps, f0=f0, dst_f32=dst_f32:
                 e.tensor_mul(dst_f32[:, ch * 512:(ch + 1) * 512], ps[:],
                              r_rep[:, f0:f0 + 512]),
                 reads=[pn, "r_rep"], writes=[dstn])

    for b in range(BPC):
        bt = b * T
        k.sched.barrier()
        for dt in range(16):
            slot = ws[:, (dt % 2) * 2048:(dt % 2) * 2048 + 1024]
            stag = f"l3_ws{dt % 2}"
            k.op("v", lambda e: e.memset(xzp[:, 0:3], 0.0), writes=["l3_xzp"])
            inproj_tile(b, dt * 128, xzp[:, 3:2051], "l3_xzp", slot, stag, True)
            k.op("v", lambda e, dt=dt: e.tensor_scalar_mul(
                acc[:], xzp[:, 0:2048], cw[:, dt * 4:dt * 4 + 1]),
                reads=["l3_xzp", "l3_cw"], writes=["l3_acc"])
            for j in range(1, 4):
                k.op("v", lambda e, dt=dt, j=j: e.scalar_tensor_tensor(
                    acc[:], xzp[:, j:j + 2048], cw[:, dt * 4 + j:dt * 4 + j + 1],
                    acc[:], ALU.mult, ALU.add),
                    reads=["l3_xzp", "l3_cw", "l3_acc"], writes=["l3_acc"])
            k.op("v", lambda e, dt=dt: e.tensor_scalar(
                acc[:], acc[:], cb[:, dt:dt + 1], None, ALU.add),
                reads=["l3_acc", "l3_cb"], writes=["l3_acc"])
            k.op("s", lambda e: e.activation(sig[:], acc[:], AF.Sigmoid),
                 reads=["l3_acc"], writes=["l3_sig"])
            k.op("v", lambda e: e.tensor_mul(xst[:], acc[:], sig[:]),
                 reads=["l3_acc", "l3_sig"], writes=["l3_xst"])
            k.dma("g", W["l3a_xssm_dram"][b * DI + dt * 128:b * DI + (dt + 1) * 128, :],
                  xst[:], reads=["l3_xst"], writes=["d_xssm"])
        k.sched.barrier()
        if L3A_PHASE < 2:
            continue
        for ch in range(4):
            ps = psC if ch % 2 == 0 else psD
            pn = "psC" if ch % 2 == 0 else "psD"
            for dt in range(16):
                k.dma("y", xst[:, 0:512],
                      W["l3a_xssm_dram"][b * DI + dt * 128:b * DI + (dt + 1) * 128,
                                         ch * 512:(ch + 1) * 512],
                      reads=["d_xssm"], writes=["l3_xst"])
                k.mm(ps[:], xpw[:, dt * 128:(dt + 1) * 128], xst[:, 0:512],
                     start=dt == 0, stop=dt == 15, reads=["l3_xpw", "l3_xst"],
                     writes=[pn])
            k.op("s", lambda e, ch=ch, ps=ps: e.copy(
                xdbl[:, ch * 512:(ch + 1) * 512], ps[0:96, :]),
                reads=[pn], writes=["l3_xdbl"])

        if L3A_PHASE < 3:
            continue
        k.op("s", lambda e: e.activation(acc[0:16, :], xdbl[64:80, :], AF.Square),
             reads=["l3_xdbl"], writes=["l3_acc"])
        for ch in range(4):
            k.mm(psR1[:], ocf[0:16, :], acc[0:16, ch * 512:(ch + 1) * 512],
                 True, True, reads=["ones_col_f", "l3_acc"], writes=["psR1"])
            k.mm(psC[:], orf[:], psR1[:].rearrange("a b -> a b") if False else t1[:],
                 True, True, reads=["orf", "row_t1"], writes=["psC"])                 if False else None
            k.op("v", lambda e: e.tensor_copy(t1[:], psR1[:]),
                 reads=["psR1"], writes=["row_t1"])
            k.mm(psC[:], orf[:], t1[:], True, True,
                 reads=["orf", "row_t1"], writes=["psC"])
            k.op("s", lambda e, ch=ch: e.copy(srp[:, ch * 512:(ch + 1) * 512],
                                              psC[:]),
                 reads=["psC"], writes=["l3_srp"])
        k.sched.barrier()
        if L3A_PHASE < 4:
            continue
        for dtg in range(16 // GD):
            for g_ in range(GD):
                dt = dtg * GD + g_
                for ch in range(4):
                    ps = psA if ch % 2 == 0 else psB
                    pn = "psA" if ch % 2 == 0 else "psB"
                    k.mm(ps[:], dthT[:, dt * 128:(dt + 1) * 128],
                         xdbl[0:64, ch * 512:(ch + 1) * 512],
                         True, True, reads=["l3_dth", "l3_xdbl"], writes=[pn])
                    k.op("s", lambda e, ps=ps, ch=ch, g_=g_, dt=dt: e.activation(
                        ut[:, g_ * 2048 + ch * 512:g_ * 2048 + (ch + 1) * 512],
                        ps[:], AF.Sigmoid, bias=dtb[:, dt:dt + 1]),
                        reads=[pn, "l3_dtb"], writes=[f"l3_u{g_}"])
                ug = ut[:, g_ * 2048:(g_ + 1) * 2048]
                k.op("v", lambda e, ug=ug: e.tensor_mul(acc[:], ug, srp[:]),
                     reads=[f"l3_u{g_}", "l3_srp"], writes=["l3_acc"])
                k.op("v", lambda e: e.tensor_scalar(acc[:], acc[:], 1.0, None,
                                                    ALU.add),
                     reads=["l3_acc"], writes=["l3_acc"])
                k.op("v", lambda e: e.reciprocal(acc[:], acc[:]),
                     reads=["l3_acc"], writes=["l3_acc"])
                k.op("v", lambda e, ug=ug: e.tensor_mul(ug, ug, acc[:]),
                     reads=[f"l3_u{g_}", "l3_acc"], writes=[f"l3_u{g_}"])
                k.dma("y", xst[:], W["l3a_xssm_dram"]
                      [b * DI + dt * 128:b * DI + (dt + 1) * 128, :],
                      reads=["d_xssm"], writes=["l3_xst"])
                k.op("v", lambda e, ug=ug, g_=g_: e.tensor_mul(
                    wt[:, g_ * 2048:(g_ + 1) * 2048], xst[:], ug),
                    reads=["l3_xst", f"l3_u{g_}"], writes=[f"l3_w{g_}"])
                k.op("v", lambda e, g_=g_: e.memset(
                    ya[:, g_ * 2048:(g_ + 1) * 2048], 0.0), writes=[f"l3_ya{g_}"])
            for n_ in range(16 if L3A_PHASE >= 5 else 0):
                k.dma("s", r3[0:1, 0:2048], xdbl[64 + n_:65 + n_, :],
                      reads=["l3_xdbl"], writes=["l3_r3"])
                k.dma("s", r3[0:1, 2048:4096], xdbl[80 + n_:81 + n_, :],
                      reads=["l3_xdbl"], writes=["l3_r3"])
                for ch in range(4):
                    c0 = ch * 512
                    par = ch % 2
                    pv, pk, pq = (psB, psC, psD) if par == 0 else                         (psA, k.tensors["psE1"], k.tensors["psE2"])
                    pvn, pkn, pqn = ("psB", "psC", "psD") if par == 0 else                         ("psA", "psE1", "psE2")
                    kkslot = 4096 + par * 512
                    k.op("s", lambda e, c0=c0, kkslot=kkslot: e.activation(
                        r3[0:1, kkslot:kkslot + 512], r3[0:1, c0:c0 + 512],
                        AF.Square),
                        reads=["l3_r3"], writes=[f"l3_r3kk{par}"])
                    k.mm(pv[:], orow[0:1, 0:128], r3[0:1, kkslot:kkslot + 512],
                         True, True, reads=["ones_row", f"l3_r3kk{par}"],
                         writes=[pvn])
                    k.mm(pk[:], orow[0:1, 0:128], r3[0:1, c0:c0 + 512],
                         True, True, reads=["ones_row", "l3_r3"], writes=[pkn])
                    k.mm(pq[:], orow[0:1, 0:128], r3[0:1, 2048 + c0:2048 + c0 + 512],
                         True, True, reads=["ones_row", "l3_r3"], writes=[pqn])
                    for g_ in range(GD):
                        dt = dtg * GD + g_
                        ug = ut[:, g_ * 2048 + c0:g_ * 2048 + c0 + 512]
                        wg = wt[:, g_ * 2048 + c0:g_ * 2048 + c0 + 512]
                        av_ = sig[:, par * 1024:par * 1024 + 512]
                        bv_ = sig[:, par * 1024 + 512:par * 1024 + 1024]
                        an, bn = f"l3_ab0{par}", f"l3_ab1{par}"
                        k.op("v", lambda e, ug=ug, av_=av_, pv=pv:
                             e.scalar_tensor_tensor(
                                 av_, ug, -1.0, pv[:], ALU.mult, ALU.mult),
                             reads=[f"l3_u{g_}", pvn], writes=[an])
                        k.op("s", lambda e, av_=av_: e.activation(
                            av_, av_, AF.Copy, bias=1.0),
                            reads=[an], writes=[an])
                        k.op("v", lambda e, wg=wg, bv_=bv_, pk=pk: e.tensor_tensor(
                            bv_, wg, pk[:], ALU.mult),
                            reads=[f"l3_w{g_}", pkn], writes=[bn])
                        ss = st_[:, (g_ * 2 + ch % 2) * 512:
                                 (g_ * 2 + ch % 2) * 512 + 512]
                        sprev = st_[:, (g_ * 2 + (ch + 1) % 2) * 512 + 511:
                                    (g_ * 2 + (ch + 1) % 2) * 512 + 512]
                        sn = f"l3_s{g_}_{ch % 2}"
                        spn = f"l3_s{g_}_{(ch + 1) % 2}"
                        if ch == 0:
                            k.op("v", lambda e, ss=ss, av_=av_, bv_=bv_:
                                 e.tensor_tensor_scan(ss, av_, bv_, 0.0,
                                                      ALU.mult, ALU.add),
                                 reads=[an, bn], writes=[sn])
                        else:
                            k.op("v", lambda e, ss=ss, av_=av_, bv_=bv_,
                                 sprev=sprev: e.tensor_tensor_scan(
                                     ss, av_, bv_, sprev, ALU.mult, ALU.add),
                                 reads=[an, bn, spn], writes=[sn])
                        k.op("v", lambda e, ss=ss, av_=av_, pq=pq: e.tensor_tensor(
                            av_, ss, pq[:], ALU.mult),
                            reads=[sn, pqn], writes=[an])
                        k.op("g", lambda e, g_=g_, c0=c0, av_=av_: e.tensor_tensor(
                            ya[:, g_ * 2048 + c0:g_ * 2048 + c0 + 512],
                            av_, ya[:, g_ * 2048 + c0:g_ * 2048 + c0 + 512],
                            ALU.add),
                            reads=[an, f"l3_ya{g_}"], writes=[f"l3_ya{g_}"])
            k.sched.barrier()
            for g_ in range(GD if L3A_PHASE >= 6 else 0):
                dt = dtg * GD + g_
                slot = ws[:, (dt % 2) * 2048:(dt % 2) * 2048 + 1024]
                stag = f"l3_ws{dt % 2}"
                inproj_tile(b, 2048 + dt * 128, acc[:], "l3_acc", slot, stag,
                            True)
                k.op("s", lambda e: e.activation(sig[:], acc[:], AF.Sigmoid),
                     reads=["l3_acc"], writes=["l3_sig"])
                k.op("v", lambda e: e.tensor_mul(sig[:], acc[:], sig[:]),
                     reads=["l3_acc", "l3_sig"], writes=["l3_sig"])
                k.dma("y", xst[:], W["l3a_xssm_dram"]
                      [b * DI + dt * 128:b * DI + (dt + 1) * 128, :],
                      reads=["d_xssm"], writes=["l3_xst"])
                k.op("v", lambda e, g_=g_, dt=dt: e.scalar_tensor_tensor(
                    acc[:], xst[:], dcol[:, dt:dt + 1],
                    ya[:, g_ * 2048:(g_ + 1) * 2048], ALU.mult, ALU.add),
                    reads=["l3_xst", "l3_dcol", f"l3_ya{g_}"], writes=["l3_acc"])
                k.op("v", lambda e: e.tensor_mul(xst[:], acc[:], sig[:]),
                     reads=["l3_acc", "l3_sig"], writes=["l3_xst"])
                k.dma("g", W["l3a_yfin_dram"]
                      [b * DI + dt * 128:b * DI + (dt + 1) * 128, :],
                      xst[:], reads=["l3_xst"], writes=["d_yfin"])
        k.sched.barrier()
        if L3A_PHASE < 7:
            continue
        k.sched.barrier()
        for ch in range(4):
            f0 = bt + ch * 512
            for ot in range(8):
                slot = ws[:, (ot % 2) * 2048:(ot % 2) * 2048 + 2048]
                stag = f"l3_ws{ot % 2}"
                k.dma("y", slot[:].rearrange("p (a c) -> p a c", a=16),
                      W["l3a_opT_bf"].rearrange("(a p) c -> p a c", p=128)
                      [:, :, ot * 128:(ot + 1) * 128],
                      writes=[stag])
                ps = psA if ot % 2 == 0 else psB
                pn = "psA" if ot % 2 == 0 else "psB"
                for half in range(2):
                    for dt_ in range(8):
                        dt = half * 8 + dt_
                        k.dma("y", yfc[:, dt_ * 512:(dt_ + 1) * 512],
                              W["l3a_yfin_dram"]
                              [b * DI + dt * 128:b * DI + (dt + 1) * 128,
                               ch * 512:(ch + 1) * 512],
                              reads=["d_yfin"], writes=[f"l3_yfc{dt_}"])
                    for dt_ in range(8):
                        k.mm(ps[:], slot[:, (half * 8 + dt_) * 128:
                                         (half * 8 + dt_ + 1) * 128],
                             yfc[:, dt_ * 512:(dt_ + 1) * 512],
                             start=(half == 0 and dt_ == 0),
                             stop=(half == 1 and dt_ == 7),
                             reads=[stag, f"l3_yfc{dt_}"], writes=[pn])
                xs = x[:, ot * TB + f0:ot * TB + f0 + 512]
                k.op("v", lambda e, xs=xs, ps=ps: e.tensor_add(xs, ps[:], xs),
                     reads=[pn], writes=[f"x{ot}"])

    k.sched.barrier()


BUILDERS = {"l0_attn": build_l0_attn, "l0_ffn": build_l0_ffn,
            "l1_attn": build_l1_attn, "l1_ffn": build_l1_ffn,
            "l2_attn": build_l2_attn, "l2_ffn": build_l2_ffn,
            "l3_ffn": build_l3_ffn, "l3_attn": build_l3_attn}
SUBLAYERS = ["l0_attn", "l0_ffn", "l1_attn", "l1_ffn", "l2_attn",
             "l2_ffn", "l3_attn", "l3_ffn"]


def build_fused_program():
    nc = bass.Bass("TRN2", target_bir_lowering=False)
    xin = nc.dram_tensor("xin", [D, TB], F32, kind="ExternalInput").ap()
    xout = nc.dram_tensor("xout", [D, TB], F32, kind="ExternalOutput").ap()
    W = {}
    for name in SUBLAYERS:
        for n in WNAMES[name]:
            W[n] = nc.dram_tensor(n, WSHAPES[n], F32, kind="ExternalInput").ap()
        for n, (shp, dt) in WSCRATCH.get(name, {}).items():
            W[n] = nc.dram_tensor(n, shp, dt, kind="Internal").ap()
    W["x32"] = xin
    x32_scratch = nc.dram_tensor("x32s", [D, TB], F32, kind="Internal").ap()
    with ExitStack() as ctx:
        k = K(nc, ctx)
        build_common(k)
        build_io_load(k, xin)
        for name in SUBLAYERS:
            if name == "l1_ffn":
                stgx = k.tensors["io_stg"]
                for db in range(DBLK):
                    for hh in range(4):
                        k.op("s", lambda e, db=db, hh=hh: e.copy(
                            stgx[:, 0:1024],
                            k.tensors["x_sb"][:, db * TB + hh * 1024:
                                              db * TB + (hh + 1) * 1024]),
                            reads=[f"x{db}"], writes=["io_stg"])
                        k.dma("y", x32_scratch[db * 128:(db + 1) * 128,
                                               hh * 1024:(hh + 1) * 1024],
                              stgx[:, 0:1024], reads=["io_stg"], writes=["d_x32"])
                k.sched.barrier()
                W["x32"] = x32_scratch
            k.push_scope()
            BUILDERS[name](k, W)
            k.pop_scope()
        build_io_store(k, xout)
        sems = {}
        for e in ENGS:
            sems[("c", e)] = ctx.enter_context(nc.semaphore(f"sem_{e}"))
        for q in DMA_QUEUES:
            sems[("d", q)] = ctx.enter_context(nc.semaphore(f"dsem_{q}"))
        k.sched.realize(nc, sems)
    return nc


def build_sublayer_program(name, debug=False):
    nc = bass.Bass("TRN2", target_bir_lowering=False)
    xin = nc.dram_tensor("xin", [D, TB], F32, kind="ExternalInput").ap()
    xout = nc.dram_tensor("xout", [D, TB], F32, kind="ExternalOutput").ap()
    dbg = nc.dram_tensor("dbg", [128, 1024], F32, kind="ExternalOutput").ap() \
        if debug else None
    wnames = WNAMES[name]
    W = {n: nc.dram_tensor(n, WSHAPES[n], F32, kind="ExternalInput").ap()
         for n in wnames}
    for n, (shp, dt) in WSCRATCH.get(name, {}).items():
        W[n] = nc.dram_tensor(n, shp, dt, kind="Internal").ap()
    W["x32"] = xin
    with ExitStack() as ctx:
        k = K(nc, ctx)
        build_common(k)
        build_io_load(k, xin)
        BUILDERS[name](k, W)
        if debug:
            dstg = k.tensors["io_stg"]
            for nm, col in (("f2_lrow", 0), ("f2_mcol", 64), ("f2_ohc", 96),
                            ("f2_brows", 128), ("l3_xdbl32", 0),
                            ("l3_srp", 384), ("l3_u", 512)):
                if nm in k.tensors:
                    h = k.tensors[nm]
                    p, f = h.shape
                    n = min(f, 384 if col < 384 else 128)
                    k.op("s", lambda e, h=h, p=p, n=n: e.copy(
                        dstg[0:p, 0:n], h[:, 0:n]), reads=[nm], writes=["io_stg"])
                    k.dma("g", dbg[0:p, col:col + n], dstg[0:p, 0:n],
                          reads=["io_stg"], writes=[])
            if "l3a_yfin_dram" in W:
                k.dma("g", dstg[:, 0:384],
                      W["l3a_opT_bf"][0:128, 0:384], writes=["io_stg"])
                k.dma("g", dbg[:, 640:1024], dstg[:, 0:384],
                      reads=["io_stg"], writes=[])
            if "l2f_wsel_dram" in W:
                k.dma("y", dstg[:, 0:384], W["l2f_wsel_dram"][0:128, 0:384],
                      writes=["io_stg"]) if False else None
                k.op("s", lambda e: e.copy(dstg[:, 0:384],
                                           k.tensors["f2_ws"][:, 1024:1408]),
                     reads=["f2_ws1"], writes=["io_stg"])
                k.dma("g", dbg[:, 512:896], dstg[:, 0:384],
                      reads=["io_stg"], writes=[])
        build_io_store(k, xout)
        sems = {}
        for e in ENGS:
            sems[("c", e)] = ctx.enter_context(nc.semaphore(f"sem_{e}"))
        for q in DMA_QUEUES:
            sems[("d", q)] = ctx.enter_context(nc.semaphore(f"dsem_{q}"))
        k.sched.realize(nc, sems)
    return nc, wnames


FUSED = True


def kernel(**inputs):
    W = prep_weights(inputs)
    xs = shard_x(inputs["L0_attn_x"])
    if FUSED:
        nc = build_fused_program()
        wall = {n: W[n] for name in SUBLAYERS for n in WNAMES[name]}
        in_maps = [dict({"xin": xs[c]}, **wall) for c in range(NCORES)]
        res = run_bass_kernel_spmd(nc, in_maps, core_ids=list(range(NCORES)))
        xs = [res.results[c]["xout"] for c in range(NCORES)]
        return unshard_x(xs)
    for name in SUBLAYERS:
        nc, wnames = build_sublayer_program(name)
        in_maps = [dict({"xin": xs[c]}, **{n: W[n] for n in wnames})
                   for c in range(NCORES)]
        res = run_bass_kernel_spmd(nc, in_maps, core_ids=list(range(NCORES)))
        xs = [res.results[c]["xout"] for c in range(NCORES)]
    return unshard_x(xs)
